# version 20
# baseline (speedup 1.0000x reference)
import numpy as np
import concourse.bass as bass
import concourse.mybir as mybir
from concourse.bass_utils import run_bass_kernel_spmd

F32 = mybir.dt.float32
BF16 = mybir.dt.bfloat16
AF = mybir.ActivationFunctionType
ALU = mybir.AluOpType

T = 2048
NC = 16
LC = 128
DM = 1024
DI = 2048
H = 8
DS = 16
P = 256
DTR = 64
F1 = 4096
NKV = 2
HD = 128
DLAT = 32
NH = 8

MASKBIG = 1.0e30
DEBUG_EPI_BARRIERS = [False]

CONSTS = [1e-5, 1e-6, 1.0, 0.0, 1.0 / np.sqrt(128.0), 1.0 / DI, -1.0]
_CIDX = {v: i for i, v in enumerate(CONSTS)}

_COMPUTE_ENGINES = ("tensor", "vector", "scalar", "gpsimd")
_ALL_ENGINES = ("tensor", "vector", "scalar", "gpsimd", "sync")


class _Op:
    __slots__ = ("oid", "eng", "emit", "deps", "dma", "chan", "inc", "seq", "note", "tail")

    def __init__(self, oid, eng, emit, deps, dma, chan, note, tail):
        self.oid = oid
        self.eng = eng
        self.emit = emit
        self.deps = deps
        self.dma = dma
        self.chan = chan
        self.inc = False
        self.seq = 0
        self.note = note
        self.tail = tail


class Sched:

    def __init__(self, nc):
        self.nc = nc
        self.ops = []
        self.last_writer = {}
        self.readers = {}
        self.chan_engine = {}
        self.chan_ctr = {}
        self.eng_last = {}
        self.chan_last = {}

    def op(self, eng, emit, reads=(), writes=(), note="", tail=False):
        return self._add(eng, emit, reads, writes, dma=False, chan=None, note=note, tail=tail)

    ROT = {"wload": 12, "xload": 8, "xacc": 6, "spill": 4, "xtrans": 6,
           "gtrans": 6, "ptrans": 4, "ostore": 4, "xseed": 4, "crep": 6, "bcast": 3}

    def dma(self, chan, eng, emit, reads=(), writes=(), note=""):
        r = self.ROT.get(chan, 2)
        ctr = self.chan_ctr.get(chan, 0)
        self.chan_ctr[chan] = ctr + 1
        chan = f"{chan}#{ctr % r}"
        if chan in self.chan_engine:
            assert self.chan_engine[chan] == eng, (chan, eng)
        else:
            self.chan_engine[chan] = eng
        extra = ()
        if chan in self.chan_last:
            extra = (self.chan_last[chan],)
        return self._add(eng, emit, reads, writes, dma=True, chan=chan, note=note,
                         tail=False, extra_deps=extra)

    def _add(self, eng, emit, reads, writes, dma, chan, note, tail=False, extra_deps=()):
        oid = len(self.ops)
        deps = set(extra_deps)
        for t in reads:
            w = self.last_writer.get(t)
            if w is not None:
                deps.add(w)
        for t in writes:
            w = self.last_writer.get(t)
            if w is not None:
                deps.add(w)
            for r in self.readers.get(t, ()):
                deps.add(r)
        deps.discard(oid)
        for t in reads:
            self.readers.setdefault(t, []).append(oid)
        for t in writes:
            self.last_writer[t] = oid
            self.readers[t] = []
        self.ops.append(_Op(oid, eng, emit, deps, dma, chan, note, tail))
        if dma:
            self.chan_last[chan] = oid
        else:
            self.eng_last[eng] = oid
        return oid

    def barrier(self, note="barrier"):
        snap = list(self.eng_last.values()) + list(self.chan_last.values())
        for e in _ALL_ENGINES:
            deps = [d for d in snap if self.ops[d].eng != e or self.ops[d].dma]
            self._add(e, lambda eng: eng.nop(), (), (), dma=False, chan=None,
                      note=note, tail=False, extra_deps=deps)

    def lower(self):
        nc = self.nc
        ops = self.ops
        for op in ops:
            if op.dma:
                op.inc = True
            real = []
            for d in op.deps:
                dop = ops[d]
                if (not op.dma) and (not dop.dma) and dop.eng == op.eng and not dop.tail:
                    continue
                real.append(d)
                dop.inc = True
            op.deps = real
        import contextlib
        self._stack = contextlib.ExitStack()
        sems = {}
        for e in _ALL_ENGINES:
            sems[("eng", e)] = self._stack.enter_context(nc.semaphore(f"sem_{e}"))
        for ch in self.chan_engine:
            sems[("chan", ch)] = self._stack.enter_context(nc.semaphore(f"sem_c_{ch}"))
        counts = {}
        for op in ops:
            if op.inc:
                key = ("chan", op.chan) if op.dma else ("eng", op.eng)
                amt = 16 if op.dma else 1
                counts[key] = counts.get(key, 0) + amt
                op.seq = counts[key]
        marks = {e: {} for e in _ALL_ENGINES}
        plans = {e: [] for e in _ALL_ENGINES}
        for op in ops:
            m = marks[op.eng]
            for d in sorted(op.deps):
                dop = ops[d]
                key = ("chan", dop.chan) if dop.dma else ("eng", dop.eng)
                assert dop.inc and dop.seq > 0
                if m.get(key, 0) >= dop.seq:
                    continue
                m[key] = dop.seq
                plans[op.eng].append(("wait", key, dop.seq))
            plans[op.eng].append(("op", op))

        with nc.Block() as block:
            def mk(eng_name):
                def body(eng):
                    for item in plans[eng_name]:
                        if item[0] == "wait":
                            eng.wait_ge(sems[item[1]], item[2])
                        else:
                            op = item[1]
                            inst = op.emit(eng)
                            if op.inc:
                                inst.then_inc(
                                    sems[("chan", op.chan) if op.dma else ("eng", op.eng)],
                                    16 if op.dma else 1)
                return body
            for e, attr in (("sync", block.sync), ("tensor", block.tensor),
                            ("vector", block.vector), ("scalar", block.scalar),
                            ("gpsimd", block.gpsimd)):
                if plans[e]:
                    attr(mk(e))
        return sems, counts



def build_program(taps=(), n_sublayers=4):
    import contextlib
    taps = set(taps)
    nc = bass.Bass("TRN2", target_bir_lowering=False)

    din = {}
    def dram_in(name, shape, dtype=F32):
        din[name] = nc.dram_tensor(name, list(shape), dtype, kind="ExternalInput").ap()
        return din[name]

    x0 = dram_in("x0", (T, DM))
    w_inT = dram_in("w_inT", (DM, 4416))
    w_dtT = dram_in("w_dtT", (DTR, H))
    conv_w = dram_in("conv_w", (DI, 4))
    conv_b = dram_in("conv_b", (1, DI))
    a_log = dram_in("a_log", (1, H * DS))
    dskip = dram_in("dskip", (1, DI))
    dt_bias = dram_in("dt_bias", (H, 1))
    norm_w = dram_in("norm_w", (1, DI))
    w_resT = dram_in("w_resT", (DM, DI))
    w_outT = dram_in("w_outT", (DI, DM))
    wq = dram_in("wq", (DM, NH * HD))
    wk_down = dram_in("wk_down", (DM, NKV * DLAT))
    wv_down = dram_in("wv_down", (DM, NKV * DLAT))
    wk_up = dram_in("wk_up", (DLAT, HD))
    wv_up = dram_in("wv_up", (DLAT, HD))
    wo = dram_in("wo", (NH * HD, DM))
    wrT = dram_in("wrT", (DM, 8))
    w1 = dram_in("w1", (DM, F1))
    b1 = dram_in("b1", (1, F1))
    w2 = dram_in("w2", (F1, DM))
    b2 = dram_in("b2", (1, DM))
    cpool_h = dram_in("cpool", (128, len(CONSTS)))
    eye_f = dram_in("eye_f", (128, 128))
    eye_b = dram_in("eye_b", (128, 128), BF16)
    ones1_h = dram_in("ones1", (1, 128))
    mask_mamba_h = dram_in("mask_mamba", (128, 128))
    mask_attn_h = dram_in("mask_attn", (128, 128))
    blockmask_h = dram_in("blockmask", (128, DI), BF16)

    out = nc.dram_tensor("out", [T, DM], F32, kind="ExternalOutput").ap()

    xs = nc.dram_tensor("xs", [T, DM], F32, kind="Internal").ap()
    y_scr = nc.dram_tensor("y_scr", [T, DI], BF16, kind="Internal").ap()
    d_scr = nc.dram_tensor("d_scr", [H, T], F32, kind="Internal").ap()
    decay_scr = nc.dram_tensor("decay_scr", [16, 128], F32, kind="Internal").ap()
    c_scr = nc.dram_tensor("c_scr", [128, T], BF16, kind="Internal").ap()
    cf_scr = [nc.dram_tensor(f"cf_scr{i}", [1, DI], BF16, kind="Internal").ap()
              for i in range(2)]
    a_scr = nc.dram_tensor("a_scr", [1, 128], F32, kind="Internal").ap()

    dbg = {}
    K = Sched(nc)

    stacks = []
    def arena():
        s = contextlib.ExitStack()
        stacks.append(s)
        return s

    def close_arena(s, note=""):
        K.barrier(note="close:" + note)
        s.close()

    g_arena = arena()

    def mk_sb(stack):
        def sb(name, shape, dtype=F32):
            return stack.enter_context(nc.sbuf_tensor(name, list(shape), dtype))
        return sb

    def mk_ps(stack):
        def ps(name, shape, dtype=F32):
            return stack.enter_context(nc.psum_tensor(name, list(shape), dtype))
        return ps

    gsb = mk_sb(g_arena)

    def tap(name, make_store):
        if name in taps:
            shape, dtype, store = make_store
            dbg[name] = nc.dram_tensor("dbg_" + name, list(shape), dtype,
                                       kind="ExternalOutput").ap()
            store(dbg[name])

    def tap_simple(name, ap_, shape, dtype, rtoks):
        def store(dst):
            K.dma("dbg", "gpsimd", lambda e: e.dma_start(out=dst[:], in_=ap_),
                  reads=list(rtoks))
        tap(name, (shape, dtype, store))

    cpool = gsb("cpool_s", (128, len(CONSTS)))
    eyef = gsb("eyef", (128, 128))
    eyeb = gsb("eyeb", (128, 128), BF16)
    ones1 = gsb("ones1c", (1, 128))
    maskm = gsb("maskm", (128, 128))
    maska = gsb("maska", (128, 128))

    def CONST(v, p=128):
        return cpool[0:p, _CIDX[v]:_CIDX[v] + 1]

    for t_, hbm in ((cpool, cpool_h), (eyef, eye_f), (eyeb, eye_b), (ones1, ones1_h),
                    (maskm, mask_mamba_h), (maska, mask_attn_h)):
        K.dma("const", "sync", lambda e, t_=t_, hbm=hbm: e.dma_start(out=t_[:], in_=hbm[:]),
              writes=[(t_.name if hasattr(t_, "name") else id(t_),)])
    CPOOL_T = [("cpool_s",)]
    EYEF_T = [("eyef",)]
    EYEB_T = [("eyeb",)]
    ONES_T = [("ones1c",)]
    MASKM_T = [("maskm",)]
    MASKA_T = [("maska",)]

    for c in range(NC):
        K.dma("xseed", "sync", lambda e, c=c: e.dma_start(
                out=xs[c * LC:(c + 1) * LC, :], in_=x0[c * LC:(c + 1) * LC, :]),
              writes=[("xs", c)])

    def ln_phase(phase, sb, ps, xn_f, src=None, srctok="xs"):
        src = xs if src is None else src
        nb = 5
        bufs = []
        for i in range(nb):
            bufs.append((sb(f"{phase}lnx{i}", (128, DM)), sb(f"{phase}lnbn{i}", (128, 12)),
                         sb(f"{phase}lnmv{i}", (128, 2)), sb(f"{phase}lnrs{i}", (128, 1)),
                         sb(f"{phase}lnxt{i}", (128, DM), BF16)))
        for c in range(NC):
            xc, bnb, mv, rs, xn_tok = bufs[c % nb]
            xct = (f"{phase}xc", c % nb)
            K.dma("xload", "sync",
                  lambda e, xc=xc, c=c: e.dma_start(out=xc[:], in_=src[c * LC:(c + 1) * LC, :]),
                  reads=[(srctok, c)], writes=[xct])
            bt = (f"{phase}bn", c % nb)
            K.op("vector", lambda e, xc=xc, bnb=bnb: e.bn_stats(bnb[:, 0:6], xc[:, 0:512]),
                 reads=[xct], writes=[bt], tail=True)
            K.op("vector", lambda e, xc=xc, bnb=bnb: e.bn_stats(bnb[:, 6:12], xc[:, 512:1024]),
                 reads=[xct], writes=[bt], tail=True)
            K.op("vector", lambda e, bnb=bnb, mv=mv: e.bn_aggr(
                     mv[:, 0:2], bnb[:, 0:12].rearrange("p (n t) -> p n t", t=6)),
                 reads=[bt], writes=[(f"{phase}mv", c % nb)], tail=True)
            K.op("scalar", lambda e, mv=mv, rs=rs: e.activation(
                     rs[:, 0:1], mv[:, 1:2], AF.Sqrt, bias=CONST(1e-5)),
                 reads=[(f"{phase}mv", c % nb)] + CPOOL_T, writes=[(f"{phase}rs", c % nb)])
            K.op("vector", lambda e, rs=rs: e.reciprocal(rs[:, 0:1], rs[:, 0:1]),
                 reads=[(f"{phase}rs", c % nb)], writes=[(f"{phase}rs", c % nb)], tail=True)
            K.op("vector",
                 lambda e, xc=xc, mv=mv, rs=rs, xn_tok=xn_tok: e.tensor_scalar(
                     xn_tok[:], xc[:], mv[:, 0:1], rs[:, 0:1], ALU.subtract, ALU.mult),
                 reads=[xct, (f"{phase}mv", c % nb), (f"{phase}rs", c % nb)],
                 writes=[(f"{phase}xntok", c % nb)])
            for fc in range(8):
                K.dma("xtrans", "scalar",
                      lambda e, xn_tok=xn_tok, fc=fc, c=c: e.dma_start_transpose(
                          xn_f[fc][:, c * LC:(c + 1) * LC], xn_tok[:, fc * 128:(fc + 1) * 128]),
                      reads=[(f"{phase}xntok", c % nb)], writes=[(f"{phase}xnf", fc, c)])
        return [(f"{phase}xnf", fc, c) for fc in range(8) for c in range(NC)]

    def mk_wstream(phase, sb, n_stage=4, cols=512):
        stg = [sb(f"{phase}wstg{i}", (128, cols)) for i in range(n_stage)]
        wbf = [sb(f"{phase}wbf{i}", (128, cols), BF16) for i in range(n_stage)]
        ctr = [0]

        def load(hbm_slice_fn, ncols, note=""):
            i = ctr[0] % n_stage
            ctr[0] += 1
            stok = (f"{phase}wstg", i)
            btok = (f"{phase}wbf", i)
            K.dma("wload", "sync",
                  lambda e, i=i: e.dma_start(out=stg[i][:, 0:ncols], in_=hbm_slice_fn()),
                  writes=[stok], note=note)
            K.op("vector",
                 lambda e, i=i: e.tensor_copy(wbf[i][:, 0:ncols], stg[i][:, 0:ncols]),
                 reads=[stok], writes=[btok], note=note)
            return wbf[i], btok
        return load

    def accum_to_xs(tile_ap, rows, cols, rtok, c, note=""):
        K.dma("xacc", "gpsimd",
              lambda e: e.dma_start(out=xs[rows[0]:rows[1], cols[0]:cols[1]],
                                    in_=tile_ap, accum_op=ALU.add),
              reads=[rtok, ("xs", c)], writes=[("xs", c)], note=note)

    sl1 = arena()
    sb1 = mk_sb(sl1)
    ps1 = mk_ps(sl1)
    PS_BANKS = [ps1(f"m_bank{i}", (128, 512)) for i in range(8)]
    PSTOK = [("psbank", i) for i in range(8)]

    xn_f = [sb1(f"m_xnf{fc}", (128, T), BF16) for fc in range(8)]
    ln_arena = arena()
    xnf_toks = ln_phase("m", mk_sb(ln_arena), ps1, xn_f, src=x0, srctok="x0dram")
    close_arena(ln_arena, "m_ln")
    sp_arena = arena()
    sb1 = mk_sb(sp_arena)
    XNF = lambda fc: [("m" + "xnf", fc, c) for c in range(NC)]

    for fc in range(8):
        tap_simple(f"xnf{fc}", xn_f[fc][:], (128, T), BF16, XNF(fc))

    u_raw = [sb1(f"m_uraw{fc}", (128, T + 4), BF16) for fc in range(16)]
    B_f = sb1("m_B_f", (128, T))
    dt_f = sb1("m_dt_f", (8, T))
    A_rep = sb1("m_Arep", (128, 128))
    Cbar = sb1("m_Cbar", (128, T), BF16)
    Bbar2 = sb1("m_Bbar2", (128, T))
    dtT_all = sb1("m_dtT", (128, 128))
    Dcol2d = sb1("m_Dcol", (128, 128))
    convw_cols = sb1("m_convw", (128, 64))
    convb_cols = sb1("m_convb", (128, 16))
    dskip_rep = sb1("m_dskip", (128, DI), BF16)
    blockmask = sb1("m_blockm", (128, DI), BF16)
    ones1b = sb1("m_ones1b", (1, 128), BF16)

    proj_arena = arena()
    sbp = mk_sb(proj_arena)
    dth = sbp("m_dth", (64, T), BF16)
    C_f = sbp("m_C_f", (128, T), BF16)

    for fc in range(16):
        K.op("vector", lambda e, fc=fc: e.memset(u_raw[fc][:, 0:4], 0.0),
             writes=[("uraw0", fc)])

    wload_m = mk_wstream("m", sbp, n_stage=4, cols=128)

    def project(moff, msize, bank_grp, evac_fn, note=""):
        banks = PS_BANKS[bank_grp * 4:(bank_grp + 1) * 4]
        btoks = PSTOK[bank_grp * 4:(bank_grp + 1) * 4]
        for kc in range(8):
            wtile, wtok = wload_m(
                (lambda kc=kc: w_inT[kc * 128:(kc + 1) * 128, moff:moff + msize]),
                msize, note=note)
            for ns in range(4):
                K.op("tensor",
                     lambda e, wtile=wtile, msize=msize, kc=kc, ns=ns, b=banks[ns]:
                         e.matmul(b[0:msize, :], wtile[:, 0:msize],
                                  xn_f[kc][:, ns * 512:(ns + 1) * 512],
                                  start=(kc == 0), stop=(kc == 7)),
                     reads=[wtok] + [("m" + "xnf", kc, c) for c in range(ns * 4, ns * 4 + 4)],
                     writes=[btoks[ns]], note=note)
        for ns in range(4):
            evac_fn(ns, banks[ns], btoks[ns])

    for fc in range(16):
        def evac_u(ns, bank, btok, fc=fc):
            K.op("vector",
                 lambda e, ns=ns, bank=bank, fc=fc: e.tensor_copy(
                     u_raw[fc][:, 4 + ns * 512: 4 + (ns + 1) * 512], bank[:, :]),
                 reads=[btok], writes=[("uraw", fc, ns)], note=f"uevac{fc}")
        project(2048 + fc * 128, 128, fc % 2, evac_u, note=f"uproj{fc}")

    def evac_dth(ns, bank, btok):
        K.op("vector", lambda e, ns=ns, bank=bank: e.tensor_copy(
                 dth[:, ns * 512:(ns + 1) * 512], bank[0:64, :]),
             reads=[btok], writes=[("dth", ns)])
    project(4096, 64, 0, evac_dth, note="dthproj")

    def evac_B(ns, bank, btok):
        K.op("vector", lambda e, ns=ns, bank=bank: e.tensor_copy(
                 B_f[:, ns * 512:(ns + 1) * 512], bank[:, :]),
             reads=[btok], writes=[("B_f", ns)])
    project(4160, 128, 1, evac_B, note="Bproj")

    def evac_C(ns, bank, btok):
        K.op("vector", lambda e, ns=ns, bank=bank: e.tensor_copy(
                 C_f[:, ns * 512:(ns + 1) * 512], bank[:, :]),
             reads=[btok], writes=[("C_f", ns)])
    project(4288, 128, 0, evac_C, note="Cproj")
    K.dma("spill", "gpsimd", lambda e: e.dma_start(out=c_scr[:], in_=C_f[:]),
          reads=[("C_f", ns) for ns in range(4)], writes=[("cscr",)])
    K.op("vector", lambda e: e.tensor_copy(ones1b[:], ones1[:]), reads=ONES_T,
         writes=[("ones1b",)])

    UR_T = [("uraw", fc, ns) for fc in range(16) for ns in range(4)] + \
           [("uraw0", fc) for fc in range(16)]
    tap_simple("u_raw0", u_raw[0][:, 4:4 + T], (128, T), BF16, [("uraw", 0, ns) for ns in range(4)])
    tap_simple("dth", dth[:], (64, T), BF16, [("dth", ns) for ns in range(4)])
    tap_simple("B_f", B_f[:], (128, T), F32, [("B_f", ns) for ns in range(4)])
    tap_simple("C_f", C_f[:], (128, T), BF16, [("C_f", ns) for ns in range(4)])

    wdt_stg = sbp("wdt_stg", (64, 8))
    wdt_bf = sbp("wdt_bf", (64, 8), BF16)
    dtbias_c = sbp("dtbias_c", (8, 1))
    K.dma("wload", "sync", lambda e: e.dma_start(out=wdt_stg[:], in_=w_dtT[:]), writes=[("wdts",)])
    K.op("vector", lambda e: e.tensor_copy(wdt_bf[:], wdt_stg[:]), reads=[("wdts",)], writes=[("wdtb",)])
    K.dma("const", "sync", lambda e: e.dma_start(out=dtbias_c[:], in_=dt_bias[:]), writes=[("dtbc",)])

    sptmp = sbp("m_sptmp", (8, T))
    sptmp2 = sbp("m_sptmp2", (8, T))
    for ns in range(4):
        K.op("tensor",
             lambda e, ns=ns: e.matmul(PS_BANKS[4 + ns][0:8, :], wdt_bf[:],
                                       dth[:, ns * 512:(ns + 1) * 512], start=True, stop=True),
             reads=[("wdtb",), ("dth", ns)], writes=[PSTOK[4 + ns]])
        K.op("scalar",
             lambda e, ns=ns: e.activation(sptmp[:, ns * 512:(ns + 1) * 512],
                                           PS_BANKS[4 + ns][0:8, :], AF.Abs, bias=dtbias_c[:]),
             reads=[PSTOK[4 + ns], ("dtbc",)], writes=[("sptmp", ns)])
        K.op("scalar",
             lambda e, ns=ns: e.activation(sptmp2[:, ns * 512:(ns + 1) * 512],
                                           sptmp[:, ns * 512:(ns + 1) * 512], AF.Exp,
                                           scale=CONST(-1.0, 8)),
             reads=[("sptmp", ns)] + CPOOL_T, writes=[("sptmp2", ns)])
        K.op("scalar",
             lambda e, ns=ns: e.activation(sptmp[:, ns * 512:(ns + 1) * 512],
                                           sptmp2[:, ns * 512:(ns + 1) * 512], AF.Ln,
                                           bias=CONST(1.0, 8)),
             reads=[("sptmp2", ns)] + CPOOL_T, writes=[("sptmp", ns)])
        K.op("scalar",
             lambda e, ns=ns: e.activation(dt_f[:, ns * 512:(ns + 1) * 512],
                                           PS_BANKS[4 + ns][0:8, :], AF.Relu, bias=dtbias_c[:]),
             reads=[PSTOK[4 + ns], ("dtbc",)], writes=[("dtf", ns)])
        K.op("vector",
             lambda e, ns=ns: e.tensor_add(dt_f[:, ns * 512:(ns + 1) * 512],
                                           dt_f[:, ns * 512:(ns + 1) * 512],
                                           sptmp[:, ns * 512:(ns + 1) * 512]),
             reads=[("dtf", ns), ("sptmp", ns)], writes=[("dtf", ns)])
    K.op("vector", lambda e: e.tensor_scalar(dt_f[:], dt_f[:], 1.0, 1e-4, ALU.min, ALU.max),
         reads=[("dtf", 0), ("dtf", 1), ("dtf", 2), ("dtf", 3)], writes=[("dtf", "clip")])
    tap_simple("dt_f", dt_f[:], (8, T), F32, [("dtf", "clip")])
    tap_simple("sptmp", sptmp[:], (8, T), F32, [("sptmp", ns) for ns in range(4)])

    D8 = sbp("m_D8", (8, T))
    K.op("vector", lambda e: e.tensor_tensor_scan(D8[:], dt_f[:], dt_f[:], 0.0, ALU.add, ALU.bypass),
         reads=[("dtf", "clip")], writes=[("D8",)], tail=True)
    tap_simple("D8", D8[:], (8, T), F32, [("D8",)])
    K.dma("bcast", "gpsimd", lambda e: e.dma_start(out=d_scr[:], in_=D8[:]),
          reads=[("D8",)], writes=[("dscr8",)])

    DedgeE = sbp("m_DedgeE", (8, 17))
    K.op("vector", lambda e: e.memset(DedgeE[:, 0:1], 0.0), writes=[("De", 0)])
    K.op("vector", lambda e: e.tensor_copy(
            DedgeE[:, 1:17], D8[:].rearrange("p (c l) -> p c l", l=128)[:, :, 127:128]),
         reads=[("D8",)], writes=[("De", 1)], tail=True)
    Dl2d = sbp("m_Dl2d", (8, 16))
    K.op("vector", lambda e: e.tensor_sub(Dl2d[:], DedgeE[:, 1:17], DedgeE[:, 0:16]),
         reads=[("De", 0), ("De", 1)], writes=[("Dl2d",)], tail=True)

    Dfeat128 = sbp("m_Dfeat", (128, T))
    K.dma("bcast", "gpsimd",
          lambda e: e.dma_start(out=Dfeat128[:], in_=D8[:].unsqueeze(1).broadcast_to([8, 16, T])),
          reads=[("D8",)], writes=[("Dfeat",)])
    De128 = sbp("m_De128", (128, 16))
    K.dma("bcast", "gpsimd",
          lambda e: e.dma_start(out=De128[:], in_=DedgeE[:, 0:16].unsqueeze(1).broadcast_to([8, 16, 16])),
          reads=[("De", 0), ("De", 1)], writes=[("De128",)])
    DN128 = sbp("m_DN128", (128, 16))
    K.dma("bcast", "gpsimd",
          lambda e: e.dma_start(out=DN128[:], in_=DedgeE[:, 1:17].unsqueeze(1).broadcast_to([8, 16, 16])),
          reads=[("De", 1)], writes=[("DN128",)])
    Dl128 = sbp("m_Dl128", (128, 16))
    K.dma("bcast", "gpsimd",
          lambda e: e.dma_start(out=Dl128[:], in_=Dl2d[:].unsqueeze(1).broadcast_to([8, 16, 16])),
          reads=[("Dl2d",)], writes=[("Dl128",)])

    alog_s = sbp("m_alog", (1, 128))
    A_flat = sbp("m_Aflat", (1, 128))
    A_colf = sbp("m_Acolf", (128, 1))
    K.dma("const", "sync", lambda e: e.dma_start(out=alog_s[:], in_=a_log[:]), writes=[("alog",)])
    K.op("scalar", lambda e: e.activation(A_flat[:], alog_s[:], AF.Exp), reads=[("alog",)],
         writes=[("Aflat",)])
    K.op("vector", lambda e: e.tensor_scalar_mul(A_flat[:], A_flat[:], -1.0),
         reads=[("Aflat",)], writes=[("Aflat",)], tail=True)
    K.dma("bcast", "gpsimd", lambda e: e.dma_start(out=a_scr[:], in_=A_flat[:]),
          reads=[("Aflat",)], writes=[("ascr",)])
    K.dma("bcast", "gpsimd", lambda e: e.dma_start(out=A_rep[:], in_=a_scr[:].broadcast_to([128, 128])),
          reads=[("ascr",)], writes=[("Arep",)])
    K.op("tensor", lambda e: e.transpose(PS_BANKS[6][0:128, 0:1], A_flat[0:1, 0:128], eyef[0:1, 0:1]),
         reads=[("Aflat",)] + EYEF_T, writes=[PSTOK[6]])
    K.op("vector", lambda e: e.tensor_copy(A_colf[:], PS_BANKS[6][0:128, 0:1]),
         reads=[PSTOK[6]], writes=[("Acolf",)])

    e_scr = sbp("m_escr", (128, T))
    e_scr2 = sbp("m_escr2", (128, T))
    K.op("vector", lambda e: e.tensor_sub(
            e_scr[:].rearrange("p (c l) -> p c l", l=128),
            Dfeat128[:].rearrange("p (c l) -> p c l", l=128),
            De128[:].unsqueeze(2).broadcast_to([128, 16, 128])),
         reads=[("Dfeat",), ("De128",)], writes=[("escr",)])
    tap_simple("e_scr", e_scr[:], (128, T), F32, [("escr",)])
    K.op("scalar", lambda e: e.activation(e_scr2[:], e_scr[:], AF.Exp, scale=A_colf[:]),
         reads=[("escr",), ("Acolf",)], writes=[("escr2",)])
    tap_simple("e_scr2", e_scr2[:], (128, T), F32, [("escr2",)])
    K.op("vector", lambda e: e.tensor_mul(Cbar[:], e_scr2[:], C_f[:]),
         reads=[("escr2",)] + [("C_f", ns) for ns in range(4)], writes=[("Cbar",)])
    K.op("vector", lambda e: e.tensor_sub(
            e_scr[:].rearrange("p (c l) -> p c l", l=128),
            DN128[:].unsqueeze(2).broadcast_to([128, 16, 128]),
            Dfeat128[:].rearrange("p (c l) -> p c l", l=128)),
         reads=[("Dfeat",), ("DN128",), ("Cbar",)], writes=[("escr",)])
    K.op("scalar", lambda e: e.activation(e_scr2[:], e_scr[:], AF.Exp, scale=A_colf[:]),
         reads=[("escr",), ("Acolf",)], writes=[("escr2",)])
    K.op("vector", lambda e: e.tensor_mul(Bbar2[:], e_scr2[:], B_f[:]),
         reads=[("escr2",)] + [("B_f", ns) for ns in range(4)], writes=[("Bbar2",)])
    tap_simple("Dfeat128", Dfeat128[:], (128, T), F32, [("Dfeat",)])
    tap_simple("De128", De128[:], (128, 16), F32, [("De128",)])
    tap_simple("A_rep", A_rep[:], (128, 128), F32, [("Arep",)])
    tap_simple("A_colf", A_colf[:], (128, 1), F32, [("Acolf",)])
    tap_simple("Cbar", Cbar[:], (128, T), BF16, [("Cbar",)])
    tap_simple("Bbar2", Bbar2[:], (128, T), F32, [("Bbar2",)])

    decayspan = sbp("m_dspan", (128, 16))
    dspanT = sbp("m_dspanT", (16, 128))
    K.op("scalar", lambda e: e.activation(decayspan[:], Dl128[:], AF.Exp, scale=A_colf[:]),
         reads=[("Dl128",), ("Acolf",)], writes=[("dspan",)])
    K.op("tensor", lambda e: e.transpose(PS_BANKS[6][0:16, 0:128], decayspan[:], eyef[:]),
         reads=[("dspan",)] + EYEF_T, writes=[PSTOK[6]])
    K.op("vector", lambda e: e.tensor_copy(dspanT[:], PS_BANKS[6][0:16, 0:128]),
         reads=[PSTOK[6]], writes=[("dspanT",)])
    K.dma("bcast", "gpsimd", lambda e: e.dma_start(out=decay_scr[:], in_=dspanT[:]),
          reads=[("dspanT",)], writes=[("dscr",)])

    for c in range(NC):
        K.op("tensor", lambda e, c=c: e.transpose(
                PS_BANKS[7][0:128, 0:8], dt_f[0:8, c * LC:(c + 1) * LC], eyef[0:8, 0:8]),
             reads=[("dtf", "clip")] + EYEF_T, writes=[PSTOK[7]])
        K.op("vector", lambda e, c=c: e.tensor_copy(dtT_all[:, c * 8:(c + 1) * 8],
                                                    PS_BANKS[7][0:128, 0:8]),
             reads=[PSTOK[7]], writes=[("dtT", c)])

    st16 = sbp("m_st16", (16, 128))
    for h in range(H):
        K.dma("bcast", "gpsimd", lambda e, h=h: e.dma_start(
                out=st16[:], in_=d_scr[h:h + 1, :].rearrange("o (c l) -> (o c) l", l=128)),
              reads=[("dscr8",)], writes=[("st16",)])
        K.op("tensor", lambda e: e.transpose(PS_BANKS[6][0:128, 0:16], st16[:], eyef[0:16, 0:16]),
             reads=[("st16",)] + EYEF_T, writes=[PSTOK[6]])
        K.op("vector", lambda e, h=h: e.tensor_copy(Dcol2d[:, h * 16:(h + 1) * 16],
                                                    PS_BANKS[6][0:128, 0:16]),
             reads=[PSTOK[6]], writes=[("Dcol", h)])

    cb16 = sbp("m_cb16", (16, 128))
    K.dma("const", "sync", lambda e: e.dma_start(
            out=convw_cols[:], in_=conv_w[:].rearrange("(fc p) j -> p fc j", p=128)),
          writes=[("convw",)])
    K.dma("const", "sync", lambda e: e.dma_start(
            out=cb16[:], in_=conv_b[:].rearrange("o (fc p) -> (o fc) p", p=128)),
          writes=[("cb16",)])
    K.op("tensor", lambda e: e.transpose(PS_BANKS[7][0:128, 0:16], cb16[:], eyef[0:16, 0:16]),
         reads=[("cb16",)] + EYEF_T, writes=[PSTOK[7]])
    K.op("vector", lambda e: e.tensor_copy(convb_cols[:], PS_BANKS[7][0:128, 0:16]),
         reads=[PSTOK[7]], writes=[("convb",)])

    K.dma("bcast", "gpsimd", lambda e: e.dma_start(out=dskip_rep[:], in_=dskip[:].broadcast_to([128, DI])),
          writes=[("dskiprep",)])
    K.dma("const", "sync", lambda e: e.dma_start(out=blockmask[:], in_=blockmask_h[:]),
          writes=[("blockm",)])

    close_arena(proj_arena, "m_proj")

    scan_arena = arena()
    sbs = mk_sb(scan_arena)
    h0b = sbs("m_h0b", (128, DI), BF16)
    hT_cur = sbs("m_hT", (128, 256))
    K.op("vector", lambda e: e.memset(h0b[:], 0.0), writes=[("h0b",)])
    K.op("vector", lambda e: e.memset(hT_cur[:], 0.0), writes=[("hT",)])

    uconv_s = [sbs(f"m_uconv{i}", (128, 128)) for i in range(2)]
    uT_c = [sbs(f"m_uTc{i}", (128, DI), BF16) for i in range(2)]
    Btok_c = [sbs(f"m_Btok{i}", (128, 128)) for i in range(2)]
    Bb2T_c = [sbs(f"m_Bb2T{i}", (128, 128), BF16) for i in range(2)]
    decrep = [sbs(f"m_decrep{i}", (128, 128)) for i in range(2)]
    Uh = [sbs(f"m_Uh{i}", (128, 256), BF16) for i in range(2)]
    Sb_t = [sbs(f"m_Sb{i}", (128, 128), BF16) for i in range(16)]
    E_t = [sbs(f"m_Et{i}", (128, 128)) for i in range(2)]
    tdec = sbs("m_tdec", (128, 256))
    htF = sbs("m_htF", (128, 256), BF16)
    uD = sbs("m_uD", (128, DI), BF16)
    y_c = [sbs(f"m_yc{i}", (128, DI), BF16) for i in range(1)]
    CrepS = [sbs(f"m_crep{i}", (128, DI), BF16) for i in range(2)]
    DrepS = [sbs(f"m_drep{i}", (128, 128)) for i in range(2)]

    for c in range(NC):
        uT = uT_c[c % 2]
        uTtok = ("uT", c % 2)
        for fc in range(16):
            ub = uconv_s[fc % 2]
            ubtok = ("uconv", fc % 2)
            K.op("scalar",
                 lambda e, fc=fc, c=c, ub=ub: e.activation(
                     ub[:], u_raw[fc][:, 4 + c * LC: 4 + (c + 1) * LC], AF.Identity,
                     scale=convw_cols[:, fc * 4 + 3: fc * 4 + 4],
                     bias=convb_cols[:, fc:fc + 1]),
                 reads=[("uraw", fc, c // 4), ("uraw0", fc), ("convw",), ("convb",)],
                 writes=[ubtok], note=f"conv{c},{fc}")
            for j in range(3):
                K.op("vector",
                     lambda e, fc=fc, c=c, j=j, ub=ub: e.scalar_tensor_tensor(
                         ub[:], u_raw[fc][:, 1 + j + c * LC: 1 + j + c * LC + 128],
                         convw_cols[:, fc * 4 + j: fc * 4 + j + 1], ub[:],
                         ALU.mult, ALU.add),
                     reads=[("uraw", fc, c // 4), ("uraw", fc, max(c - 1, 0) // 4),
                            ("uraw0", fc), ("convw",), ubtok],
                     writes=[ubtok], note=f"conv{c},{fc},{j}")
            tb = PS_BANKS[6 + fc % 2]
            tbtok = PSTOK[6 + fc % 2]
            K.op("tensor", lambda e, ub=ub, tb=tb: e.transpose(tb[:, 0:128], ub[:], eyef[:]),
                 reads=[ubtok] + EYEF_T, writes=[tbtok], note=f"uT{c},{fc}")
            K.op("vector", lambda e, fc=fc, uT=uT, tb=tb: e.tensor_copy(
                    uT[:, fc * 128:(fc + 1) * 128], tb[:, 0:128]),
                 reads=[tbtok], writes=[uTtok], note=f"uTe{c},{fc}")
        Btok = Btok_c[c % 2]
        Bb2T = Bb2T_c[c % 2]
        dcr = decrep[c % 2]
        K.op("tensor", lambda e, c=c: e.transpose(PS_BANKS[6][:, 0:128],
                                                  B_f[:, c * LC:(c + 1) * LC], eyef[:]),
             reads=[("B_f", c // 4)] + EYEF_T, writes=[PSTOK[6]])
        K.op("vector", lambda e, Btok=Btok: e.tensor_copy(Btok[:], PS_BANKS[6][:, 0:128]),
             reads=[PSTOK[6]], writes=[("Btok", c % 2)])
        K.op("tensor", lambda e, c=c: e.transpose(PS_BANKS[7][:, 0:128],
                                                  Bbar2[:, c * LC:(c + 1) * LC], eyef[:]),
             reads=[("Bbar2",)] + EYEF_T, writes=[PSTOK[7]])
        K.op("vector", lambda e, Bb2T=Bb2T: e.tensor_copy(Bb2T[:], PS_BANKS[7][:, 0:128]),
             reads=[PSTOK[7]], writes=[("Bb2T", c % 2)])
        K.dma("bcast", "gpsimd", lambda e, c=c, dcr=dcr: e.dma_start(
                out=dcr[:], in_=decay_scr[c:c + 1, :].broadcast_to([128, 128])),
              reads=[("dscr",)], writes=[("dcrep", c % 2)])

        if c > 0:
            for ns in range(4):
                K.op("tensor", lambda e, c=c, ns=ns: e.matmul(
                        PS_BANKS[ns][:, :], Cbar[:, c * LC:(c + 1) * LC],
                        h0b[:, ns * 512:(ns + 1) * 512], start=True, stop=False),
                     reads=[("Cbar",), ("h0b",)], writes=[PSTOK[ns]], note=f"Y1:{c},{ns}")

        for h in range(H):
            U = Uh[h % 2]
            Utok = ("Uh", h % 2)
            K.op("vector", lambda e, h=h, c=c, U=U, uT=uT: e.tensor_scalar_mul(
                    U[:], uT[:, h * 256:(h + 1) * 256], dtT_all[:, c * 8 + h: c * 8 + h + 1]),
                 reads=[uTtok, ("dtT", c)], writes=[Utok], note=f"Uh{c},{h}")
            dr = DrepS[h % 2]
            K.dma("crep", "gpsimd", lambda e, h=h, c=c, dr=dr: e.dma_start(
                    out=dr[:], in_=d_scr[h:h + 1, c * LC:(c + 1) * LC].broadcast_to([128, 128])),
                  reads=[("dscr8",)], writes=[("DrepS", h % 2)], note=f"Drep{c},{h}")
            E = E_t[h % 2]
            Etok = ("E", h % 2)
            K.op("vector", lambda e, h=h, c=c, E=E, dr=dr: e.scalar_tensor_tensor(
                    E[:], dr[:], Dcol2d[:, h * 16 + c: h * 16 + c + 1],
                    maskm[:], ALU.subtract, ALU.add),
                 reads=[("DrepS", h % 2), ("Dcol", h)] + MASKM_T, writes=[Etok], note=f"E{c},{h}")
            K.dma("crep", "gpsimd", lambda e, h=h, c=c, cr_=cf_scr[h % 2]: e.dma_start(
                    out=cr_[0:1, :], in_=c_scr[h * 16:(h + 1) * 16, c * LC:(c + 1) * LC]),
                  reads=[("cscr",)], writes=[("cfst", h % 2)], note=f"cfst{c},{h}")
            crS = CrepS[h % 2]
            K.dma("crep", "gpsimd", lambda e, h=h, crS=crS, cr_=cf_scr[h % 2]: e.dma_start(
                    out=crS[:], in_=cr_[0:1, :].broadcast_to([128, DI])),
                  reads=[("cfst", h % 2)], writes=[("CrepS", h % 2)], note=f"crbc{c},{h}")
            bank = PS_BANKS[h // 2]
            btok = PSTOK[h // 2]
            for n in range(16):
                hn = h * 16 + n
                Sb = Sb_t[n]
                K.op("scalar", lambda e, hn=hn, Sb=Sb, E=E: e.activation(
                        Sb[:], E[:], AF.Exp, scale=A_rep[:, hn:hn + 1]),
                     reads=[Etok, ("Arep",)], writes=[("Sb", n)], note=f"S{c},{h},{n}")
                K.op("vector", lambda e, hn=hn, n=n, Sb=Sb, Btok=Btok, crS=crS:
                     e.scalar_tensor_tensor(
                        Sb[:], Sb[:], Btok[:, hn:hn + 1],
                        crS[:, n * 128:(n + 1) * 128], ALU.mult, ALU.mult),
                     reads=[("Sb", n), ("Btok", c % 2), ("CrepS", h % 2)],
                     writes=[("Sb", n)], note=f"SB{c},{h},{n}")
                first = (c == 0 and h % 2 == 0 and n == 0)
                last = (h % 2 == 1 and n == 15)
                K.op("tensor", lambda e, h=h, Sb=Sb, U=U, bank=bank, first=first, last=last:
                     e.matmul(bank[:, (h % 2) * 256:(h % 2) * 256 + 256], Sb[:], U[:],
                              start=first, stop=last),
                     reads=[("Sb", n), Utok], writes=[btok], note=f"MM{c},{h},{n}")
            for half in range(2):
                first = (h == 0 and half == 0)
                last = (h == H - 1 and half == 1)
                K.op("tensor", lambda e, h=h, half=half, U=U, Bb2T=Bb2T, first=first, last=last:
                     e.matmul(PS_BANKS[4][:, half * 128 + h * 16: half * 128 + h * 16 + 16],
                              U[:, half * 128:(half + 1) * 128], Bb2T[:, h * 16:(h + 1) * 16],
                              start=first, stop=last),
                     reads=[Utok, ("Bb2T", c % 2)], writes=[PSTOK[4]], note=f"HU{c},{h},{half}")

        K.op("vector", lambda e, dcr=dcr: e.tensor_mul(
                tdec[:].rearrange("p (a b) -> p a b", b=128),
                hT_cur[:].rearrange("p (a b) -> p a b", b=128),
                dcr[:].unsqueeze(1).broadcast_to([128, 2, 128])),
             reads=[("hT",), ("dcrep", c % 2)], writes=[("tdec",)], note=f"hdec{c}")
        K.op("vector", lambda e: e.tensor_add(hT_cur[:], tdec[:], PS_BANKS[4][:, 0:256]),
             reads=[("tdec",), PSTOK[4]], writes=[("hT",)], note=f"hadd{c}")
        for half in range(2):
            K.op("tensor", lambda e, half=half: e.transpose(
                    PS_BANKS[7][:, 0:128], hT_cur[:, half * 128:(half + 1) * 128], eyef[:]),
                 reads=[("hT",)] + EYEF_T, writes=[PSTOK[7]], note=f"hTr{c},{half}")
            K.op("vector", lambda e, half=half: e.tensor_copy(
                    htF[:, half * 128:(half + 1) * 128], PS_BANKS[7][:, 0:128]),
                 reads=[PSTOK[7]], writes=[("htF",)], note=f"htF{c},{half}")
        K.op("vector", lambda e: e.tensor_mul(
                h0b[:].rearrange("p (a b) -> p a b", b=256),
                htF[:].unsqueeze(1).broadcast_to([128, 8, 256]),
                blockmask[:].rearrange("p (a b) -> p a b", b=256)),
             reads=[("htF",), ("blockm",)], writes=[("h0b",)], note=f"h0b{c}")

        K.op("vector", lambda e, uT=uT: e.tensor_mul(uD[:], uT[:], dskip_rep[:]),
             reads=[uTtok, ("dskiprep",)], writes=[("uD",)], note=f"uD{c}")
        yc = y_c[0]
        for b in range(4):
            K.op("vector", lambda e, b=b, yc=yc: e.tensor_add(
                    yc[:, b * 512:(b + 1) * 512], PS_BANKS[b][:, :], uD[:, b * 512:(b + 1) * 512]),
                 reads=[PSTOK[b], ("uD",)], writes=[("yc", 0)], note=f"yev{c},{b}")
        K.dma("spill", "gpsimd", lambda e, c=c, yc=yc: e.dma_start(
                out=y_scr[c * LC:(c + 1) * LC, :], in_=yc[:]),
              reads=[("yc", 0)], writes=[("yscr", c)], note=f"yspill{c}")

    if "y_all" in taps:
        def store_y(dst):
            K.dma("dbg", "gpsimd", lambda e: e.dma_start(out=dst[:], in_=y_scr[:]),
                  reads=[("yscr", c) for c in range(NC)])
        tap("y_all", ((T, DI), BF16, store_y))

    close_arena(scan_arena, "m_scan")
    close_arena(sp_arena, "m_spersist")

    epi = arena()
    sbe = mk_sb(epi)
    normw_rep = sbe("m_normw", (128, DI))
    K.dma("const", "sync", lambda e: e.dma_start(out=normw_rep[:], in_=norm_w[:].broadcast_to([128, DI])),
          writes=[("normw",)])
    z_sl = [sbe(f"m_zsl{i}", (128, DI), BF16) for i in range(4)]
    res_sl = [sbe(f"m_rsl{i}", (128, DI), BF16) for i in range(4)]
    y_sl = [sbe(f"m_ysl{i}", (128, DI), BF16) for i in range(2)]
    gbuf = [sbe(f"m_g{i}", (128, DI)) for i in range(2)]
    gsq = sbe("m_gsq", (128, DI), BF16)
    ssq = sbe("m_ssq", (128, 8))
    gn_c = [sbe(f"m_gn{i}", (128, DI), BF16) for i in range(2)]
    gnT_sl = [sbe(f"m_gnT{i}", (128, 512), BF16) for i in range(16)]
    oacc = [sbe(f"m_oacc{i}", (128, 512)) for i in range(4)]
    wload_e = mk_wstream("e", sbe, n_stage=4, cols=512)

    tap_simple("xnf0_epi", xn_f[0][:], (128, T), BF16, XNF(0) + [("yscr", c) for c in range(NC)])
    for sl in range(4):
        for (wsrc, dst, dtok, act) in ((w_inT, z_sl, "zsl", True), (w_resT, res_sl, "rsl", False)):
            for zs in range(4):
                if DEBUG_EPI_BARRIERS[0]:
                    K.barrier("dbg_epi")
                for kc in range(8):
                    wt, wtok = wload_e(
                        (lambda kc=kc, zs=zs, wsrc=wsrc: wsrc[kc * 128:(kc + 1) * 128,
                                                              zs * 512:(zs + 1) * 512]),
                        512, note=f"{dtok}w{sl}")
                    for tc in range(4):
                        c = sl * 4 + tc
                        K.op("tensor", lambda e, wt=wt, kc=kc, tc=tc, c=c, zs=zs:
                             e.matmul(PS_BANKS[tc][:, :], xn_f[kc][:, c * LC:(c + 1) * LC],
                                      wt[:, 0:512], start=(kc == 0), stop=(kc == 7)),
                             reads=[wtok, ("m" + "xnf", kc, c)], writes=[PSTOK[tc]],
                             note=f"{dtok}mm{sl},{zs},{kc},{tc}")
                for tc in range(4):
                    if act:
                        K.op("scalar", lambda e, tc=tc, zs=zs, dst=dst: e.activation(
                                dst[tc][:, zs * 512:(zs + 1) * 512], PS_BANKS[tc][:, :], AF.Silu),
                             reads=[PSTOK[tc]], writes=[(dtok, tc, zs)], note=f"{dtok}ev{sl},{zs},{tc}")
                    else:
                        K.op("vector", lambda e, tc=tc, zs=zs, dst=dst: e.tensor_copy(
                                dst[tc][:, zs * 512:(zs + 1) * 512], PS_BANKS[tc][:, :]),
                             reads=[PSTOK[tc]], writes=[(dtok, tc, zs)], note=f"{dtok}ev{sl},{zs},{tc}")
        if sl == 0:
            tap_simple("z_c0", z_sl[0][:], (128, DI), BF16, [("zsl", 0, zs) for zs in range(4)])
            tap_simple("res_c0", res_sl[0][:], (128, DI), BF16, [("rsl", 0, zs) for zs in range(4)])
        for tc in range(4):
            if DEBUG_EPI_BARRIERS[0]:
                K.barrier("dbg_epi2")
            c = sl * 4 + tc
            ysl = y_sl[tc % 2]
            K.dma("xload", "sync", lambda e, c=c, ysl=ysl: e.dma_start(
                    out=ysl[:], in_=y_scr[c * LC:(c + 1) * LC, :]),
                  reads=[("yscr", c)], writes=[("ysl", tc % 2)])
            g = gbuf[tc % 2]
            gtok = ("g", tc % 2)
            K.op("vector", lambda e, ysl=ysl, tc=tc, g=g: e.tensor_mul(g[:], ysl[:], z_sl[tc][:]),
                 reads=[("ysl", tc % 2)] + [("zsl", tc, zs) for zs in range(4)],
                 writes=[gtok], note=f"gmul{c}")
            K.op("vector", lambda e, tc=tc, g=g: e.tensor_add(g[:], g[:], res_sl[tc][:]),
                 reads=[gtok] + [("rsl", tc, zs) for zs in range(4)],
                 writes=[gtok], note=f"gadd{c}")
            K.op("scalar", lambda e, g=g, tc=tc: e.activation(
                    gsq[:], g[:], AF.Square, accum_out=ssq[:, tc:tc + 1]),
                 reads=[gtok], writes=[("gsq",), ("ssq", tc)], note=f"gsq{c}", tail=True)
            K.op("scalar", lambda e, tc=tc: e.activation(
                    ssq[:, 4 + tc:5 + tc], ssq[:, tc:tc + 1], AF.Sqrt, scale=CONST(1.0 / DI),
                    bias=CONST(1e-6)),
                 reads=[("ssq", tc)] + CPOOL_T, writes=[("ssq2", tc)], tail=True)
            K.op("vector", lambda e, tc=tc: e.reciprocal(ssq[:, tc:tc + 1], ssq[:, 4 + tc:5 + tc]),
                 reads=[("ssq2", tc)], writes=[("ssq", tc)], tail=True)
            gn = gn_c[tc % 2]
            K.op("vector", lambda e, g=g, gn=gn, tc=tc: e.scalar_tensor_tensor(
                    gn[:], g[:], ssq[:, tc:tc + 1], normw_rep[:], ALU.mult, ALU.mult),
                 reads=[gtok, ("ssq", tc), ("normw",)], writes=[("gn", tc % 2)], note=f"gn{c}")
            if sl == 0 and tc == 0:
                tap_simple("gn_c0", gn_c[0][:], (128, DI), BF16, [("gn", 0)])
            for fc in range(16):
                K.dma("gtrans", "scalar", lambda e, fc=fc, tc=tc, gn=gn: e.dma_start_transpose(
                        gnT_sl[fc][:, tc * 128:(tc + 1) * 128], gn[:, fc * 128:(fc + 1) * 128]),
                      reads=[("gn", tc % 2)], writes=[("gnT", fc, tc)], note=f"gnT{c},{fc}")
        for half in range(2):
            for kc in range(16):
                wt, wtok = wload_e(
                    (lambda kc=kc, half=half: w_outT[kc * 128:(kc + 1) * 128,
                                                     half * 512:(half + 1) * 512]),
                    512, note=f"woutw{sl}")
                for tc in range(4):
                    K.op("tensor", lambda e, wt=wt, kc=kc, tc=tc, half=half:
                         e.matmul(PS_BANKS[4 + tc][:, :], gnT_sl[kc][:, tc * 128:(tc + 1) * 128],
                                  wt[:, 0:512], start=(kc == 0), stop=(kc == 15)),
                         reads=[wtok, ("gnT", kc, tc)], writes=[PSTOK[4 + tc]],
                         note=f"wout{sl},{half},{kc},{tc}")
            for tc in range(4):
                c = sl * 4 + tc
                K.op("vector", lambda e, tc=tc: e.tensor_copy(oacc[tc][:], PS_BANKS[4 + tc][:, :]),
                     reads=[PSTOK[4 + tc]], writes=[("oacc", tc)], note=f"oev{sl},{half},{tc}")
                accum_to_xs(oacc[tc][:], (c * LC, (c + 1) * LC), (half * 512, (half + 1) * 512),
                            ("oacc", tc), c, note=f"oxs{sl},{half},{tc}")

    close_arena(epi, "m_epi")
    close_arena(sl1, "sl1")

    def emit_ffn(ph):
        fa = arena()
        sbf = mk_sb(fa)
        psf = mk_ps(fa)
        BK = [psf(f"{ph}bank{i}", (128, 512)) for i in range(8)]
        BT = [(f"{ph}psb", i) for i in range(8)]
        xnf = [sbf(f"{ph}xnf{fc}", (128, T), BF16) for fc in range(8)]
        la = arena()
        ln_phase(ph, mk_sb(la), None, xnf)
        close_arena(la, ph + "ln")
        XT = lambda kc: [(ph + "xnf", kc, c) for c in range(NC)]

        wl = mk_wstream(ph, sbf, n_stage=6, cols=128)

        wr_stg = sbf(f"{ph}wrs", (128, 64))
        wr_bf = sbf(f"{ph}wrb", (128, 64), BF16)
        K.dma("wload", "sync", lambda e: e.dma_start(
                out=wr_stg[:].rearrange("p (a c) -> p a c", c=8),
                in_=wrT[:].rearrange("(a b) c -> b a c", b=128)),
              writes=[(ph + "wrs",)])
        K.op("vector", lambda e: e.tensor_copy(wr_bf[:], wr_stg[:]),
             reads=[(ph + "wrs",)], writes=[(ph + "wrb",)])
        expr = sbf(f"{ph}expr", (128, 8))
        sumr = sbf(f"{ph}sumr", (128, 2))
        probs7 = sbf(f"{ph}probs7", (128, 16))
        for c in range(NC):
            for kc in range(8):
                K.op("tensor", lambda e, c=c, kc=kc: e.matmul(
                        BK[7][0:128, 0:8], xnf[kc][:, c * LC:(c + 1) * LC],
                        wr_bf[:, kc * 8:(kc + 1) * 8], start=(kc == 0), stop=(kc == 7)),
                     reads=[(ph + "wrb",), (ph + "xnf", kc, c)], writes=[BT[7]],
                     note=f"{ph}r{c}")
            K.op("scalar", lambda e, c=c: e.activation(
                    expr[:], BK[7][0:128, 0:8], AF.Exp, accum_out=sumr[:, 0:1]),
                 reads=[BT[7]], writes=[(ph + "expr",), (ph + "sumr",)], tail=True,
                 note=f"{ph}rexp{c}")
            K.op("vector", lambda e: e.reciprocal(sumr[:, 1:2], sumr[:, 0:1]),
                 reads=[(ph + "sumr",)], writes=[(ph + "sumr", 2)], tail=True)
            K.op("vector", lambda e, c=c: e.tensor_scalar_mul(
                    probs7[:, c:c + 1], expr[:, 7:8], sumr[:, 1:2]),
                 reads=[(ph + "expr",), (ph + "sumr", 2)], writes=[(ph + "p7", c)], tail=True)
        p7T = sbf(f"{ph}p7T", (16, 128))
        p7row = sbf(f"{ph}p7row", (1, T))
        probs_rep = sbf(f"{ph}prep", (128, T))
        p7_scr = nc.dram_tensor(f"{ph}p7scr", [16, 128], F32, kind="Internal").ap()
        K.op("tensor", lambda e: e.transpose(BK[6][0:16, 0:128], probs7[:], eyef[:]),
             reads=[(ph + "p7", c) for c in range(NC)] + EYEF_T, writes=[BT[6]])
        K.op("vector", lambda e: e.tensor_copy(p7T[:], BK[6][0:16, 0:128]),
             reads=[BT[6]], writes=[(ph + "p7T",)])
        K.dma("bcast", "gpsimd", lambda e: e.dma_start(out=p7_scr[:], in_=p7T[:]),
              reads=[(ph + "p7T",)], writes=[(ph + "p7scr",)])
        K.dma("bcast", "gpsimd", lambda e: e.dma_start(
                out=p7row[0:1, :], in_=p7_scr[:]),
              reads=[(ph + "p7scr",)], writes=[(ph + "p7row",)])
        for zs in range(4):
            K.op("tensor", lambda e, zs=zs: e.matmul(
                    BK[6][:, :], ones1[:], p7row[0:1, zs * 512:(zs + 1) * 512],
                    start=True, stop=True),
                 reads=ONES_T + [(ph + "p7row",)], writes=[BT[6]], note=f"{ph}prep{zs}")
            K.op("vector", lambda e, zs=zs: e.tensor_copy(
                    probs_rep[:, zs * 512:(zs + 1) * 512], BK[6][:, :]),
                 reads=[BT[6]], writes=[(ph + "prep", zs)], note=f"{ph}prepe{zs}")

        b1_16 = sbf(f"{ph}b116", (32, 128))
        b1c = sbf(f"{ph}b1c", (128, 32))
        b2_16 = sbf(f"{ph}b216", (8, 128))
        b2c = sbf(f"{ph}b2c", (128, 8))
        K.dma("const", "sync", lambda e: e.dma_start(
                out=b1_16[:], in_=b1[:].rearrange("o (a b) -> (o a) b", b=128)),
              writes=[(ph + "b116",)])
        K.op("tensor", lambda e: e.transpose(BK[6][0:128, 0:32], b1_16[:], eyef[0:32, 0:32]),
             reads=[(ph + "b116",)] + EYEF_T, writes=[BT[6]])
        K.op("vector", lambda e: e.tensor_copy(b1c[:], BK[6][0:128, 0:32]),
             reads=[BT[6]], writes=[(ph + "b1c",)])
        K.dma("const", "sync", lambda e: e.dma_start(
                out=b2_16[:], in_=b2[:].rearrange("o (a b) -> (o a) b", b=128)),
              writes=[(ph + "b216",)])
        K.op("tensor", lambda e: e.transpose(BK[6][0:128, 0:8], b2_16[:], eyef[0:8, 0:8]),
             reads=[(ph + "b216",)] + EYEF_T, writes=[BT[6]])
        K.op("vector", lambda e: e.tensor_copy(b2c[:], BK[6][0:128, 0:8]),
             reads=[BT[6]], writes=[(ph + "b2c",)])

        h1g = [sbf(f"{ph}h1g{m}", (128, T), BF16) for m in range(32)]
        for m in range(32):
            for kc in range(8):
                wt, wtok = wl((lambda kc=kc, m=m: w1[kc * 128:(kc + 1) * 128,
                                                    m * 128:(m + 1) * 128]), 128,
                              note=f"{ph}w1{m}")
                for ns in range(4):
                    K.op("tensor", lambda e, wt=wt, kc=kc, ns=ns, m=m: e.matmul(
                            BK[(m % 2) * 4 + ns][:, :], wt[:, 0:128],
                            xnf[kc][:, ns * 512:(ns + 1) * 512],
                            start=(kc == 0), stop=(kc == 7)),
                         reads=[wtok] + XT(kc), writes=[BT[(m % 2) * 4 + ns]],
                         note=f"{ph}w1mm{m},{kc},{ns}")
            for ns in range(4):
                K.op("scalar", lambda e, m=m, ns=ns: e.activation(
                        h1g[m][:, ns * 512:(ns + 1) * 512], BK[(m % 2) * 4 + ns][:, :],
                        AF.Gelu, bias=b1c[:, m:m + 1]),
                     reads=[BT[(m % 2) * 4 + ns], (ph + "b1c",)],
                     writes=[(ph + "h1g", m, ns)], note=f"{ph}gelu{m},{ns}")

        outf = [sbf(f"{ph}outf{i}", (128, T)) for i in range(2)]
        otok = [sbf(f"{ph}otok{i}", (128, 128)) for i in range(4)]
        for m in range(8):
            g = (m % 2) * 4
            for kc in range(32):
                wt, wtok = wl((lambda kc=kc, m=m: w2[kc * 128:(kc + 1) * 128,
                                                    m * 128:(m + 1) * 128]), 128,
                              note=f"{ph}w2{m}")
                for ns in range(4):
                    K.op("tensor", lambda e, wt=wt, kc=kc, ns=ns, g=g, m=m: e.matmul(
                            BK[g + ns][:, :], wt[:, 0:128],
                            h1g[kc][:, ns * 512:(ns + 1) * 512],
                            start=(kc == 0), stop=(kc == 31)),
                         reads=[wtok, (ph + "h1g", kc, ns)], writes=[BT[g + ns]],
                         note=f"{ph}w2mm{m},{kc},{ns}")
            of = outf[m % 2]
            oft = (ph + "outf", m % 2)
            for ns in range(4):
                K.op("vector", lambda e, m=m, ns=ns, g=g, of=of: e.scalar_tensor_tensor(
                        of[:, ns * 512:(ns + 1) * 512], BK[g + ns][:, :], b2c[:, m:m + 1],
                        probs_rep[:, ns * 512:(ns + 1) * 512], ALU.add, ALU.mult),
                     reads=[BT[g + ns], (ph + "b2c",)] + [(ph + "prep", ns)],
                     writes=[oft], note=f"{ph}oev{m},{ns}")
            for tc in range(NC):
                tbk = ((m + 1) % 2) * 4 + (tc % 4)
                K.op("tensor", lambda e, tc=tc, of=of, tbk=tbk: e.transpose(
                        BK[tbk][:, 0:128], of[:, tc * 128:(tc + 1) * 128], eyef[:]),
                     reads=[oft] + EYEF_T, writes=[BT[tbk]], note=f"{ph}otr{m},{tc}")
                oa = otok[tc % 4]
                K.op("vector", lambda e, tc=tc, oa=oa, tbk=tbk: e.tensor_copy(
                        oa[:], BK[tbk][:, 0:128]),
                     reads=[BT[tbk]], writes=[(ph + "otok", tc % 4)], note=f"{ph}oco{m},{tc}")
                K.dma("xacc", "gpsimd",
                      lambda e, tc=tc, m=m, oa=oa: e.dma_start(
                          out=xs[tc * LC:(tc + 1) * LC, m * 128:(m + 1) * 128],
                          in_=oa[:], accum_op=ALU.add),
                      reads=[(ph + "otok", tc % 4), ("xs", tc)], writes=[("xs", tc)],
                      note=f"{ph}oacc{m},{tc}")
        close_arena(fa, ph)

    if n_sublayers >= 2:
        emit_ffn("f1")

    def emit_attn(ph="a"):
        aa = arena()
        sba = mk_sb(aa)
        psa = mk_ps(aa)
        SC = psa(f"{ph}sc", (128, 2048))
        BKrest = [psa(f"{ph}bank{i}", (128, 512)) for i in range(4)]
        BK = [SC[:, i * 512:(i + 1) * 512] for i in range(4)] + BKrest
        BT = [(f"{ph}psb", i) for i in range(8)]
        xnf = [sba(f"{ph}xnf{fc}", (128, T), BF16) for fc in range(8)]
        la = arena()
        ln_phase(ph, mk_sb(la), None, xnf)
        close_arena(la, ph + "ln")
        XT = lambda kc: [(ph + "xnf", kc, c) for c in range(NC)]
        wl = mk_wstream(ph, sba, n_stage=6, cols=128)

        q_f = [sba(f"{ph}qf{m}", (128, T), BF16) for m in range(8)]
        for m in range(8):
            for kc in range(8):
                wt, wtok = wl((lambda kc=kc, m=m: wq[kc * 128:(kc + 1) * 128,
                                                    m * 128:(m + 1) * 128]), 128,
                              note=f"{ph}wq{m}")
                for ns in range(4):
                    K.op("tensor", lambda e, wt=wt, kc=kc, ns=ns, m=m: e.matmul(
                            BK[(m % 2) * 4 + ns][:, :], wt[:, 0:128],
                            xnf[kc][:, ns * 512:(ns + 1) * 512],
                            start=(kc == 0), stop=(kc == 7)),
                         reads=[wtok] + XT(kc), writes=[BT[(m % 2) * 4 + ns]],
                         note=f"{ph}qmm{m},{kc},{ns}")
            for ns in range(4):
                K.op("vector", lambda e, m=m, ns=ns: e.tensor_scalar_mul(
                        q_f[m][:, ns * 512:(ns + 1) * 512], BK[(m % 2) * 4 + ns][:, :],
                        1.0 / np.sqrt(128.0)),
                     reads=[BT[(m % 2) * 4 + ns]], writes=[(ph + "qf", m, ns)],
                     note=f"{ph}qev{m},{ns}")
        klat = sba(f"{ph}klat", (64, T), BF16)
        vlat = sba(f"{ph}vlat", (64, T), BF16)
        for (wsrc, dst, dtok) in ((wk_down, klat, "klat"), (wv_down, vlat, "vlat")):
            for kc in range(8):
                wt, wtok = wl((lambda kc=kc, wsrc=wsrc: wsrc[kc * 128:(kc + 1) * 128, 0:64]),
                              64, note=f"{ph}{dtok}")
                for ns in range(4):
                    K.op("tensor", lambda e, wt=wt, kc=kc, ns=ns: e.matmul(
                            BK[4 + ns][0:64, :], wt[:, 0:64],
                            xnf[kc][:, ns * 512:(ns + 1) * 512],
                            start=(kc == 0), stop=(kc == 7)),
                         reads=[wtok] + XT(kc), writes=[BT[4 + ns]],
                         note=f"{ph}{dtok}mm{kc},{ns}")
            for ns in range(4):
                K.op("vector", lambda e, ns=ns, dst=dst: e.tensor_copy(
                        dst[:, ns * 512:(ns + 1) * 512], BK[4 + ns][0:64, :]),
                     reads=[BT[4 + ns]], writes=[(ph + dtok, ns)], note=f"{ph}{dtok}ev{ns}")

        wup_stg = sba(f"{ph}wups", (64, 128))
        wkup2 = sba(f"{ph}wkup2", (64, 128), BF16)
        wvup2 = sba(f"{ph}wvup2", (64, 128), BF16)
        for (hbm, dst, tok) in ((wk_up, wkup2, "wkup2"), (wv_up, wvup2, "wvup2")):
            for kv in range(2):
                K.dma("wload", "sync", lambda e, hbm=hbm, kv=kv: e.dma_start(
                        out=wup_stg[kv * 32:(kv + 1) * 32, :], in_=hbm[:]),
                      writes=[(ph + "wups", kv)])
            K.op("vector", lambda e, dst=dst: e.tensor_copy(dst[:], wup_stg[:]),
                 reads=[(ph + "wups", 0), (ph + "wups", 1)], writes=[(ph + tok,)])
        k_f = [sba(f"{ph}kf{kv}", (128, T), BF16) for kv in range(2)]
        v_tok = [sba(f"{ph}vt{kv}", (128, T), BF16) for kv in range(2)]
        for kv in range(2):
            for ns in range(4):
                K.op("tensor", lambda e, kv=kv, ns=ns: e.matmul(
                        BK[ns][:, :], wkup2[kv * 32:(kv + 1) * 32, :],
                        klat[kv * 32:(kv + 1) * 32, ns * 512:(ns + 1) * 512],
                        start=True, stop=True),
                     reads=[(ph + "wkup2",)] + [(ph + "klat", ns)], writes=[BT[ns]],
                     note=f"{ph}kup{kv},{ns}")
                K.op("vector", lambda e, kv=kv, ns=ns: e.tensor_copy(
                        k_f[kv][:, ns * 512:(ns + 1) * 512], BK[ns][:, :]),
                     reads=[BT[ns]], writes=[(ph + "kf", kv, ns)], note=f"{ph}kev{kv},{ns}")
            for tc in range(NC):
                K.op("tensor", lambda e, kv=kv, tc=tc: e.matmul(
                        BK[4 + tc % 2][:, 0:128], vlat[kv * 32:(kv + 1) * 32, tc * LC:(tc + 1) * LC],
                        wvup2[kv * 32:(kv + 1) * 32, :], start=True, stop=True),
                     reads=[(ph + "wvup2",)] + [(ph + "vlat", tc // 4)], writes=[BT[4 + tc % 2]],
                     note=f"{ph}vup{kv},{tc}")
                K.op("vector", lambda e, kv=kv, tc=tc: e.tensor_copy(
                        v_tok[kv][:, tc * 128:(tc + 1) * 128], BK[4 + tc % 2][:, 0:128]),
                     reads=[BT[4 + tc % 2]], writes=[(ph + "vt", kv, tc)], note=f"{ph}vev{kv},{tc}")

        OT = [sba(f"{ph}ot{m}", (128, T), BF16) for m in range(8)]
        P_sb = [sba(f"{ph}P{i}", (128, T), BF16) for i in range(2)]
        PT_t = [[sba(f"{ph}PT{i}_{k}", (128, 128), BF16) for k in range(NC)] for i in range(2)]
        mxr = [sba(f"{ph}mx{i}", (128, 8)) for i in range(2)]
        rnormA = [sba(f"{ph}rnA{m}", (128, 16)) for m in range(8)]

        def attn_front(h, qc):
            kv = h // 4
            nk = (qc + 1) * 128
            nsb = (nk + 511) // 512
            mx = mxr[qc % 2]
            for sb_ in range(nsb):
                w = min(512, nk - sb_ * 512)
                K.op("tensor", lambda e, h=h, qc=qc, sb_=sb_, w=w, kv=kv: e.matmul(
                        BK[sb_][:, 0:w], q_f[h][:, qc * LC:(qc + 1) * LC],
                        k_f[kv][:, sb_ * 512: sb_ * 512 + w], start=True, stop=True),
                     reads=[(ph + "qf", h, qc // 4)] + [(ph + "kf", kv, n_) for n_ in range(4)],
                     writes=[BT[sb_]], note=f"{ph}sc{h},{qc},{sb_}")
            dsb = (nk - 128) // 512
            doff = (nk - 128) % 512
            K.op("vector", lambda e, dsb=dsb, doff=doff: e.tensor_add(
                    BK[dsb][:, doff:doff + 128], BK[dsb][:, doff:doff + 128], maska[:]),
                 reads=[BT[dsb]] + MASKA_T, writes=[BT[dsb]], note=f"{ph}msk{h},{qc}")
            sctoks = [BT[s_] for s_ in range(nsb)]
            K.op("vector", lambda e, nk=nk, mx=mx: e.tensor_reduce(
                    mx[:, 5:6], SC[:, 0:nk], mybir.AxisListType.X, ALU.max, negate=True),
                 reads=sctoks, writes=[(ph + "mx", qc % 2, "n")], tail=True,
                 note=f"{ph}mxn{h},{qc}")
            P = P_sb[qc % 2]
            K.op("scalar", lambda e, nk=nk, P=P, mx=mx: e.activation(
                    P[:, 0:nk], SC[:, 0:nk], AF.Exp, bias=mx[:, 5:6],
                    accum_out=mx[:, 3:4]),
                 reads=sctoks + [(ph + "mx", qc % 2, "n")],
                 writes=[(ph + "P", qc % 2, sb_) for sb_ in range(nsb)] +
                        [(ph + "rsum", qc % 2)],
                 tail=True, note=f"{ph}exp{h},{qc}")
            K.op("vector", lambda e, h=h, qc=qc, mx=mx: e.reciprocal(
                    rnormA[h][:, qc:qc + 1], mx[:, 3:4]),
                 reads=[(ph + "rsum", qc % 2)], writes=[(ph + "rnA", h, qc)], tail=True,
                 note=f"{ph}rn{h},{qc}")

        def attn_back(h, qc):
            kv = h // 4
            P = P_sb[qc % 2]
            for kb in range(qc + 1):
                PT = PT_t[qc % 2][kb]
                tb = 6 + kb % 2
                K.op("tensor", lambda e, kb=kb, P=P, tb=tb: e.transpose(
                        BK[tb][:, 0:64].bitcast(BF16), P[:, kb * 128:(kb + 1) * 128], eyeb[:]),
                     reads=[(ph + "P", qc % 2, kb * 128 // 512)] + EYEB_T,
                     writes=[BT[tb]], note=f"{ph}ptr{h},{qc},{kb}")
                K.op("vector" if kb % 2 == 0 else "scalar",
                     (lambda e, kb=kb, PT=PT, tb=tb: e.tensor_copy(
                         PT[:], BK[tb][:, 0:64].bitcast(BF16))) if kb % 2 == 0 else
                     (lambda e, kb=kb, PT=PT, tb=tb: e.copy(
                         PT[:], BK[tb][:, 0:64].bitcast(BF16))),
                     reads=[BT[tb]], writes=[(ph + "PT", qc % 2, kb)],
                     note=f"{ph}pte{h},{qc},{kb}")
            for kb in range(qc + 1):
                PT = PT_t[qc % 2][kb]
                K.op("tensor", lambda e, kb=kb, kv=kv, qc=qc, PT=PT: e.matmul(
                        BK[4 + qc % 2][:, 0:128], v_tok[kv][:, kb * 128:(kb + 1) * 128],
                        PT[:], start=(kb == 0), stop=(kb == qc)),
                     reads=[(ph + "PT", qc % 2, kb), (ph + "vt", kv, kb)],
                     writes=[BT[4 + qc % 2]], note=f"{ph}pv{h},{qc},{kb}")
            K.op("vector", lambda e, h=h, qc=qc: e.tensor_copy(
                    OT[h][:, qc * 128:(qc + 1) * 128], BK[4 + qc % 2][:, 0:128]),
                 reads=[BT[4 + qc % 2]], writes=[(ph + "ot", h, qc)], note=f"{ph}oev{h},{qc}")

        prev = None
        for h in range(NH):
            for qc in range(NC):
                attn_front(h, qc)
                if prev is not None:
                    attn_back(*prev)
                prev = (h, qc)
        attn_back(*prev)

        rnT = sba(f"{ph}rnT", (16, 128))
        rnrow = sba(f"{ph}rnrow", (1, T))
        rnrep = sba(f"{ph}rnrep", (128, T))
        rn_scr = nc.dram_tensor(f"{ph}rnscr", [16, 128], F32, kind="Internal").ap()
        for h in range(NH):
            K.op("tensor", lambda e, h=h: e.transpose(BK[6][0:16, 0:128], rnormA[h][:], eyef[:]),
                 reads=[(ph + "rnA", h, qc) for qc in range(NC)] + EYEF_T, writes=[BT[6]],
                 note=f"{ph}rnT{h}")
            K.op("vector", lambda e: e.tensor_copy(rnT[:], BK[6][0:16, 0:128]),
                 reads=[BT[6]], writes=[(ph + "rnT",)], note=f"{ph}rnTe{h}")
            K.dma("bcast", "gpsimd", lambda e: e.dma_start(out=rn_scr[:], in_=rnT[:]),
                  reads=[(ph + "rnT",)], writes=[(ph + "rnscr",)])
            K.dma("bcast", "gpsimd", lambda e: e.dma_start(out=rnrow[0:1, :], in_=rn_scr[:]),
                  reads=[(ph + "rnscr",)], writes=[(ph + "rnrow",)])
            for zs in range(4):
                K.op("tensor", lambda e, zs=zs: e.matmul(
                        BK[6][:, :], ones1[:], rnrow[0:1, zs * 512:(zs + 1) * 512],
                        start=True, stop=True),
                     reads=ONES_T + [(ph + "rnrow",)], writes=[BT[6]], note=f"{ph}rnb{h},{zs}")
                K.op("vector", lambda e, zs=zs: e.tensor_copy(
                        rnrep[:, zs * 512:(zs + 1) * 512], BK[6][:, :]),
                     reads=[BT[6]], writes=[(ph + "rnrep", zs)], note=f"{ph}rne{h},{zs}")
            K.op("vector", lambda e, h=h: e.tensor_mul(OT[h][:], OT[h][:], rnrep[:]),
                 reads=[(ph + "ot", h, qc) for qc in range(NC)] +
                       [(ph + "rnrep", zs) for zs in range(4)],
                 writes=[(ph + "otn", h)], note=f"{ph}on{h}")

        wo_sb = [sba(f"{ph}wo{oc}", (128, DM), BF16) for oc in range(8)]
        wo_stg = [sba(f"{ph}wos{i}", (128, DM)) for i in range(2)]
        for oc in range(8):
            stgt = wo_stg[oc % 2]
            K.dma("wload", "sync", lambda e, oc=oc, stgt=stgt: e.dma_start(
                    out=stgt[:], in_=wo[oc * 128:(oc + 1) * 128, :]),
                  writes=[(ph + "wos", oc % 2)])
            K.op("vector", lambda e, oc=oc, stgt=stgt: e.tensor_copy(wo_sb[oc][:], stgt[:]),
                 reads=[(ph + "wos", oc % 2)], writes=[(ph + "wo", oc)])
        oacc = [sba(f"{ph}oacc{i}", (128, 512)) for i in range(4)]
        for tc in range(NC):
            for half in range(2):
                b = (tc % 2) * 4 + half
                for oc in range(8):
                    K.op("tensor", lambda e, tc=tc, half=half, oc=oc, b=b: e.matmul(
                            BK[b][:, :], OT[oc][:, tc * LC:(tc + 1) * LC],
                            wo_sb[oc][:, half * 512:(half + 1) * 512],
                            start=(oc == 0), stop=(oc == 7)),
                         reads=[(ph + "otn", oc), (ph + "wo", oc)], writes=[BT[b]],
                         note=f"{ph}wo{tc},{half},{oc}")
                oa = oacc[(tc % 2) * 2 + half]
                K.op("vector", lambda e, b=b, oa=oa: e.tensor_copy(oa[:], BK[b][:, :]),
                     reads=[BT[b]], writes=[(ph + "oacc", (tc % 2) * 2 + half)],
                     note=f"{ph}woev{tc},{half}")
                K.dma("xacc", "gpsimd", lambda e, tc=tc, half=half, oa=oa: e.dma_start(
                        out=xs[tc * LC:(tc + 1) * LC, half * 512:(half + 1) * 512],
                        in_=oa[:], accum_op=ALU.add),
                      reads=[(ph + "oacc", (tc % 2) * 2 + half), ("xs", tc)],
                      writes=[("xs", tc)], note=f"{ph}woacc{tc},{half}")
        close_arena(aa, ph)

    if n_sublayers >= 3:
        emit_attn("a")
    if n_sublayers >= 4:
        emit_ffn("f2")

    for c in range(NC):
        K.dma("ostore", "gpsimd",
              lambda e, c=c: e.dma_start(out=out[c * LC:(c + 1) * LC, :], in_=xs[c * LC:(c + 1) * LC, :]),
              reads=[("xs", c)], writes=[("outd", c)])

    fin = gsb("fin", (1, 1))
    K.op("vector", lambda e: e.memset(fin[:], 1.0), reads=[("outd", c) for c in range(NC)],
         writes=[("fin",)], note="fence")

    K.lower()
    for s in stacks:
        try:
            s.close()
        except Exception:
            pass
    return nc, dbg



def _mask_mamba():
    s = np.arange(128)[:, None]
    t = np.arange(128)[None, :]
    return np.where(t >= s, 0.0, MASKBIG).astype(np.float32)


def _blockmask():
    import ml_dtypes
    m = np.zeros((128, DI), np.float32)
    for h in range(8):
        m[h * 16:(h + 1) * 16, h * 256:(h + 1) * 256] = 1.0
    return m.astype(ml_dtypes.bfloat16)


def _mask_attn():
    q = np.arange(128)[:, None]
    k = np.arange(128)[None, :]
    return np.where(k <= q, 0.0, -MASKBIG).astype(np.float32)


def host_inputs(inputs, core):
    import ml_dtypes
    f = np.ascontiguousarray
    x = inputs["L0_attn_a_x"]
    d = {
        "x0": f(np.asarray(x[core], dtype=np.float32)),
        "w_inT": f(np.asarray(inputs["L0_attn_a_W_in"]).T),
        "w_dtT": f(np.asarray(inputs["L0_attn_a_W_dt"]).T),
        "conv_w": f(np.asarray(inputs["L0_attn_a_conv_w"])),
        "conv_b": f(np.asarray(inputs["L0_attn_a_conv_b"]).reshape(1, DI)),
        "a_log": f(np.asarray(inputs["L0_attn_a_A_log"]).reshape(1, H * DS)),
        "dskip": f(np.asarray(inputs["L0_attn_a_Dskip"]).reshape(1, DI)),
        "dt_bias": f(np.asarray(inputs["L0_attn_a_dt_bias"]).reshape(H, 1)),
        "norm_w": f(np.asarray(inputs["L0_attn_a_norm_w"]).reshape(1, DI)),
        "w_resT": f(np.asarray(inputs["L0_attn_a_W_res"]).T),
        "w_outT": f(np.asarray(inputs["L0_attn_a_W_out"]).T),
        "wq": f(np.asarray(inputs["L0_attn_b_wq"])),
        "wk_down": f(np.asarray(inputs["L0_attn_b_wk_down"])),
        "wv_down": f(np.asarray(inputs["L0_attn_b_wv_down"])),
        "wk_up": f(np.asarray(inputs["L0_attn_b_wk_up"])),
        "wv_up": f(np.asarray(inputs["L0_attn_b_wv_up"])),
        "wo": f(np.asarray(inputs["L0_attn_b_wo"])),
        "wrT": f(np.asarray(inputs["L0_ffn_Wr"]).T),
        "w1": f(np.asarray(inputs["L0_ffn_W1"][7])),
        "b1": f(np.asarray(inputs["L0_ffn_b1"][7]).reshape(1, F1)),
        "w2": f(np.asarray(inputs["L0_ffn_W2"][7])),
        "b2": f(np.asarray(inputs["L0_ffn_b2"][7]).reshape(1, DM)),
        "cpool": np.broadcast_to(np.asarray(CONSTS, dtype=np.float32)[None, :],
                                 (128, len(CONSTS))).copy(),
        "eye_f": np.eye(128, dtype=np.float32),
        "eye_b": np.eye(128, dtype=ml_dtypes.bfloat16),
        "ones1": np.ones((1, 128), dtype=np.float32),
        "mask_mamba": _mask_mamba(),
        "mask_attn": _mask_attn(),
        "blockmask": _blockmask(),
    }
    return d


_PROG_CACHE = {}


def _get_prog(taps=()):
    key = tuple(sorted(taps))
    if key not in _PROG_CACHE:
        _PROG_CACHE[key] = build_program(taps=key)
    return _PROG_CACHE[key]


def kernel(**inputs):
    nc, dbg = _get_prog()
    in_maps = [host_inputs(inputs, core) for core in range(8)]
    res = run_bass_kernel_spmd(nc, in_maps, core_ids=list(range(8)))
    outs = np.stack([np.asarray(r["out"]) for r in res.results], axis=0)
    return outs.astype(np.float32)


# version 21 (final)
# speedup vs baseline: 1.0063x; 1.0063x over previous
import numpy as np
import concourse.bass as bass
import concourse.mybir as mybir
from concourse.bass_utils import run_bass_kernel_spmd

F32 = mybir.dt.float32
BF16 = mybir.dt.bfloat16
AF = mybir.ActivationFunctionType
ALU = mybir.AluOpType

T = 2048
NC = 16
LC = 128
DM = 1024
DI = 2048
H = 8
DS = 16
P = 256
DTR = 64
F1 = 4096
NKV = 2
HD = 128
DLAT = 32
NH = 8

MASKBIG = 1.0e30
DEBUG_EPI_BARRIERS = [False]

CONSTS = [1e-5, 1e-6, 1.0, 0.0, 1.0 / np.sqrt(128.0), 1.0 / DI, -1.0]
_CIDX = {v: i for i, v in enumerate(CONSTS)}

_COMPUTE_ENGINES = ("tensor", "vector", "scalar", "gpsimd")
_ALL_ENGINES = ("tensor", "vector", "scalar", "gpsimd", "sync")


class _Op:
    __slots__ = ("oid", "eng", "emit", "deps", "dma", "chan", "inc", "seq", "note", "tail")

    def __init__(self, oid, eng, emit, deps, dma, chan, note, tail):
        self.oid = oid
        self.eng = eng
        self.emit = emit
        self.deps = deps
        self.dma = dma
        self.chan = chan
        self.inc = False
        self.seq = 0
        self.note = note
        self.tail = tail


class Sched:

    def __init__(self, nc):
        self.nc = nc
        self.ops = []
        self.last_writer = {}
        self.readers = {}
        self.chan_engine = {}
        self.chan_ctr = {}
        self.eng_last = {}
        self.chan_last = {}

    def op(self, eng, emit, reads=(), writes=(), note="", tail=False):
        return self._add(eng, emit, reads, writes, dma=False, chan=None, note=note, tail=tail)

    ROT = {"wload": 8, "xload": 6, "xacc": 4, "spill": 4, "xtrans": 4,
           "gtrans": 4, "ptrans": 4, "ostore": 4, "xseed": 4}

    def dma(self, chan, eng, emit, reads=(), writes=(), note=""):
        r = self.ROT.get(chan, 2)
        ctr = self.chan_ctr.get(chan, 0)
        self.chan_ctr[chan] = ctr + 1
        chan = f"{chan}#{ctr % r}"
        if chan in self.chan_engine:
            assert self.chan_engine[chan] == eng, (chan, eng)
        else:
            self.chan_engine[chan] = eng
        extra = ()
        if chan in self.chan_last:
            extra = (self.chan_last[chan],)
        return self._add(eng, emit, reads, writes, dma=True, chan=chan, note=note,
                         tail=False, extra_deps=extra)

    def _add(self, eng, emit, reads, writes, dma, chan, note, tail=False, extra_deps=()):
        oid = len(self.ops)
        deps = set(extra_deps)
        for t in reads:
            w = self.last_writer.get(t)
            if w is not None:
                deps.add(w)
        for t in writes:
            w = self.last_writer.get(t)
            if w is not None:
                deps.add(w)
            for r in self.readers.get(t, ()):
                deps.add(r)
        deps.discard(oid)
        for t in reads:
            self.readers.setdefault(t, []).append(oid)
        for t in writes:
            self.last_writer[t] = oid
            self.readers[t] = []
        self.ops.append(_Op(oid, eng, emit, deps, dma, chan, note, tail))
        if dma:
            self.chan_last[chan] = oid
        else:
            self.eng_last[eng] = oid
        return oid

    def barrier(self, note="barrier"):
        snap = list(self.eng_last.values()) + list(self.chan_last.values())
        for e in _ALL_ENGINES:
            deps = [d for d in snap if self.ops[d].eng != e or self.ops[d].dma]
            self._add(e, lambda eng: eng.nop(), (), (), dma=False, chan=None,
                      note=note, tail=False, extra_deps=deps)

    def lower(self):
        nc = self.nc
        ops = self.ops
        for op in ops:
            if op.dma:
                op.inc = True
            real = []
            for d in op.deps:
                dop = ops[d]
                if (not op.dma) and (not dop.dma) and dop.eng == op.eng and not dop.tail:
                    continue
                real.append(d)
                dop.inc = True
            op.deps = real
        import contextlib
        self._stack = contextlib.ExitStack()
        sems = {}
        for e in _ALL_ENGINES:
            sems[("eng", e)] = self._stack.enter_context(nc.semaphore(f"sem_{e}"))
        for ch in self.chan_engine:
            sems[("chan", ch)] = self._stack.enter_context(nc.semaphore(f"sem_c_{ch}"))
        counts = {}
        for op in ops:
            if op.inc:
                key = ("chan", op.chan) if op.dma else ("eng", op.eng)
                amt = 16 if op.dma else 1
                counts[key] = counts.get(key, 0) + amt
                op.seq = counts[key]
        marks = {e: {} for e in _ALL_ENGINES}
        plans = {e: [] for e in _ALL_ENGINES}
        for op in ops:
            m = marks[op.eng]
            for d in sorted(op.deps):
                dop = ops[d]
                key = ("chan", dop.chan) if dop.dma else ("eng", dop.eng)
                assert dop.inc and dop.seq > 0
                if m.get(key, 0) >= dop.seq:
                    continue
                m[key] = dop.seq
                plans[op.eng].append(("wait", key, dop.seq))
            plans[op.eng].append(("op", op))

        with nc.Block() as block:
            def mk(eng_name):
                def body(eng):
                    for item in plans[eng_name]:
                        if item[0] == "wait":
                            eng.wait_ge(sems[item[1]], item[2])
                        else:
                            op = item[1]
                            inst = op.emit(eng)
                            if op.inc:
                                inst.then_inc(
                                    sems[("chan", op.chan) if op.dma else ("eng", op.eng)],
                                    16 if op.dma else 1)
                return body
            for e, attr in (("sync", block.sync), ("tensor", block.tensor),
                            ("vector", block.vector), ("scalar", block.scalar),
                            ("gpsimd", block.gpsimd)):
                if plans[e]:
                    attr(mk(e))
        return sems, counts



def build_program(taps=(), n_sublayers=4):
    import contextlib
    taps = set(taps)
    nc = bass.Bass("TRN2", target_bir_lowering=False)

    din = {}
    def dram_in(name, shape, dtype=F32):
        din[name] = nc.dram_tensor(name, list(shape), dtype, kind="ExternalInput").ap()
        return din[name]

    x0 = dram_in("x0", (T, DM))
    w_inT = dram_in("w_inT", (DM, 4416))
    w_dtT = dram_in("w_dtT", (DTR, H))
    conv_w = dram_in("conv_w", (DI, 4))
    conv_b = dram_in("conv_b", (1, DI))
    a_log = dram_in("a_log", (1, H * DS))
    dskip = dram_in("dskip", (1, DI))
    dt_bias = dram_in("dt_bias", (H, 1))
    norm_w = dram_in("norm_w", (1, DI))
    w_resT = dram_in("w_resT", (DM, DI))
    w_outT = dram_in("w_outT", (DI, DM))
    wq = dram_in("wq", (DM, NH * HD))
    wk_down = dram_in("wk_down", (DM, NKV * DLAT))
    wv_down = dram_in("wv_down", (DM, NKV * DLAT))
    wk_up = dram_in("wk_up", (DLAT, HD))
    wv_up = dram_in("wv_up", (DLAT, HD))
    wo = dram_in("wo", (NH * HD, DM))
    wrT = dram_in("wrT", (DM, 8))
    w1 = dram_in("w1", (DM, F1))
    b1 = dram_in("b1", (1, F1))
    w2 = dram_in("w2", (F1, DM))
    b2 = dram_in("b2", (1, DM))
    cpool_h = dram_in("cpool", (128, len(CONSTS)))
    eye_f = dram_in("eye_f", (128, 128))
    eye_b = dram_in("eye_b", (128, 128), BF16)
    ones1_h = dram_in("ones1", (1, 128))
    mask_mamba_h = dram_in("mask_mamba", (128, 128))
    mask_attn_h = dram_in("mask_attn", (128, 128))
    blockmask_h = dram_in("blockmask", (128, DI), BF16)

    out = nc.dram_tensor("out", [T, DM], F32, kind="ExternalOutput").ap()

    xs = nc.dram_tensor("xs", [T, DM], F32, kind="Internal").ap()
    y_scr = nc.dram_tensor("y_scr", [T, DI], BF16, kind="Internal").ap()
    d_scr = nc.dram_tensor("d_scr", [H, T], F32, kind="Internal").ap()
    decay_scr = nc.dram_tensor("decay_scr", [16, 128], F32, kind="Internal").ap()
    c_scr = nc.dram_tensor("c_scr", [128, T], BF16, kind="Internal").ap()
    cf_scr = [nc.dram_tensor(f"cf_scr{i}", [1, DI], BF16, kind="Internal").ap()
              for i in range(2)]
    a_scr = nc.dram_tensor("a_scr", [1, 128], F32, kind="Internal").ap()

    dbg = {}
    K = Sched(nc)

    stacks = []
    def arena():
        s = contextlib.ExitStack()
        stacks.append(s)
        return s

    def close_arena(s, note=""):
        K.barrier(note="close:" + note)
        s.close()

    g_arena = arena()

    def mk_sb(stack):
        def sb(name, shape, dtype=F32):
            return stack.enter_context(nc.sbuf_tensor(name, list(shape), dtype))
        return sb

    def mk_ps(stack):
        def ps(name, shape, dtype=F32):
            return stack.enter_context(nc.psum_tensor(name, list(shape), dtype))
        return ps

    gsb = mk_sb(g_arena)

    def tap(name, make_store):
        if name in taps:
            shape, dtype, store = make_store
            dbg[name] = nc.dram_tensor("dbg_" + name, list(shape), dtype,
                                       kind="ExternalOutput").ap()
            store(dbg[name])

    def tap_simple(name, ap_, shape, dtype, rtoks):
        def store(dst):
            K.dma("dbg", "gpsimd", lambda e: e.dma_start(out=dst[:], in_=ap_),
                  reads=list(rtoks))
        tap(name, (shape, dtype, store))

    cpool = gsb("cpool_s", (128, len(CONSTS)))
    eyef = gsb("eyef", (128, 128))
    eyeb = gsb("eyeb", (128, 128), BF16)
    ones1 = gsb("ones1c", (1, 128))
    maskm = gsb("maskm", (128, 128))
    maska = gsb("maska", (128, 128))

    def CONST(v, p=128):
        return cpool[0:p, _CIDX[v]:_CIDX[v] + 1]

    for t_, hbm in ((cpool, cpool_h), (eyef, eye_f), (eyeb, eye_b), (ones1, ones1_h),
                    (maskm, mask_mamba_h), (maska, mask_attn_h)):
        K.dma("const", "sync", lambda e, t_=t_, hbm=hbm: e.dma_start(out=t_[:], in_=hbm[:]),
              writes=[(t_.name if hasattr(t_, "name") else id(t_),)])
    CPOOL_T = [("cpool_s",)]
    EYEF_T = [("eyef",)]
    EYEB_T = [("eyeb",)]
    ONES_T = [("ones1c",)]
    MASKM_T = [("maskm",)]
    MASKA_T = [("maska",)]

    for c in range(NC):
        K.dma("xseed", "sync", lambda e, c=c: e.dma_start(
                out=xs[c * LC:(c + 1) * LC, :], in_=x0[c * LC:(c + 1) * LC, :]),
              writes=[("xs", c)])

    def ln_phase(phase, sb, ps, xn_f, src=None, srctok="xs"):
        src = xs if src is None else src
        nb = 5
        bufs = []
        for i in range(nb):
            bufs.append((sb(f"{phase}lnx{i}", (128, DM)), sb(f"{phase}lnbn{i}", (128, 12)),
                         sb(f"{phase}lnmv{i}", (128, 2)), sb(f"{phase}lnrs{i}", (128, 1)),
                         sb(f"{phase}lnxt{i}", (128, DM), BF16)))
        for c in range(NC):
            xc, bnb, mv, rs, xn_tok = bufs[c % nb]
            xct = (f"{phase}xc", c % nb)
            K.dma("xload", "sync",
                  lambda e, xc=xc, c=c: e.dma_start(out=xc[:], in_=src[c * LC:(c + 1) * LC, :]),
                  reads=[(srctok, c)], writes=[xct])
            bt = (f"{phase}bn", c % nb)
            K.op("vector", lambda e, xc=xc, bnb=bnb: e.bn_stats(bnb[:, 0:6], xc[:, 0:512]),
                 reads=[xct], writes=[bt], tail=True)
            K.op("vector", lambda e, xc=xc, bnb=bnb: e.bn_stats(bnb[:, 6:12], xc[:, 512:1024]),
                 reads=[xct], writes=[bt], tail=True)
            K.op("vector", lambda e, bnb=bnb, mv=mv: e.bn_aggr(
                     mv[:, 0:2], bnb[:, 0:12].rearrange("p (n t) -> p n t", t=6)),
                 reads=[bt], writes=[(f"{phase}mv", c % nb)], tail=True)
            K.op("scalar", lambda e, mv=mv, rs=rs: e.activation(
                     rs[:, 0:1], mv[:, 1:2], AF.Sqrt, bias=CONST(1e-5)),
                 reads=[(f"{phase}mv", c % nb)] + CPOOL_T, writes=[(f"{phase}rs", c % nb)])
            K.op("vector", lambda e, rs=rs: e.reciprocal(rs[:, 0:1], rs[:, 0:1]),
                 reads=[(f"{phase}rs", c % nb)], writes=[(f"{phase}rs", c % nb)], tail=True)
            K.op("vector",
                 lambda e, xc=xc, mv=mv, rs=rs, xn_tok=xn_tok: e.tensor_scalar(
                     xn_tok[:], xc[:], mv[:, 0:1], rs[:, 0:1], ALU.subtract, ALU.mult),
                 reads=[xct, (f"{phase}mv", c % nb), (f"{phase}rs", c % nb)],
                 writes=[(f"{phase}xntok", c % nb)])
            for fc in range(8):
                K.dma("xtrans", "scalar",
                      lambda e, xn_tok=xn_tok, fc=fc, c=c: e.dma_start_transpose(
                          xn_f[fc][:, c * LC:(c + 1) * LC], xn_tok[:, fc * 128:(fc + 1) * 128]),
                      reads=[(f"{phase}xntok", c % nb)], writes=[(f"{phase}xnf", fc, c)])
        return [(f"{phase}xnf", fc, c) for fc in range(8) for c in range(NC)]

    def mk_wstream(phase, sb, n_stage=4, cols=512):
        stg = [sb(f"{phase}wstg{i}", (128, cols)) for i in range(n_stage)]
        wbf = [sb(f"{phase}wbf{i}", (128, cols), BF16) for i in range(n_stage)]
        ctr = [0]

        def load(hbm_slice_fn, ncols, note=""):
            i = ctr[0] % n_stage
            ctr[0] += 1
            stok = (f"{phase}wstg", i)
            btok = (f"{phase}wbf", i)
            K.dma("wload", "sync",
                  lambda e, i=i: e.dma_start(out=stg[i][:, 0:ncols], in_=hbm_slice_fn()),
                  writes=[stok], note=note)
            K.op("vector",
                 lambda e, i=i: e.tensor_copy(wbf[i][:, 0:ncols], stg[i][:, 0:ncols]),
                 reads=[stok], writes=[btok], note=note)
            return wbf[i], btok
        return load

    def accum_to_xs(tile_ap, rows, cols, rtok, c, note=""):
        K.dma("xacc", "gpsimd",
              lambda e: e.dma_start(out=xs[rows[0]:rows[1], cols[0]:cols[1]],
                                    in_=tile_ap, accum_op=ALU.add),
              reads=[rtok, ("xs", c)], writes=[("xs", c)], note=note)

    sl1 = arena()
    sb1 = mk_sb(sl1)
    ps1 = mk_ps(sl1)
    PS_BANKS = [ps1(f"m_bank{i}", (128, 512)) for i in range(8)]
    PSTOK = [("psbank", i) for i in range(8)]

    xn_f = [sb1(f"m_xnf{fc}", (128, T), BF16) for fc in range(8)]
    ln_arena = arena()
    xnf_toks = ln_phase("m", mk_sb(ln_arena), ps1, xn_f, src=x0, srctok="x0dram")
    close_arena(ln_arena, "m_ln")
    sp_arena = arena()
    sb1 = mk_sb(sp_arena)
    XNF = lambda fc: [("m" + "xnf", fc, c) for c in range(NC)]

    for fc in range(8):
        tap_simple(f"xnf{fc}", xn_f[fc][:], (128, T), BF16, XNF(fc))

    u_raw = [sb1(f"m_uraw{fc}", (128, T + 4), BF16) for fc in range(16)]
    B_f = sb1("m_B_f", (128, T))
    dt_f = sb1("m_dt_f", (8, T))
    A_rep = sb1("m_Arep", (128, 128))
    Cbar = sb1("m_Cbar", (128, T), BF16)
    Bbar2 = sb1("m_Bbar2", (128, T))
    dtT_all = sb1("m_dtT", (128, 128))
    Dcol2d = sb1("m_Dcol", (128, 128))
    convw_cols = sb1("m_convw", (128, 64))
    convb_cols = sb1("m_convb", (128, 16))
    dskip_rep = sb1("m_dskip", (128, DI), BF16)
    blockmask = sb1("m_blockm", (128, DI), BF16)
    ones1b = sb1("m_ones1b", (1, 128), BF16)

    proj_arena = arena()
    sbp = mk_sb(proj_arena)
    dth = sbp("m_dth", (64, T), BF16)
    C_f = sbp("m_C_f", (128, T), BF16)

    for fc in range(16):
        K.op("vector", lambda e, fc=fc: e.memset(u_raw[fc][:, 0:4], 0.0),
             writes=[("uraw0", fc)])

    wload_m = mk_wstream("m", sbp, n_stage=6, cols=128)

    def project(moff, msize, bank_grp, evac_fn, note=""):
        banks = PS_BANKS[bank_grp * 4:(bank_grp + 1) * 4]
        btoks = PSTOK[bank_grp * 4:(bank_grp + 1) * 4]
        for kc in range(8):
            wtile, wtok = wload_m(
                (lambda kc=kc: w_inT[kc * 128:(kc + 1) * 128, moff:moff + msize]),
                msize, note=note)
            for ns in range(4):
                K.op("tensor",
                     lambda e, wtile=wtile, msize=msize, kc=kc, ns=ns, b=banks[ns]:
                         e.matmul(b[0:msize, :], wtile[:, 0:msize],
                                  xn_f[kc][:, ns * 512:(ns + 1) * 512],
                                  start=(kc == 0), stop=(kc == 7)),
                     reads=[wtok] + [("m" + "xnf", kc, c) for c in range(ns * 4, ns * 4 + 4)],
                     writes=[btoks[ns]], note=note)
        for ns in range(4):
            evac_fn(ns, banks[ns], btoks[ns])

    for fc in range(16):
        def evac_u(ns, bank, btok, fc=fc):
            K.op("vector",
                 lambda e, ns=ns, bank=bank, fc=fc: e.tensor_copy(
                     u_raw[fc][:, 4 + ns * 512: 4 + (ns + 1) * 512], bank[:, :]),
                 reads=[btok], writes=[("uraw", fc, ns)], note=f"uevac{fc}")
        project(2048 + fc * 128, 128, fc % 2, evac_u, note=f"uproj{fc}")

    def evac_dth(ns, bank, btok):
        K.op("vector", lambda e, ns=ns, bank=bank: e.tensor_copy(
                 dth[:, ns * 512:(ns + 1) * 512], bank[0:64, :]),
             reads=[btok], writes=[("dth", ns)])
    project(4096, 64, 0, evac_dth, note="dthproj")

    def evac_B(ns, bank, btok):
        K.op("vector", lambda e, ns=ns, bank=bank: e.tensor_copy(
                 B_f[:, ns * 512:(ns + 1) * 512], bank[:, :]),
             reads=[btok], writes=[("B_f", ns)])
    project(4160, 128, 1, evac_B, note="Bproj")

    def evac_C(ns, bank, btok):
        K.op("vector", lambda e, ns=ns, bank=bank: e.tensor_copy(
                 C_f[:, ns * 512:(ns + 1) * 512], bank[:, :]),
             reads=[btok], writes=[("C_f", ns)])
    project(4288, 128, 0, evac_C, note="Cproj")
    K.dma("spill", "gpsimd", lambda e: e.dma_start(out=c_scr[:], in_=C_f[:]),
          reads=[("C_f", ns) for ns in range(4)], writes=[("cscr",)])
    K.op("vector", lambda e: e.tensor_copy(ones1b[:], ones1[:]), reads=ONES_T,
         writes=[("ones1b",)])

    UR_T = [("uraw", fc, ns) for fc in range(16) for ns in range(4)] + \
           [("uraw0", fc) for fc in range(16)]
    tap_simple("u_raw0", u_raw[0][:, 4:4 + T], (128, T), BF16, [("uraw", 0, ns) for ns in range(4)])
    tap_simple("dth", dth[:], (64, T), BF16, [("dth", ns) for ns in range(4)])
    tap_simple("B_f", B_f[:], (128, T), F32, [("B_f", ns) for ns in range(4)])
    tap_simple("C_f", C_f[:], (128, T), BF16, [("C_f", ns) for ns in range(4)])

    wdt_stg = sbp("wdt_stg", (64, 8))
    wdt_bf = sbp("wdt_bf", (64, 8), BF16)
    dtbias_c = sbp("dtbias_c", (8, 1))
    K.dma("wload", "sync", lambda e: e.dma_start(out=wdt_stg[:], in_=w_dtT[:]), writes=[("wdts",)])
    K.op("vector", lambda e: e.tensor_copy(wdt_bf[:], wdt_stg[:]), reads=[("wdts",)], writes=[("wdtb",)])
    K.dma("const", "sync", lambda e: e.dma_start(out=dtbias_c[:], in_=dt_bias[:]), writes=[("dtbc",)])

    sptmp = sbp("m_sptmp", (8, T))
    sptmp2 = sbp("m_sptmp2", (8, T))
    for ns in range(4):
        K.op("tensor",
             lambda e, ns=ns: e.matmul(PS_BANKS[4 + ns][0:8, :], wdt_bf[:],
                                       dth[:, ns * 512:(ns + 1) * 512], start=True, stop=True),
             reads=[("wdtb",), ("dth", ns)], writes=[PSTOK[4 + ns]])
        K.op("scalar",
             lambda e, ns=ns: e.activation(sptmp[:, ns * 512:(ns + 1) * 512],
                                           PS_BANKS[4 + ns][0:8, :], AF.Abs, bias=dtbias_c[:]),
             reads=[PSTOK[4 + ns], ("dtbc",)], writes=[("sptmp", ns)])
        K.op("scalar",
             lambda e, ns=ns: e.activation(sptmp2[:, ns * 512:(ns + 1) * 512],
                                           sptmp[:, ns * 512:(ns + 1) * 512], AF.Exp,
                                           scale=CONST(-1.0, 8)),
             reads=[("sptmp", ns)] + CPOOL_T, writes=[("sptmp2", ns)])
        K.op("scalar",
             lambda e, ns=ns: e.activation(sptmp[:, ns * 512:(ns + 1) * 512],
                                           sptmp2[:, ns * 512:(ns + 1) * 512], AF.Ln,
                                           bias=CONST(1.0, 8)),
             reads=[("sptmp2", ns)] + CPOOL_T, writes=[("sptmp", ns)])
        K.op("scalar",
             lambda e, ns=ns: e.activation(dt_f[:, ns * 512:(ns + 1) * 512],
                                           PS_BANKS[4 + ns][0:8, :], AF.Relu, bias=dtbias_c[:]),
             reads=[PSTOK[4 + ns], ("dtbc",)], writes=[("dtf", ns)])
        K.op("vector",
             lambda e, ns=ns: e.tensor_add(dt_f[:, ns * 512:(ns + 1) * 512],
                                           dt_f[:, ns * 512:(ns + 1) * 512],
                                           sptmp[:, ns * 512:(ns + 1) * 512]),
             reads=[("dtf", ns), ("sptmp", ns)], writes=[("dtf", ns)])
    K.op("vector", lambda e: e.tensor_scalar(dt_f[:], dt_f[:], 1.0, 1e-4, ALU.min, ALU.max),
         reads=[("dtf", 0), ("dtf", 1), ("dtf", 2), ("dtf", 3)], writes=[("dtf", "clip")])
    tap_simple("dt_f", dt_f[:], (8, T), F32, [("dtf", "clip")])
    tap_simple("sptmp", sptmp[:], (8, T), F32, [("sptmp", ns) for ns in range(4)])

    D8 = sbp("m_D8", (8, T))
    K.op("vector", lambda e: e.tensor_tensor_scan(D8[:], dt_f[:], dt_f[:], 0.0, ALU.add, ALU.bypass),
         reads=[("dtf", "clip")], writes=[("D8",)], tail=True)
    tap_simple("D8", D8[:], (8, T), F32, [("D8",)])
    K.dma("bcast", "gpsimd", lambda e: e.dma_start(out=d_scr[:], in_=D8[:]),
          reads=[("D8",)], writes=[("dscr8",)])

    DedgeE = sbp("m_DedgeE", (8, 17))
    K.op("vector", lambda e: e.memset(DedgeE[:, 0:1], 0.0), writes=[("De", 0)])
    K.op("vector", lambda e: e.tensor_copy(
            DedgeE[:, 1:17], D8[:].rearrange("p (c l) -> p c l", l=128)[:, :, 127:128]),
         reads=[("D8",)], writes=[("De", 1)], tail=True)
    Dl2d = sbp("m_Dl2d", (8, 16))
    K.op("vector", lambda e: e.tensor_sub(Dl2d[:], DedgeE[:, 1:17], DedgeE[:, 0:16]),
         reads=[("De", 0), ("De", 1)], writes=[("Dl2d",)], tail=True)

    Dfeat128 = sbp("m_Dfeat", (128, T))
    K.dma("bcast", "gpsimd",
          lambda e: e.dma_start(out=Dfeat128[:], in_=D8[:].unsqueeze(1).broadcast_to([8, 16, T])),
          reads=[("D8",)], writes=[("Dfeat",)])
    De128 = sbp("m_De128", (128, 16))
    K.dma("bcast", "gpsimd",
          lambda e: e.dma_start(out=De128[:], in_=DedgeE[:, 0:16].unsqueeze(1).broadcast_to([8, 16, 16])),
          reads=[("De", 0), ("De", 1)], writes=[("De128",)])
    DN128 = sbp("m_DN128", (128, 16))
    K.dma("bcast", "gpsimd",
          lambda e: e.dma_start(out=DN128[:], in_=DedgeE[:, 1:17].unsqueeze(1).broadcast_to([8, 16, 16])),
          reads=[("De", 1)], writes=[("DN128",)])
    Dl128 = sbp("m_Dl128", (128, 16))
    K.dma("bcast", "gpsimd",
          lambda e: e.dma_start(out=Dl128[:], in_=Dl2d[:].unsqueeze(1).broadcast_to([8, 16, 16])),
          reads=[("Dl2d",)], writes=[("Dl128",)])

    alog_s = sbp("m_alog", (1, 128))
    A_flat = sbp("m_Aflat", (1, 128))
    A_colf = sbp("m_Acolf", (128, 1))
    K.dma("const", "sync", lambda e: e.dma_start(out=alog_s[:], in_=a_log[:]), writes=[("alog",)])
    K.op("scalar", lambda e: e.activation(A_flat[:], alog_s[:], AF.Exp), reads=[("alog",)],
         writes=[("Aflat",)])
    K.op("vector", lambda e: e.tensor_scalar_mul(A_flat[:], A_flat[:], -1.0),
         reads=[("Aflat",)], writes=[("Aflat",)], tail=True)
    K.dma("bcast", "gpsimd", lambda e: e.dma_start(out=a_scr[:], in_=A_flat[:]),
          reads=[("Aflat",)], writes=[("ascr",)])
    K.dma("bcast", "gpsimd", lambda e: e.dma_start(out=A_rep[:], in_=a_scr[:].broadcast_to([128, 128])),
          reads=[("ascr",)], writes=[("Arep",)])
    K.op("tensor", lambda e: e.transpose(PS_BANKS[6][0:128, 0:1], A_flat[0:1, 0:128], eyef[0:1, 0:1]),
         reads=[("Aflat",)] + EYEF_T, writes=[PSTOK[6]])
    K.op("vector", lambda e: e.tensor_copy(A_colf[:], PS_BANKS[6][0:128, 0:1]),
         reads=[PSTOK[6]], writes=[("Acolf",)])

    e_scr = sbp("m_escr", (128, T))
    e_scr2 = sbp("m_escr2", (128, T))
    K.op("vector", lambda e: e.tensor_sub(
            e_scr[:].rearrange("p (c l) -> p c l", l=128),
            Dfeat128[:].rearrange("p (c l) -> p c l", l=128),
            De128[:].unsqueeze(2).broadcast_to([128, 16, 128])),
         reads=[("Dfeat",), ("De128",)], writes=[("escr",)])
    tap_simple("e_scr", e_scr[:], (128, T), F32, [("escr",)])
    K.op("scalar", lambda e: e.activation(e_scr2[:], e_scr[:], AF.Exp, scale=A_colf[:]),
         reads=[("escr",), ("Acolf",)], writes=[("escr2",)])
    tap_simple("e_scr2", e_scr2[:], (128, T), F32, [("escr2",)])
    K.op("vector", lambda e: e.tensor_mul(Cbar[:], e_scr2[:], C_f[:]),
         reads=[("escr2",)] + [("C_f", ns) for ns in range(4)], writes=[("Cbar",)])
    K.op("vector", lambda e: e.tensor_sub(
            e_scr[:].rearrange("p (c l) -> p c l", l=128),
            DN128[:].unsqueeze(2).broadcast_to([128, 16, 128]),
            Dfeat128[:].rearrange("p (c l) -> p c l", l=128)),
         reads=[("Dfeat",), ("DN128",), ("Cbar",)], writes=[("escr",)])
    K.op("scalar", lambda e: e.activation(e_scr2[:], e_scr[:], AF.Exp, scale=A_colf[:]),
         reads=[("escr",), ("Acolf",)], writes=[("escr2",)])
    K.op("vector", lambda e: e.tensor_mul(Bbar2[:], e_scr2[:], B_f[:]),
         reads=[("escr2",)] + [("B_f", ns) for ns in range(4)], writes=[("Bbar2",)])
    tap_simple("Dfeat128", Dfeat128[:], (128, T), F32, [("Dfeat",)])
    tap_simple("De128", De128[:], (128, 16), F32, [("De128",)])
    tap_simple("A_rep", A_rep[:], (128, 128), F32, [("Arep",)])
    tap_simple("A_colf", A_colf[:], (128, 1), F32, [("Acolf",)])
    tap_simple("Cbar", Cbar[:], (128, T), BF16, [("Cbar",)])
    tap_simple("Bbar2", Bbar2[:], (128, T), F32, [("Bbar2",)])

    decayspan = sbp("m_dspan", (128, 16))
    dspanT = sbp("m_dspanT", (16, 128))
    K.op("scalar", lambda e: e.activation(decayspan[:], Dl128[:], AF.Exp, scale=A_colf[:]),
         reads=[("Dl128",), ("Acolf",)], writes=[("dspan",)])
    K.op("tensor", lambda e: e.transpose(PS_BANKS[6][0:16, 0:128], decayspan[:], eyef[:]),
         reads=[("dspan",)] + EYEF_T, writes=[PSTOK[6]])
    K.op("vector", lambda e: e.tensor_copy(dspanT[:], PS_BANKS[6][0:16, 0:128]),
         reads=[PSTOK[6]], writes=[("dspanT",)])
    K.dma("bcast", "gpsimd", lambda e: e.dma_start(out=decay_scr[:], in_=dspanT[:]),
          reads=[("dspanT",)], writes=[("dscr",)])

    for c in range(NC):
        K.op("tensor", lambda e, c=c: e.transpose(
                PS_BANKS[7][0:128, 0:8], dt_f[0:8, c * LC:(c + 1) * LC], eyef[0:8, 0:8]),
             reads=[("dtf", "clip")] + EYEF_T, writes=[PSTOK[7]])
        K.op("vector", lambda e, c=c: e.tensor_copy(dtT_all[:, c * 8:(c + 1) * 8],
                                                    PS_BANKS[7][0:128, 0:8]),
             reads=[PSTOK[7]], writes=[("dtT", c)])

    st16 = sbp("m_st16", (16, 128))
    for h in range(H):
        K.dma("bcast", "gpsimd", lambda e, h=h: e.dma_start(
                out=st16[:], in_=d_scr[h:h + 1, :].rearrange("o (c l) -> (o c) l", l=128)),
              reads=[("dscr8",)], writes=[("st16",)])
        K.op("tensor", lambda e: e.transpose(PS_BANKS[6][0:128, 0:16], st16[:], eyef[0:16, 0:16]),
             reads=[("st16",)] + EYEF_T, writes=[PSTOK[6]])
        K.op("vector", lambda e, h=h: e.tensor_copy(Dcol2d[:, h * 16:(h + 1) * 16],
                                                    PS_BANKS[6][0:128, 0:16]),
             reads=[PSTOK[6]], writes=[("Dcol", h)])

    cb16 = sbp("m_cb16", (16, 128))
    K.dma("const", "sync", lambda e: e.dma_start(
            out=convw_cols[:], in_=conv_w[:].rearrange("(fc p) j -> p fc j", p=128)),
          writes=[("convw",)])
    K.dma("const", "sync", lambda e: e.dma_start(
            out=cb16[:], in_=conv_b[:].rearrange("o (fc p) -> (o fc) p", p=128)),
          writes=[("cb16",)])
    K.op("tensor", lambda e: e.transpose(PS_BANKS[7][0:128, 0:16], cb16[:], eyef[0:16, 0:16]),
         reads=[("cb16",)] + EYEF_T, writes=[PSTOK[7]])
    K.op("vector", lambda e: e.tensor_copy(convb_cols[:], PS_BANKS[7][0:128, 0:16]),
         reads=[PSTOK[7]], writes=[("convb",)])

    K.dma("bcast", "gpsimd", lambda e: e.dma_start(out=dskip_rep[:], in_=dskip[:].broadcast_to([128, DI])),
          writes=[("dskiprep",)])
    K.dma("const", "sync", lambda e: e.dma_start(out=blockmask[:], in_=blockmask_h[:]),
          writes=[("blockm",)])

    close_arena(proj_arena, "m_proj")

    scan_arena = arena()
    sbs = mk_sb(scan_arena)
    h0b = sbs("m_h0b", (128, DI), BF16)
    hT_cur = sbs("m_hT", (128, 256))
    K.op("vector", lambda e: e.memset(h0b[:], 0.0), writes=[("h0b",)])
    K.op("vector", lambda e: e.memset(hT_cur[:], 0.0), writes=[("hT",)])

    uconv_s = [sbs(f"m_uconv{i}", (128, 128)) for i in range(2)]
    uT_c = [sbs(f"m_uTc{i}", (128, DI), BF16) for i in range(2)]
    Btok_c = [sbs(f"m_Btok{i}", (128, 128)) for i in range(2)]
    Bb2T_c = [sbs(f"m_Bb2T{i}", (128, 128), BF16) for i in range(2)]
    decrep = [sbs(f"m_decrep{i}", (128, 128)) for i in range(2)]
    Uh = [sbs(f"m_Uh{i}", (128, 256), BF16) for i in range(2)]
    Sb_t = [sbs(f"m_Sb{i}", (128, 128), BF16) for i in range(16)]
    E_t = [sbs(f"m_Et{i}", (128, 128)) for i in range(2)]
    tdec = sbs("m_tdec", (128, 256))
    htF = sbs("m_htF", (128, 256), BF16)
    uD = sbs("m_uD", (128, DI), BF16)
    y_c = [sbs(f"m_yc{i}", (128, DI), BF16) for i in range(1)]
    CrepS = [sbs(f"m_crep{i}", (128, DI), BF16) for i in range(2)]
    DrepS = [sbs(f"m_drep{i}", (128, 128)) for i in range(2)]

    for c in range(NC):
        uT = uT_c[c % 2]
        uTtok = ("uT", c % 2)
        for fc in range(16):
            ub = uconv_s[fc % 2]
            ubtok = ("uconv", fc % 2)
            K.op("scalar",
                 lambda e, fc=fc, c=c, ub=ub: e.activation(
                     ub[:], u_raw[fc][:, 4 + c * LC: 4 + (c + 1) * LC], AF.Identity,
                     scale=convw_cols[:, fc * 4 + 3: fc * 4 + 4],
                     bias=convb_cols[:, fc:fc + 1]),
                 reads=[("uraw", fc, c // 4), ("uraw0", fc), ("convw",), ("convb",)],
                 writes=[ubtok], note=f"conv{c},{fc}")
            for j in range(3):
                K.op("vector",
                     lambda e, fc=fc, c=c, j=j, ub=ub: e.scalar_tensor_tensor(
                         ub[:], u_raw[fc][:, 1 + j + c * LC: 1 + j + c * LC + 128],
                         convw_cols[:, fc * 4 + j: fc * 4 + j + 1], ub[:],
                         ALU.mult, ALU.add),
                     reads=[("uraw", fc, c // 4), ("uraw", fc, max(c - 1, 0) // 4),
                            ("uraw0", fc), ("convw",), ubtok],
                     writes=[ubtok], note=f"conv{c},{fc},{j}")
            tb = PS_BANKS[6 + fc % 2]
            tbtok = PSTOK[6 + fc % 2]
            K.op("tensor", lambda e, ub=ub, tb=tb: e.transpose(tb[:, 0:128], ub[:], eyef[:]),
                 reads=[ubtok] + EYEF_T, writes=[tbtok], note=f"uT{c},{fc}")
            K.op("vector", lambda e, fc=fc, uT=uT, tb=tb: e.tensor_copy(
                    uT[:, fc * 128:(fc + 1) * 128], tb[:, 0:128]),
                 reads=[tbtok], writes=[uTtok], note=f"uTe{c},{fc}")
        Btok = Btok_c[c % 2]
        Bb2T = Bb2T_c[c % 2]
        dcr = decrep[c % 2]
        K.op("tensor", lambda e, c=c: e.transpose(PS_BANKS[6][:, 0:128],
                                                  B_f[:, c * LC:(c + 1) * LC], eyef[:]),
             reads=[("B_f", c // 4)] + EYEF_T, writes=[PSTOK[6]])
        K.op("vector", lambda e, Btok=Btok: e.tensor_copy(Btok[:], PS_BANKS[6][:, 0:128]),
             reads=[PSTOK[6]], writes=[("Btok", c % 2)])
        K.op("tensor", lambda e, c=c: e.transpose(PS_BANKS[7][:, 0:128],
                                                  Bbar2[:, c * LC:(c + 1) * LC], eyef[:]),
             reads=[("Bbar2",)] + EYEF_T, writes=[PSTOK[7]])
        K.op("vector", lambda e, Bb2T=Bb2T: e.tensor_copy(Bb2T[:], PS_BANKS[7][:, 0:128]),
             reads=[PSTOK[7]], writes=[("Bb2T", c % 2)])
        K.dma("bcast", "gpsimd", lambda e, c=c, dcr=dcr: e.dma_start(
                out=dcr[:], in_=decay_scr[c:c + 1, :].broadcast_to([128, 128])),
              reads=[("dscr",)], writes=[("dcrep", c % 2)])

        if c > 0:
            for ns in range(4):
                K.op("tensor", lambda e, c=c, ns=ns: e.matmul(
                        PS_BANKS[ns][:, :], Cbar[:, c * LC:(c + 1) * LC],
                        h0b[:, ns * 512:(ns + 1) * 512], start=True, stop=False),
                     reads=[("Cbar",), ("h0b",)], writes=[PSTOK[ns]], note=f"Y1:{c},{ns}")

        for h in range(H):
            U = Uh[h % 2]
            Utok = ("Uh", h % 2)
            K.op("vector", lambda e, h=h, c=c, U=U, uT=uT: e.tensor_scalar_mul(
                    U[:], uT[:, h * 256:(h + 1) * 256], dtT_all[:, c * 8 + h: c * 8 + h + 1]),
                 reads=[uTtok, ("dtT", c)], writes=[Utok], note=f"Uh{c},{h}")
            dr = DrepS[h % 2]
            K.dma("crep", "gpsimd", lambda e, h=h, c=c, dr=dr: e.dma_start(
                    out=dr[:], in_=d_scr[h:h + 1, c * LC:(c + 1) * LC].broadcast_to([128, 128])),
                  reads=[("dscr8",)], writes=[("DrepS", h % 2)], note=f"Drep{c},{h}")
            E = E_t[h % 2]
            Etok = ("E", h % 2)
            K.op("vector", lambda e, h=h, c=c, E=E, dr=dr: e.scalar_tensor_tensor(
                    E[:], dr[:], Dcol2d[:, h * 16 + c: h * 16 + c + 1],
                    maskm[:], ALU.subtract, ALU.add),
                 reads=[("DrepS", h % 2), ("Dcol", h)] + MASKM_T, writes=[Etok], note=f"E{c},{h}")
            K.dma("crep", "gpsimd", lambda e, h=h, c=c, cr_=cf_scr[h % 2]: e.dma_start(
                    out=cr_[0:1, :], in_=c_scr[h * 16:(h + 1) * 16, c * LC:(c + 1) * LC]),
                  reads=[("cscr",)], writes=[("cfst", h % 2)], note=f"cfst{c},{h}")
            crS = CrepS[h % 2]
            K.dma("crep", "gpsimd", lambda e, h=h, crS=crS, cr_=cf_scr[h % 2]: e.dma_start(
                    out=crS[:], in_=cr_[0:1, :].broadcast_to([128, DI])),
                  reads=[("cfst", h % 2)], writes=[("CrepS", h % 2)], note=f"crbc{c},{h}")
            bank = PS_BANKS[h // 2]
            btok = PSTOK[h // 2]
            for n in range(16):
                hn = h * 16 + n
                Sb = Sb_t[n]
                K.op("scalar", lambda e, hn=hn, Sb=Sb, E=E: e.activation(
                        Sb[:], E[:], AF.Exp, scale=A_rep[:, hn:hn + 1]),
                     reads=[Etok, ("Arep",)], writes=[("Sb", n)], note=f"S{c},{h},{n}")
                K.op("vector", lambda e, hn=hn, n=n, Sb=Sb, Btok=Btok, crS=crS:
                     e.scalar_tensor_tensor(
                        Sb[:], Sb[:], Btok[:, hn:hn + 1],
                        crS[:, n * 128:(n + 1) * 128], ALU.mult, ALU.mult),
                     reads=[("Sb", n), ("Btok", c % 2), ("CrepS", h % 2)],
                     writes=[("Sb", n)], note=f"SB{c},{h},{n}")
                first = (c == 0 and h % 2 == 0 and n == 0)
                last = (h % 2 == 1 and n == 15)
                K.op("tensor", lambda e, h=h, Sb=Sb, U=U, bank=bank, first=first, last=last:
                     e.matmul(bank[:, (h % 2) * 256:(h % 2) * 256 + 256], Sb[:], U[:],
                              start=first, stop=last),
                     reads=[("Sb", n), Utok], writes=[btok], note=f"MM{c},{h},{n}")
            for half in range(2):
                first = (h == 0 and half == 0)
                last = (h == H - 1 and half == 1)
                K.op("tensor", lambda e, h=h, half=half, U=U, Bb2T=Bb2T, first=first, last=last:
                     e.matmul(PS_BANKS[4][:, half * 128 + h * 16: half * 128 + h * 16 + 16],
                              U[:, half * 128:(half + 1) * 128], Bb2T[:, h * 16:(h + 1) * 16],
                              start=first, stop=last),
                     reads=[Utok, ("Bb2T", c % 2)], writes=[PSTOK[4]], note=f"HU{c},{h},{half}")

        K.op("vector", lambda e, dcr=dcr: e.tensor_mul(
                tdec[:].rearrange("p (a b) -> p a b", b=128),
                hT_cur[:].rearrange("p (a b) -> p a b", b=128),
                dcr[:].unsqueeze(1).broadcast_to([128, 2, 128])),
             reads=[("hT",), ("dcrep", c % 2)], writes=[("tdec",)], note=f"hdec{c}")
        K.op("vector", lambda e: e.tensor_add(hT_cur[:], tdec[:], PS_BANKS[4][:, 0:256]),
             reads=[("tdec",), PSTOK[4]], writes=[("hT",)], note=f"hadd{c}")
        for half in range(2):
            K.op("tensor", lambda e, half=half: e.transpose(
                    PS_BANKS[7][:, 0:128], hT_cur[:, half * 128:(half + 1) * 128], eyef[:]),
                 reads=[("hT",)] + EYEF_T, writes=[PSTOK[7]], note=f"hTr{c},{half}")
            K.op("vector", lambda e, half=half: e.tensor_copy(
                    htF[:, half * 128:(half + 1) * 128], PS_BANKS[7][:, 0:128]),
                 reads=[PSTOK[7]], writes=[("htF",)], note=f"htF{c},{half}")
        K.op("vector", lambda e: e.tensor_mul(
                h0b[:].rearrange("p (a b) -> p a b", b=256),
                htF[:].unsqueeze(1).broadcast_to([128, 8, 256]),
                blockmask[:].rearrange("p (a b) -> p a b", b=256)),
             reads=[("htF",), ("blockm",)], writes=[("h0b",)], note=f"h0b{c}")

        K.op("vector", lambda e, uT=uT: e.tensor_mul(uD[:], uT[:], dskip_rep[:]),
             reads=[uTtok, ("dskiprep",)], writes=[("uD",)], note=f"uD{c}")
        yc = y_c[0]
        for b in range(4):
            K.op("vector", lambda e, b=b, yc=yc: e.tensor_add(
                    yc[:, b * 512:(b + 1) * 512], PS_BANKS[b][:, :], uD[:, b * 512:(b + 1) * 512]),
                 reads=[PSTOK[b], ("uD",)], writes=[("yc", 0)], note=f"yev{c},{b}")
        K.dma("spill", "gpsimd", lambda e, c=c, yc=yc: e.dma_start(
                out=y_scr[c * LC:(c + 1) * LC, :], in_=yc[:]),
              reads=[("yc", 0)], writes=[("yscr", c)], note=f"yspill{c}")

    if "y_all" in taps:
        def store_y(dst):
            K.dma("dbg", "gpsimd", lambda e: e.dma_start(out=dst[:], in_=y_scr[:]),
                  reads=[("yscr", c) for c in range(NC)])
        tap("y_all", ((T, DI), BF16, store_y))

    close_arena(scan_arena, "m_scan")
    close_arena(sp_arena, "m_spersist")

    epi = arena()
    sbe = mk_sb(epi)
    normw_rep = sbe("m_normw", (128, DI))
    K.dma("const", "sync", lambda e: e.dma_start(out=normw_rep[:], in_=norm_w[:].broadcast_to([128, DI])),
          writes=[("normw",)])
    z_sl = [sbe(f"m_zsl{i}", (128, DI), BF16) for i in range(4)]
    res_sl = [sbe(f"m_rsl{i}", (128, DI), BF16) for i in range(4)]
    y_sl = [sbe(f"m_ysl{i}", (128, DI), BF16) for i in range(2)]
    gbuf = [sbe(f"m_g{i}", (128, DI)) for i in range(2)]
    gsq = sbe("m_gsq", (128, DI), BF16)
    ssq = sbe("m_ssq", (128, 8))
    gn_c = [sbe(f"m_gn{i}", (128, DI), BF16) for i in range(2)]
    gnT_sl = [sbe(f"m_gnT{i}", (128, 512), BF16) for i in range(16)]
    oacc = [sbe(f"m_oacc{i}", (128, 512)) for i in range(4)]
    wload_e = mk_wstream("e", sbe, n_stage=6, cols=512)

    tap_simple("xnf0_epi", xn_f[0][:], (128, T), BF16, XNF(0) + [("yscr", c) for c in range(NC)])
    for sl in range(4):
        for (wsrc, dst, dtok, act) in ((w_inT, z_sl, "zsl", True), (w_resT, res_sl, "rsl", False)):
            for zs in range(4):
                if DEBUG_EPI_BARRIERS[0]:
                    K.barrier("dbg_epi")
                for kc in range(8):
                    wt, wtok = wload_e(
                        (lambda kc=kc, zs=zs, wsrc=wsrc: wsrc[kc * 128:(kc + 1) * 128,
                                                              zs * 512:(zs + 1) * 512]),
                        512, note=f"{dtok}w{sl}")
                    for tc in range(4):
                        c = sl * 4 + tc
                        K.op("tensor", lambda e, wt=wt, kc=kc, tc=tc, c=c, zs=zs:
                             e.matmul(PS_BANKS[tc][:, :], xn_f[kc][:, c * LC:(c + 1) * LC],
                                      wt[:, 0:512], start=(kc == 0), stop=(kc == 7)),
                             reads=[wtok, ("m" + "xnf", kc, c)], writes=[PSTOK[tc]],
                             note=f"{dtok}mm{sl},{zs},{kc},{tc}")
                for tc in range(4):
                    if act:
                        K.op("scalar", lambda e, tc=tc, zs=zs, dst=dst: e.activation(
                                dst[tc][:, zs * 512:(zs + 1) * 512], PS_BANKS[tc][:, :], AF.Silu),
                             reads=[PSTOK[tc]], writes=[(dtok, tc, zs)], note=f"{dtok}ev{sl},{zs},{tc}")
                    else:
                        K.op("vector", lambda e, tc=tc, zs=zs, dst=dst: e.tensor_copy(
                                dst[tc][:, zs * 512:(zs + 1) * 512], PS_BANKS[tc][:, :]),
                             reads=[PSTOK[tc]], writes=[(dtok, tc, zs)], note=f"{dtok}ev{sl},{zs},{tc}")
        if sl == 0:
            tap_simple("z_c0", z_sl[0][:], (128, DI), BF16, [("zsl", 0, zs) for zs in range(4)])
            tap_simple("res_c0", res_sl[0][:], (128, DI), BF16, [("rsl", 0, zs) for zs in range(4)])
        for tc in range(4):
            if DEBUG_EPI_BARRIERS[0]:
                K.barrier("dbg_epi2")
            c = sl * 4 + tc
            ysl = y_sl[tc % 2]
            K.dma("xload", "sync", lambda e, c=c, ysl=ysl: e.dma_start(
                    out=ysl[:], in_=y_scr[c * LC:(c + 1) * LC, :]),
                  reads=[("yscr", c)], writes=[("ysl", tc % 2)])
            g = gbuf[tc % 2]
            gtok = ("g", tc % 2)
            K.op("vector", lambda e, ysl=ysl, tc=tc, g=g: e.tensor_mul(g[:], ysl[:], z_sl[tc][:]),
                 reads=[("ysl", tc % 2)] + [("zsl", tc, zs) for zs in range(4)],
                 writes=[gtok], note=f"gmul{c}")
            K.op("vector", lambda e, tc=tc, g=g: e.tensor_add(g[:], g[:], res_sl[tc][:]),
                 reads=[gtok] + [("rsl", tc, zs) for zs in range(4)],
                 writes=[gtok], note=f"gadd{c}")
            K.op("scalar", lambda e, g=g, tc=tc: e.activation(
                    gsq[:], g[:], AF.Square, accum_out=ssq[:, tc:tc + 1]),
                 reads=[gtok], writes=[("gsq",), ("ssq", tc)], note=f"gsq{c}", tail=True)
            K.op("scalar", lambda e, tc=tc: e.activation(
                    ssq[:, 4 + tc:5 + tc], ssq[:, tc:tc + 1], AF.Sqrt, scale=CONST(1.0 / DI),
                    bias=CONST(1e-6)),
                 reads=[("ssq", tc)] + CPOOL_T, writes=[("ssq2", tc)], tail=True)
            K.op("vector", lambda e, tc=tc: e.reciprocal(ssq[:, tc:tc + 1], ssq[:, 4 + tc:5 + tc]),
                 reads=[("ssq2", tc)], writes=[("ssq", tc)], tail=True)
            gn = gn_c[tc % 2]
            K.op("vector", lambda e, g=g, gn=gn, tc=tc: e.scalar_tensor_tensor(
                    gn[:], g[:], ssq[:, tc:tc + 1], normw_rep[:], ALU.mult, ALU.mult),
                 reads=[gtok, ("ssq", tc), ("normw",)], writes=[("gn", tc % 2)], note=f"gn{c}")
            if sl == 0 and tc == 0:
                tap_simple("gn_c0", gn_c[0][:], (128, DI), BF16, [("gn", 0)])
            for fc in range(16):
                K.dma("gtrans", "scalar", lambda e, fc=fc, tc=tc, gn=gn: e.dma_start_transpose(
                        gnT_sl[fc][:, tc * 128:(tc + 1) * 128], gn[:, fc * 128:(fc + 1) * 128]),
                      reads=[("gn", tc % 2)], writes=[("gnT", fc, tc)], note=f"gnT{c},{fc}")
        for half in range(2):
            for kc in range(16):
                wt, wtok = wload_e(
                    (lambda kc=kc, half=half: w_outT[kc * 128:(kc + 1) * 128,
                                                     half * 512:(half + 1) * 512]),
                    512, note=f"woutw{sl}")
                for tc in range(4):
                    K.op("tensor", lambda e, wt=wt, kc=kc, tc=tc, half=half:
                         e.matmul(PS_BANKS[4 + tc][:, :], gnT_sl[kc][:, tc * 128:(tc + 1) * 128],
                                  wt[:, 0:512], start=(kc == 0), stop=(kc == 15)),
                         reads=[wtok, ("gnT", kc, tc)], writes=[PSTOK[4 + tc]],
                         note=f"wout{sl},{half},{kc},{tc}")
            for tc in range(4):
                c = sl * 4 + tc
                K.op("vector", lambda e, tc=tc: e.tensor_copy(oacc[tc][:], PS_BANKS[4 + tc][:, :]),
                     reads=[PSTOK[4 + tc]], writes=[("oacc", tc)], note=f"oev{sl},{half},{tc}")
                accum_to_xs(oacc[tc][:], (c * LC, (c + 1) * LC), (half * 512, (half + 1) * 512),
                            ("oacc", tc), c, note=f"oxs{sl},{half},{tc}")

    close_arena(epi, "m_epi")
    close_arena(sl1, "sl1")

    def emit_ffn(ph):
        fa = arena()
        sbf = mk_sb(fa)
        psf = mk_ps(fa)
        BK = [psf(f"{ph}bank{i}", (128, 512)) for i in range(8)]
        BT = [(f"{ph}psb", i) for i in range(8)]
        xnf = [sbf(f"{ph}xnf{fc}", (128, T), BF16) for fc in range(8)]
        la = arena()
        ln_phase(ph, mk_sb(la), None, xnf)
        close_arena(la, ph + "ln")
        XT = lambda kc: [(ph + "xnf", kc, c) for c in range(NC)]

        wl = mk_wstream(ph, sbf, n_stage=6, cols=128)

        wr_stg = sbf(f"{ph}wrs", (128, 64))
        wr_bf = sbf(f"{ph}wrb", (128, 64), BF16)
        K.dma("wload", "sync", lambda e: e.dma_start(
                out=wr_stg[:].rearrange("p (a c) -> p a c", c=8),
                in_=wrT[:].rearrange("(a b) c -> b a c", b=128)),
              writes=[(ph + "wrs",)])
        K.op("vector", lambda e: e.tensor_copy(wr_bf[:], wr_stg[:]),
             reads=[(ph + "wrs",)], writes=[(ph + "wrb",)])
        expr = sbf(f"{ph}expr", (128, 8))
        sumr = sbf(f"{ph}sumr", (128, 2))
        probs7 = sbf(f"{ph}probs7", (128, 16))
        for c in range(NC):
            for kc in range(8):
                K.op("tensor", lambda e, c=c, kc=kc: e.matmul(
                        BK[7][0:128, 0:8], xnf[kc][:, c * LC:(c + 1) * LC],
                        wr_bf[:, kc * 8:(kc + 1) * 8], start=(kc == 0), stop=(kc == 7)),
                     reads=[(ph + "wrb",), (ph + "xnf", kc, c)], writes=[BT[7]],
                     note=f"{ph}r{c}")
            K.op("scalar", lambda e, c=c: e.activation(
                    expr[:], BK[7][0:128, 0:8], AF.Exp, accum_out=sumr[:, 0:1]),
                 reads=[BT[7]], writes=[(ph + "expr",), (ph + "sumr",)], tail=True,
                 note=f"{ph}rexp{c}")
            K.op("vector", lambda e: e.reciprocal(sumr[:, 1:2], sumr[:, 0:1]),
                 reads=[(ph + "sumr",)], writes=[(ph + "sumr", 2)], tail=True)
            K.op("vector", lambda e, c=c: e.tensor_scalar_mul(
                    probs7[:, c:c + 1], expr[:, 7:8], sumr[:, 1:2]),
                 reads=[(ph + "expr",), (ph + "sumr", 2)], writes=[(ph + "p7", c)], tail=True)
        p7T = sbf(f"{ph}p7T", (16, 128))
        p7row = sbf(f"{ph}p7row", (1, T))
        probs_rep = sbf(f"{ph}prep", (128, T))
        p7_scr = nc.dram_tensor(f"{ph}p7scr", [16, 128], F32, kind="Internal").ap()
        K.op("tensor", lambda e: e.transpose(BK[6][0:16, 0:128], probs7[:], eyef[:]),
             reads=[(ph + "p7", c) for c in range(NC)] + EYEF_T, writes=[BT[6]])
        K.op("vector", lambda e: e.tensor_copy(p7T[:], BK[6][0:16, 0:128]),
             reads=[BT[6]], writes=[(ph + "p7T",)])
        K.dma("bcast", "gpsimd", lambda e: e.dma_start(out=p7_scr[:], in_=p7T[:]),
              reads=[(ph + "p7T",)], writes=[(ph + "p7scr",)])
        K.dma("bcast", "gpsimd", lambda e: e.dma_start(
                out=p7row[0:1, :], in_=p7_scr[:]),
              reads=[(ph + "p7scr",)], writes=[(ph + "p7row",)])
        for zs in range(4):
            K.op("tensor", lambda e, zs=zs: e.matmul(
                    BK[6][:, :], ones1[:], p7row[0:1, zs * 512:(zs + 1) * 512],
                    start=True, stop=True),
                 reads=ONES_T + [(ph + "p7row",)], writes=[BT[6]], note=f"{ph}prep{zs}")
            K.op("vector", lambda e, zs=zs: e.tensor_copy(
                    probs_rep[:, zs * 512:(zs + 1) * 512], BK[6][:, :]),
                 reads=[BT[6]], writes=[(ph + "prep", zs)], note=f"{ph}prepe{zs}")

        b1_16 = sbf(f"{ph}b116", (32, 128))
        b1c = sbf(f"{ph}b1c", (128, 32))
        b2_16 = sbf(f"{ph}b216", (8, 128))
        b2c = sbf(f"{ph}b2c", (128, 8))
        K.dma("const", "sync", lambda e: e.dma_start(
                out=b1_16[:], in_=b1[:].rearrange("o (a b) -> (o a) b", b=128)),
              writes=[(ph + "b116",)])
        K.op("tensor", lambda e: e.transpose(BK[6][0:128, 0:32], b1_16[:], eyef[0:32, 0:32]),
             reads=[(ph + "b116",)] + EYEF_T, writes=[BT[6]])
        K.op("vector", lambda e: e.tensor_copy(b1c[:], BK[6][0:128, 0:32]),
             reads=[BT[6]], writes=[(ph + "b1c",)])
        K.dma("const", "sync", lambda e: e.dma_start(
                out=b2_16[:], in_=b2[:].rearrange("o (a b) -> (o a) b", b=128)),
              writes=[(ph + "b216",)])
        K.op("tensor", lambda e: e.transpose(BK[6][0:128, 0:8], b2_16[:], eyef[0:8, 0:8]),
             reads=[(ph + "b216",)] + EYEF_T, writes=[BT[6]])
        K.op("vector", lambda e: e.tensor_copy(b2c[:], BK[6][0:128, 0:8]),
             reads=[BT[6]], writes=[(ph + "b2c",)])

        h1g = [sbf(f"{ph}h1g{m}", (128, T), BF16) for m in range(32)]
        for m in range(32):
            for kc in range(8):
                wt, wtok = wl((lambda kc=kc, m=m: w1[kc * 128:(kc + 1) * 128,
                                                    m * 128:(m + 1) * 128]), 128,
                              note=f"{ph}w1{m}")
                for ns in range(4):
                    K.op("tensor", lambda e, wt=wt, kc=kc, ns=ns, m=m: e.matmul(
                            BK[(m % 2) * 4 + ns][:, :], wt[:, 0:128],
                            xnf[kc][:, ns * 512:(ns + 1) * 512],
                            start=(kc == 0), stop=(kc == 7)),
                         reads=[wtok] + XT(kc), writes=[BT[(m % 2) * 4 + ns]],
                         note=f"{ph}w1mm{m},{kc},{ns}")
            for ns in range(4):
                K.op("scalar", lambda e, m=m, ns=ns: e.activation(
                        h1g[m][:, ns * 512:(ns + 1) * 512], BK[(m % 2) * 4 + ns][:, :],
                        AF.Gelu, bias=b1c[:, m:m + 1]),
                     reads=[BT[(m % 2) * 4 + ns], (ph + "b1c",)],
                     writes=[(ph + "h1g", m, ns)], note=f"{ph}gelu{m},{ns}")

        outf = [sbf(f"{ph}outf{i}", (128, T)) for i in range(2)]
        otok = [sbf(f"{ph}otok{i}", (128, 128)) for i in range(4)]
        for m in range(8):
            g = (m % 2) * 4
            for kc in range(32):
                wt, wtok = wl((lambda kc=kc, m=m: w2[kc * 128:(kc + 1) * 128,
                                                    m * 128:(m + 1) * 128]), 128,
                              note=f"{ph}w2{m}")
                for ns in range(4):
                    K.op("tensor", lambda e, wt=wt, kc=kc, ns=ns, g=g, m=m: e.matmul(
                            BK[g + ns][:, :], wt[:, 0:128],
                            h1g[kc][:, ns * 512:(ns + 1) * 512],
                            start=(kc == 0), stop=(kc == 31)),
                         reads=[wtok, (ph + "h1g", kc, ns)], writes=[BT[g + ns]],
                         note=f"{ph}w2mm{m},{kc},{ns}")
            of = outf[m % 2]
            oft = (ph + "outf", m % 2)
            for ns in range(4):
                K.op("vector", lambda e, m=m, ns=ns, g=g, of=of: e.scalar_tensor_tensor(
                        of[:, ns * 512:(ns + 1) * 512], BK[g + ns][:, :], b2c[:, m:m + 1],
                        probs_rep[:, ns * 512:(ns + 1) * 512], ALU.add, ALU.mult),
                     reads=[BT[g + ns], (ph + "b2c",)] + [(ph + "prep", ns)],
                     writes=[oft], note=f"{ph}oev{m},{ns}")
            for tc in range(NC):
                tbk = ((m + 1) % 2) * 4 + (tc % 4)
                K.op("tensor", lambda e, tc=tc, of=of, tbk=tbk: e.transpose(
                        BK[tbk][:, 0:128], of[:, tc * 128:(tc + 1) * 128], eyef[:]),
                     reads=[oft] + EYEF_T, writes=[BT[tbk]], note=f"{ph}otr{m},{tc}")
                oa = otok[tc % 4]
                K.op("vector", lambda e, tc=tc, oa=oa, tbk=tbk: e.tensor_copy(
                        oa[:], BK[tbk][:, 0:128]),
                     reads=[BT[tbk]], writes=[(ph + "otok", tc % 4)], note=f"{ph}oco{m},{tc}")
                K.dma("xacc", "gpsimd",
                      lambda e, tc=tc, m=m, oa=oa: e.dma_start(
                          out=xs[tc * LC:(tc + 1) * LC, m * 128:(m + 1) * 128],
                          in_=oa[:], accum_op=ALU.add),
                      reads=[(ph + "otok", tc % 4), ("xs", tc)], writes=[("xs", tc)],
                      note=f"{ph}oacc{m},{tc}")
        close_arena(fa, ph)

    if n_sublayers >= 2:
        emit_ffn("f1")

    def emit_attn(ph="a"):
        aa = arena()
        sba = mk_sb(aa)
        psa = mk_ps(aa)
        SC = psa(f"{ph}sc", (128, 2048))
        BKrest = [psa(f"{ph}bank{i}", (128, 512)) for i in range(4)]
        BK = [SC[:, i * 512:(i + 1) * 512] for i in range(4)] + BKrest
        BT = [(f"{ph}psb", i) for i in range(8)]
        xnf = [sba(f"{ph}xnf{fc}", (128, T), BF16) for fc in range(8)]
        la = arena()
        ln_phase(ph, mk_sb(la), None, xnf)
        close_arena(la, ph + "ln")
        XT = lambda kc: [(ph + "xnf", kc, c) for c in range(NC)]
        wl = mk_wstream(ph, sba, n_stage=6, cols=128)

        q_f = [sba(f"{ph}qf{m}", (128, T), BF16) for m in range(8)]
        for m in range(8):
            for kc in range(8):
                wt, wtok = wl((lambda kc=kc, m=m: wq[kc * 128:(kc + 1) * 128,
                                                    m * 128:(m + 1) * 128]), 128,
                              note=f"{ph}wq{m}")
                for ns in range(4):
                    K.op("tensor", lambda e, wt=wt, kc=kc, ns=ns, m=m: e.matmul(
                            BK[(m % 2) * 4 + ns][:, :], wt[:, 0:128],
                            xnf[kc][:, ns * 512:(ns + 1) * 512],
                            start=(kc == 0), stop=(kc == 7)),
                         reads=[wtok] + XT(kc), writes=[BT[(m % 2) * 4 + ns]],
                         note=f"{ph}qmm{m},{kc},{ns}")
            for ns in range(4):
                K.op("vector", lambda e, m=m, ns=ns: e.tensor_scalar_mul(
                        q_f[m][:, ns * 512:(ns + 1) * 512], BK[(m % 2) * 4 + ns][:, :],
                        1.0 / np.sqrt(128.0)),
                     reads=[BT[(m % 2) * 4 + ns]], writes=[(ph + "qf", m, ns)],
                     note=f"{ph}qev{m},{ns}")
        klat = sba(f"{ph}klat", (64, T), BF16)
        vlat = sba(f"{ph}vlat", (64, T), BF16)
        for (wsrc, dst, dtok) in ((wk_down, klat, "klat"), (wv_down, vlat, "vlat")):
            for kc in range(8):
                wt, wtok = wl((lambda kc=kc, wsrc=wsrc: wsrc[kc * 128:(kc + 1) * 128, 0:64]),
                              64, note=f"{ph}{dtok}")
                for ns in range(4):
                    K.op("tensor", lambda e, wt=wt, kc=kc, ns=ns: e.matmul(
                            BK[4 + ns][0:64, :], wt[:, 0:64],
                            xnf[kc][:, ns * 512:(ns + 1) * 512],
                            start=(kc == 0), stop=(kc == 7)),
                         reads=[wtok] + XT(kc), writes=[BT[4 + ns]],
                         note=f"{ph}{dtok}mm{kc},{ns}")
            for ns in range(4):
                K.op("vector", lambda e, ns=ns, dst=dst: e.tensor_copy(
                        dst[:, ns * 512:(ns + 1) * 512], BK[4 + ns][0:64, :]),
                     reads=[BT[4 + ns]], writes=[(ph + dtok, ns)], note=f"{ph}{dtok}ev{ns}")

        wup_stg = sba(f"{ph}wups", (64, 128))
        wkup2 = sba(f"{ph}wkup2", (64, 128), BF16)
        wvup2 = sba(f"{ph}wvup2", (64, 128), BF16)
        for (hbm, dst, tok) in ((wk_up, wkup2, "wkup2"), (wv_up, wvup2, "wvup2")):
            for kv in range(2):
                K.dma("wload", "sync", lambda e, hbm=hbm, kv=kv: e.dma_start(
                        out=wup_stg[kv * 32:(kv + 1) * 32, :], in_=hbm[:]),
                      writes=[(ph + "wups", kv)])
            K.op("vector", lambda e, dst=dst: e.tensor_copy(dst[:], wup_stg[:]),
                 reads=[(ph + "wups", 0), (ph + "wups", 1)], writes=[(ph + tok,)])
        k_f = [sba(f"{ph}kf{kv}", (128, T), BF16) for kv in range(2)]
        v_tok = [sba(f"{ph}vt{kv}", (128, T), BF16) for kv in range(2)]
        for kv in range(2):
            for ns in range(4):
                K.op("tensor", lambda e, kv=kv, ns=ns: e.matmul(
                        BK[ns][:, :], wkup2[kv * 32:(kv + 1) * 32, :],
                        klat[kv * 32:(kv + 1) * 32, ns * 512:(ns + 1) * 512],
                        start=True, stop=True),
                     reads=[(ph + "wkup2",)] + [(ph + "klat", ns)], writes=[BT[ns]],
                     note=f"{ph}kup{kv},{ns}")
                K.op("vector", lambda e, kv=kv, ns=ns: e.tensor_copy(
                        k_f[kv][:, ns * 512:(ns + 1) * 512], BK[ns][:, :]),
                     reads=[BT[ns]], writes=[(ph + "kf", kv, ns)], note=f"{ph}kev{kv},{ns}")
            for tc in range(NC):
                K.op("tensor", lambda e, kv=kv, tc=tc: e.matmul(
                        BK[4 + tc % 2][:, 0:128], vlat[kv * 32:(kv + 1) * 32, tc * LC:(tc + 1) * LC],
                        wvup2[kv * 32:(kv + 1) * 32, :], start=True, stop=True),
                     reads=[(ph + "wvup2",)] + [(ph + "vlat", tc // 4)], writes=[BT[4 + tc % 2]],
                     note=f"{ph}vup{kv},{tc}")
                K.op("vector", lambda e, kv=kv, tc=tc: e.tensor_copy(
                        v_tok[kv][:, tc * 128:(tc + 1) * 128], BK[4 + tc % 2][:, 0:128]),
                     reads=[BT[4 + tc % 2]], writes=[(ph + "vt", kv, tc)], note=f"{ph}vev{kv},{tc}")

        OT = [sba(f"{ph}ot{m}", (128, T), BF16) for m in range(8)]
        P_sb = [sba(f"{ph}P{i}", (128, T), BF16) for i in range(2)]
        PT_t = [[sba(f"{ph}PT{i}_{k}", (128, 128), BF16) for k in range(NC)] for i in range(2)]
        mxr = [sba(f"{ph}mx{i}", (128, 8)) for i in range(2)]
        rnormA = [sba(f"{ph}rnA{m}", (128, 16)) for m in range(8)]

        def attn_front(h, qc):
            kv = h // 4
            nk = (qc + 1) * 128
            nsb = (nk + 511) // 512
            mx = mxr[qc % 2]
            for sb_ in range(nsb):
                w = min(512, nk - sb_ * 512)
                K.op("tensor", lambda e, h=h, qc=qc, sb_=sb_, w=w, kv=kv: e.matmul(
                        BK[sb_][:, 0:w], q_f[h][:, qc * LC:(qc + 1) * LC],
                        k_f[kv][:, sb_ * 512: sb_ * 512 + w], start=True, stop=True),
                     reads=[(ph + "qf", h, qc // 4)] + [(ph + "kf", kv, n_) for n_ in range(4)],
                     writes=[BT[sb_]], note=f"{ph}sc{h},{qc},{sb_}")
            dsb = (nk - 128) // 512
            doff = (nk - 128) % 512
            K.op("vector", lambda e, dsb=dsb, doff=doff: e.tensor_add(
                    BK[dsb][:, doff:doff + 128], BK[dsb][:, doff:doff + 128], maska[:]),
                 reads=[BT[dsb]] + MASKA_T, writes=[BT[dsb]], note=f"{ph}msk{h},{qc}")
            sctoks = [BT[s_] for s_ in range(nsb)]
            K.op("vector", lambda e, nk=nk, mx=mx: e.tensor_reduce(
                    mx[:, 5:6], SC[:, 0:nk], mybir.AxisListType.X, ALU.max, negate=True),
                 reads=sctoks, writes=[(ph + "mx", qc % 2, "n")], tail=True,
                 note=f"{ph}mxn{h},{qc}")
            P = P_sb[qc % 2]
            K.op("scalar", lambda e, nk=nk, P=P, mx=mx: e.activation(
                    P[:, 0:nk], SC[:, 0:nk], AF.Exp, bias=mx[:, 5:6],
                    accum_out=mx[:, 3:4]),
                 reads=sctoks + [(ph + "mx", qc % 2, "n")],
                 writes=[(ph + "P", qc % 2, sb_) for sb_ in range(nsb)] +
                        [(ph + "rsum", qc % 2)],
                 tail=True, note=f"{ph}exp{h},{qc}")
            K.op("vector", lambda e, h=h, qc=qc, mx=mx: e.reciprocal(
                    rnormA[h][:, qc:qc + 1], mx[:, 3:4]),
                 reads=[(ph + "rsum", qc % 2)], writes=[(ph + "rnA", h, qc)], tail=True,
                 note=f"{ph}rn{h},{qc}")

        def attn_back(h, qc):
            kv = h // 4
            P = P_sb[qc % 2]
            for kb in range(qc + 1):
                PT = PT_t[qc % 2][kb]
                tb = 6 + kb % 2
                K.op("tensor", lambda e, kb=kb, P=P, tb=tb: e.transpose(
                        BK[tb][:, 0:64].bitcast(BF16), P[:, kb * 128:(kb + 1) * 128], eyeb[:]),
                     reads=[(ph + "P", qc % 2, kb * 128 // 512)] + EYEB_T,
                     writes=[BT[tb]], note=f"{ph}ptr{h},{qc},{kb}")
                K.op("vector" if kb % 2 == 0 else "scalar",
                     (lambda e, kb=kb, PT=PT, tb=tb: e.tensor_copy(
                         PT[:], BK[tb][:, 0:64].bitcast(BF16))) if kb % 2 == 0 else
                     (lambda e, kb=kb, PT=PT, tb=tb: e.copy(
                         PT[:], BK[tb][:, 0:64].bitcast(BF16))),
                     reads=[BT[tb]], writes=[(ph + "PT", qc % 2, kb)],
                     note=f"{ph}pte{h},{qc},{kb}")
            for kb in range(qc + 1):
                PT = PT_t[qc % 2][kb]
                K.op("tensor", lambda e, kb=kb, kv=kv, qc=qc, PT=PT: e.matmul(
                        BK[4 + qc % 2][:, 0:128], v_tok[kv][:, kb * 128:(kb + 1) * 128],
                        PT[:], start=(kb == 0), stop=(kb == qc)),
                     reads=[(ph + "PT", qc % 2, kb), (ph + "vt", kv, kb)],
                     writes=[BT[4 + qc % 2]], note=f"{ph}pv{h},{qc},{kb}")
            K.op("vector", lambda e, h=h, qc=qc: e.tensor_copy(
                    OT[h][:, qc * 128:(qc + 1) * 128], BK[4 + qc % 2][:, 0:128]),
                 reads=[BT[4 + qc % 2]], writes=[(ph + "ot", h, qc)], note=f"{ph}oev{h},{qc}")

        prev = None
        for h in range(NH):
            for qc in range(NC):
                attn_front(h, qc)
                if prev is not None:
                    attn_back(*prev)
                prev = (h, qc)
        attn_back(*prev)

        rnT = sba(f"{ph}rnT", (16, 128))
        rnrow = sba(f"{ph}rnrow", (1, T))
        rnrep = sba(f"{ph}rnrep", (128, T))
        rn_scr = nc.dram_tensor(f"{ph}rnscr", [16, 128], F32, kind="Internal").ap()
        for h in range(NH):
            K.op("tensor", lambda e, h=h: e.transpose(BK[6][0:16, 0:128], rnormA[h][:], eyef[:]),
                 reads=[(ph + "rnA", h, qc) for qc in range(NC)] + EYEF_T, writes=[BT[6]],
                 note=f"{ph}rnT{h}")
            K.op("vector", lambda e: e.tensor_copy(rnT[:], BK[6][0:16, 0:128]),
                 reads=[BT[6]], writes=[(ph + "rnT",)], note=f"{ph}rnTe{h}")
            K.dma("bcast", "gpsimd", lambda e: e.dma_start(out=rn_scr[:], in_=rnT[:]),
                  reads=[(ph + "rnT",)], writes=[(ph + "rnscr",)])
            K.dma("bcast", "gpsimd", lambda e: e.dma_start(out=rnrow[0:1, :], in_=rn_scr[:]),
                  reads=[(ph + "rnscr",)], writes=[(ph + "rnrow",)])
            for zs in range(4):
                K.op("tensor", lambda e, zs=zs: e.matmul(
                        BK[6][:, :], ones1[:], rnrow[0:1, zs * 512:(zs + 1) * 512],
                        start=True, stop=True),
                     reads=ONES_T + [(ph + "rnrow",)], writes=[BT[6]], note=f"{ph}rnb{h},{zs}")
                K.op("vector", lambda e, zs=zs: e.tensor_copy(
                        rnrep[:, zs * 512:(zs + 1) * 512], BK[6][:, :]),
                     reads=[BT[6]], writes=[(ph + "rnrep", zs)], note=f"{ph}rne{h},{zs}")
            K.op("vector", lambda e, h=h: e.tensor_mul(OT[h][:], OT[h][:], rnrep[:]),
                 reads=[(ph + "ot", h, qc) for qc in range(NC)] +
                       [(ph + "rnrep", zs) for zs in range(4)],
                 writes=[(ph + "otn", h)], note=f"{ph}on{h}")

        wo_sb = [sba(f"{ph}wo{oc}", (128, DM), BF16) for oc in range(8)]
        wo_stg = [sba(f"{ph}wos{i}", (128, DM)) for i in range(2)]
        for oc in range(8):
            stgt = wo_stg[oc % 2]
            K.dma("wload", "sync", lambda e, oc=oc, stgt=stgt: e.dma_start(
                    out=stgt[:], in_=wo[oc * 128:(oc + 1) * 128, :]),
                  writes=[(ph + "wos", oc % 2)])
            K.op("vector", lambda e, oc=oc, stgt=stgt: e.tensor_copy(wo_sb[oc][:], stgt[:]),
                 reads=[(ph + "wos", oc % 2)], writes=[(ph + "wo", oc)])
        oacc = [sba(f"{ph}oacc{i}", (128, 512)) for i in range(4)]
        for tc in range(NC):
            for half in range(2):
                b = (tc % 2) * 4 + half
                for oc in range(8):
                    K.op("tensor", lambda e, tc=tc, half=half, oc=oc, b=b: e.matmul(
                            BK[b][:, :], OT[oc][:, tc * LC:(tc + 1) * LC],
                            wo_sb[oc][:, half * 512:(half + 1) * 512],
                            start=(oc == 0), stop=(oc == 7)),
                         reads=[(ph + "otn", oc), (ph + "wo", oc)], writes=[BT[b]],
                         note=f"{ph}wo{tc},{half},{oc}")
                oa = oacc[(tc % 2) * 2 + half]
                K.op("vector", lambda e, b=b, oa=oa: e.tensor_copy(oa[:], BK[b][:, :]),
                     reads=[BT[b]], writes=[(ph + "oacc", (tc % 2) * 2 + half)],
                     note=f"{ph}woev{tc},{half}")
                K.dma("xacc", "gpsimd", lambda e, tc=tc, half=half, oa=oa: e.dma_start(
                        out=xs[tc * LC:(tc + 1) * LC, half * 512:(half + 1) * 512],
                        in_=oa[:], accum_op=ALU.add),
                      reads=[(ph + "oacc", (tc % 2) * 2 + half), ("xs", tc)],
                      writes=[("xs", tc)], note=f"{ph}woacc{tc},{half}")
        close_arena(aa, ph)

    if n_sublayers >= 3:
        emit_attn("a")
    if n_sublayers >= 4:
        emit_ffn("f2")

    for c in range(NC):
        K.dma("ostore", "gpsimd",
              lambda e, c=c: e.dma_start(out=out[c * LC:(c + 1) * LC, :], in_=xs[c * LC:(c + 1) * LC, :]),
              reads=[("xs", c)], writes=[("outd", c)])

    fin = gsb("fin", (1, 1))
    K.op("vector", lambda e: e.memset(fin[:], 1.0), reads=[("outd", c) for c in range(NC)],
         writes=[("fin",)], note="fence")

    K.lower()
    for s in stacks:
        try:
            s.close()
        except Exception:
            pass
    return nc, dbg



def _mask_mamba():
    s = np.arange(128)[:, None]
    t = np.arange(128)[None, :]
    return np.where(t >= s, 0.0, MASKBIG).astype(np.float32)


def _blockmask():
    import ml_dtypes
    m = np.zeros((128, DI), np.float32)
    for h in range(8):
        m[h * 16:(h + 1) * 16, h * 256:(h + 1) * 256] = 1.0
    return m.astype(ml_dtypes.bfloat16)


def _mask_attn():
    q = np.arange(128)[:, None]
    k = np.arange(128)[None, :]
    return np.where(k <= q, 0.0, -MASKBIG).astype(np.float32)


def host_inputs(inputs, core):
    import ml_dtypes
    f = np.ascontiguousarray
    x = inputs["L0_attn_a_x"]
    d = {
        "x0": f(np.asarray(x[core], dtype=np.float32)),
        "w_inT": f(np.asarray(inputs["L0_attn_a_W_in"]).T),
        "w_dtT": f(np.asarray(inputs["L0_attn_a_W_dt"]).T),
        "conv_w": f(np.asarray(inputs["L0_attn_a_conv_w"])),
        "conv_b": f(np.asarray(inputs["L0_attn_a_conv_b"]).reshape(1, DI)),
        "a_log": f(np.asarray(inputs["L0_attn_a_A_log"]).reshape(1, H * DS)),
        "dskip": f(np.asarray(inputs["L0_attn_a_Dskip"]).reshape(1, DI)),
        "dt_bias": f(np.asarray(inputs["L0_attn_a_dt_bias"]).reshape(H, 1)),
        "norm_w": f(np.asarray(inputs["L0_attn_a_norm_w"]).reshape(1, DI)),
        "w_resT": f(np.asarray(inputs["L0_attn_a_W_res"]).T),
        "w_outT": f(np.asarray(inputs["L0_attn_a_W_out"]).T),
        "wq": f(np.asarray(inputs["L0_attn_b_wq"])),
        "wk_down": f(np.asarray(inputs["L0_attn_b_wk_down"])),
        "wv_down": f(np.asarray(inputs["L0_attn_b_wv_down"])),
        "wk_up": f(np.asarray(inputs["L0_attn_b_wk_up"])),
        "wv_up": f(np.asarray(inputs["L0_attn_b_wv_up"])),
        "wo": f(np.asarray(inputs["L0_attn_b_wo"])),
        "wrT": f(np.asarray(inputs["L0_ffn_Wr"]).T),
        "w1": f(np.asarray(inputs["L0_ffn_W1"][7])),
        "b1": f(np.asarray(inputs["L0_ffn_b1"][7]).reshape(1, F1)),
        "w2": f(np.asarray(inputs["L0_ffn_W2"][7])),
        "b2": f(np.asarray(inputs["L0_ffn_b2"][7]).reshape(1, DM)),
        "cpool": np.broadcast_to(np.asarray(CONSTS, dtype=np.float32)[None, :],
                                 (128, len(CONSTS))).copy(),
        "eye_f": np.eye(128, dtype=np.float32),
        "eye_b": np.eye(128, dtype=ml_dtypes.bfloat16),
        "ones1": np.ones((1, 128), dtype=np.float32),
        "mask_mamba": _mask_mamba(),
        "mask_attn": _mask_attn(),
        "blockmask": _blockmask(),
    }
    return d


_PROG_CACHE = {}


def _get_prog(taps=()):
    key = tuple(sorted(taps))
    if key not in _PROG_CACHE:
        _PROG_CACHE[key] = build_program(taps=key)
    return _PROG_CACHE[key]


def kernel(**inputs):
    nc, dbg = _get_prog()
    in_maps = [host_inputs(inputs, core) for core in range(8)]
    res = run_bass_kernel_spmd(nc, in_maps, core_ids=list(range(8)))
    outs = np.stack([np.asarray(r["out"]) for r in res.results], axis=0)
    return outs.astype(np.float32)


# version 22
# speedup vs baseline: 1.0129x; 1.0066x over previous
import numpy as np
import concourse.bass as bass
import concourse.mybir as mybir
from concourse.bass_utils import run_bass_kernel_spmd

F32 = mybir.dt.float32
BF16 = mybir.dt.bfloat16
AF = mybir.ActivationFunctionType
ALU = mybir.AluOpType

T = 2048
NC = 16
LC = 128
DM = 1024
DI = 2048
H = 8
DS = 16
P = 256
DTR = 64
F1 = 4096
NKV = 2
HD = 128
DLAT = 32
NH = 8

MASKBIG = 1.0e30
DEBUG_EPI_BARRIERS = [False]

CONSTS = [1e-5, 1e-6, 1.0, 0.0, 1.0 / np.sqrt(128.0), 1.0 / DI, -1.0]
_CIDX = {v: i for i, v in enumerate(CONSTS)}

_COMPUTE_ENGINES = ("tensor", "vector", "scalar", "gpsimd")
_ALL_ENGINES = ("tensor", "vector", "scalar", "gpsimd", "sync")


class _Op:
    __slots__ = ("oid", "eng", "emit", "deps", "dma", "chan", "inc", "seq", "note", "tail")

    def __init__(self, oid, eng, emit, deps, dma, chan, note, tail):
        self.oid = oid
        self.eng = eng
        self.emit = emit
        self.deps = deps
        self.dma = dma
        self.chan = chan
        self.inc = False
        self.seq = 0
        self.note = note
        self.tail = tail


class Sched:

    def __init__(self, nc):
        self.nc = nc
        self.ops = []
        self.last_writer = {}
        self.readers = {}
        self.chan_engine = {}
        self.chan_ctr = {}
        self.eng_last = {}
        self.chan_last = {}

    def op(self, eng, emit, reads=(), writes=(), note="", tail=False):
        return self._add(eng, emit, reads, writes, dma=False, chan=None, note=note, tail=tail)

    ROT = {"wload": 8, "xload": 6, "xacc": 4, "spill": 4, "xtrans": 4,
           "gtrans": 4, "ptrans": 4, "ostore": 4, "xseed": 4}

    def dma(self, chan, eng, emit, reads=(), writes=(), note=""):
        r = self.ROT.get(chan, 2)
        ctr = self.chan_ctr.get(chan, 0)
        self.chan_ctr[chan] = ctr + 1
        chan = f"{chan}#{ctr % r}"
        if chan in self.chan_engine:
            assert self.chan_engine[chan] == eng, (chan, eng)
        else:
            self.chan_engine[chan] = eng
        extra = ()
        if chan in self.chan_last:
            extra = (self.chan_last[chan],)
        return self._add(eng, emit, reads, writes, dma=True, chan=chan, note=note,
                         tail=False, extra_deps=extra)

    def _add(self, eng, emit, reads, writes, dma, chan, note, tail=False, extra_deps=()):
        oid = len(self.ops)
        deps = set(extra_deps)
        for t in reads:
            w = self.last_writer.get(t)
            if w is not None:
                deps.add(w)
        for t in writes:
            w = self.last_writer.get(t)
            if w is not None:
                deps.add(w)
            for r in self.readers.get(t, ()):
                deps.add(r)
        deps.discard(oid)
        for t in reads:
            self.readers.setdefault(t, []).append(oid)
        for t in writes:
            self.last_writer[t] = oid
            self.readers[t] = []
        self.ops.append(_Op(oid, eng, emit, deps, dma, chan, note, tail))
        if dma:
            self.chan_last[chan] = oid
        else:
            self.eng_last[eng] = oid
        return oid

    def barrier(self, note="barrier"):
        snap = list(self.eng_last.values()) + list(self.chan_last.values())
        for e in _ALL_ENGINES:
            deps = [d for d in snap if self.ops[d].eng != e or self.ops[d].dma]
            self._add(e, lambda eng: eng.nop(), (), (), dma=False, chan=None,
                      note=note, tail=False, extra_deps=deps)

    def lower(self):
        nc = self.nc
        ops = self.ops
        for op in ops:
            if op.dma:
                op.inc = True
            real = []
            for d in op.deps:
                dop = ops[d]
                if (not op.dma) and (not dop.dma) and dop.eng == op.eng and not dop.tail:
                    continue
                real.append(d)
                dop.inc = True
            op.deps = real
        import contextlib
        self._stack = contextlib.ExitStack()
        sems = {}
        for e in _ALL_ENGINES:
            sems[("eng", e)] = self._stack.enter_context(nc.semaphore(f"sem_{e}"))
        for ch in self.chan_engine:
            sems[("chan", ch)] = self._stack.enter_context(nc.semaphore(f"sem_c_{ch}"))
        counts = {}
        for op in ops:
            if op.inc:
                key = ("chan", op.chan) if op.dma else ("eng", op.eng)
                amt = 16 if op.dma else 1
                counts[key] = counts.get(key, 0) + amt
                op.seq = counts[key]
        marks = {e: {} for e in _ALL_ENGINES}
        plans = {e: [] for e in _ALL_ENGINES}
        for op in ops:
            m = marks[op.eng]
            for d in sorted(op.deps):
                dop = ops[d]
                key = ("chan", dop.chan) if dop.dma else ("eng", dop.eng)
                assert dop.inc and dop.seq > 0
                if m.get(key, 0) >= dop.seq:
                    continue
                m[key] = dop.seq
                plans[op.eng].append(("wait", key, dop.seq))
            plans[op.eng].append(("op", op))

        with nc.Block() as block:
            def mk(eng_name):
                def body(eng):
                    for item in plans[eng_name]:
                        if item[0] == "wait":
                            eng.wait_ge(sems[item[1]], item[2])
                        else:
                            op = item[1]
                            inst = op.emit(eng)
                            if op.inc:
                                inst.then_inc(
                                    sems[("chan", op.chan) if op.dma else ("eng", op.eng)],
                                    16 if op.dma else 1)
                return body
            for e, attr in (("sync", block.sync), ("tensor", block.tensor),
                            ("vector", block.vector), ("scalar", block.scalar),
                            ("gpsimd", block.gpsimd)):
                if plans[e]:
                    attr(mk(e))
        return sems, counts



def build_program(taps=(), n_sublayers=4):
    import contextlib
    taps = set(taps)
    nc = bass.Bass("TRN2", target_bir_lowering=False)

    din = {}
    def dram_in(name, shape, dtype=F32):
        din[name] = nc.dram_tensor(name, list(shape), dtype, kind="ExternalInput").ap()
        return din[name]

    x0 = dram_in("x0", (T, DM))
    w_inT = dram_in("w_inT", (DM, 4416))
    w_dtT = dram_in("w_dtT", (DTR, H))
    conv_w = dram_in("conv_w", (DI, 4))
    conv_b = dram_in("conv_b", (1, DI))
    a_log = dram_in("a_log", (1, H * DS))
    dskip = dram_in("dskip", (1, DI))
    dt_bias = dram_in("dt_bias", (H, 1))
    norm_w = dram_in("norm_w", (1, DI))
    w_resT = dram_in("w_resT", (DM, DI))
    w_outT = dram_in("w_outT", (DI, DM))
    wq = dram_in("wq", (DM, NH * HD))
    wk_down = dram_in("wk_down", (DM, NKV * DLAT))
    wv_down = dram_in("wv_down", (DM, NKV * DLAT))
    wk_up = dram_in("wk_up", (DLAT, HD))
    wv_up = dram_in("wv_up", (DLAT, HD))
    wo = dram_in("wo", (NH * HD, DM))
    wrT = dram_in("wrT", (DM, 8))
    w1 = dram_in("w1", (DM, F1))
    b1 = dram_in("b1", (1, F1))
    w2 = dram_in("w2", (F1, DM))
    b2 = dram_in("b2", (1, DM))
    cpool_h = dram_in("cpool", (128, len(CONSTS)))
    eye_f = dram_in("eye_f", (128, 128))
    eye_b = dram_in("eye_b", (128, 128), BF16)
    ones1_h = dram_in("ones1", (1, 128))
    mask_mamba_h = dram_in("mask_mamba", (128, 128))
    mask_attn_h = dram_in("mask_attn", (128, 128))
    blockmask_h = dram_in("blockmask", (128, DI), BF16)

    out = nc.dram_tensor("out", [T, DM], F32, kind="ExternalOutput").ap()

    xs = nc.dram_tensor("xs", [T, DM], F32, kind="Internal").ap()
    y_scr = nc.dram_tensor("y_scr", [T, DI], BF16, kind="Internal").ap()
    d_scr = nc.dram_tensor("d_scr", [H, T], F32, kind="Internal").ap()
    decay_scr = nc.dram_tensor("decay_scr", [16, 128], F32, kind="Internal").ap()
    c_scr = nc.dram_tensor("c_scr", [128, T], BF16, kind="Internal").ap()
    cf_scr = [nc.dram_tensor(f"cf_scr{i}", [1, DI], BF16, kind="Internal").ap()
              for i in range(2)]
    a_scr = nc.dram_tensor("a_scr", [1, 128], F32, kind="Internal").ap()

    dbg = {}
    K = Sched(nc)

    stacks = []
    def arena():
        s = contextlib.ExitStack()
        stacks.append(s)
        return s

    def close_arena(s, note=""):
        K.barrier(note="close:" + note)
        s.close()

    g_arena = arena()

    def mk_sb(stack):
        def sb(name, shape, dtype=F32):
            return stack.enter_context(nc.sbuf_tensor(name, list(shape), dtype))
        return sb

    def mk_ps(stack):
        def ps(name, shape, dtype=F32):
            return stack.enter_context(nc.psum_tensor(name, list(shape), dtype))
        return ps

    gsb = mk_sb(g_arena)

    def tap(name, make_store):
        if name in taps:
            shape, dtype, store = make_store
            dbg[name] = nc.dram_tensor("dbg_" + name, list(shape), dtype,
                                       kind="ExternalOutput").ap()
            store(dbg[name])

    def tap_simple(name, ap_, shape, dtype, rtoks):
        def store(dst):
            K.dma("dbg", "gpsimd", lambda e: e.dma_start(out=dst[:], in_=ap_),
                  reads=list(rtoks))
        tap(name, (shape, dtype, store))

    cpool = gsb("cpool_s", (128, len(CONSTS)))
    eyef = gsb("eyef", (128, 128))
    eyeb = gsb("eyeb", (128, 128), BF16)
    ones1 = gsb("ones1c", (1, 128))
    maskm = gsb("maskm", (128, 128))
    maska = gsb("maska", (128, 128))

    def CONST(v, p=128):
        return cpool[0:p, _CIDX[v]:_CIDX[v] + 1]

    for t_, hbm in ((cpool, cpool_h), (eyef, eye_f), (eyeb, eye_b), (ones1, ones1_h),
                    (maskm, mask_mamba_h), (maska, mask_attn_h)):
        K.dma("const", "sync", lambda e, t_=t_, hbm=hbm: e.dma_start(out=t_[:], in_=hbm[:]),
              writes=[(t_.name if hasattr(t_, "name") else id(t_),)])
    CPOOL_T = [("cpool_s",)]
    EYEF_T = [("eyef",)]
    EYEB_T = [("eyeb",)]
    ONES_T = [("ones1c",)]
    MASKM_T = [("maskm",)]
    MASKA_T = [("maska",)]

    for c in range(NC):
        K.dma("xseed", "sync", lambda e, c=c: e.dma_start(
                out=xs[c * LC:(c + 1) * LC, :], in_=x0[c * LC:(c + 1) * LC, :]),
              writes=[("xs", c)])

    def ln_phase(phase, sb, ps, xn_f, src=None, srctok="xs"):
        src = xs if src is None else src
        nb = 5
        bufs = []
        for i in range(nb):
            bufs.append((sb(f"{phase}lnx{i}", (128, DM)), sb(f"{phase}lnbn{i}", (128, 12)),
                         sb(f"{phase}lnmv{i}", (128, 2)), sb(f"{phase}lnrs{i}", (128, 1)),
                         sb(f"{phase}lnxt{i}", (128, DM), BF16)))
        for c in range(NC):
            xc, bnb, mv, rs, xn_tok = bufs[c % nb]
            xct = (f"{phase}xc", c % nb)
            K.dma("xload", "sync",
                  lambda e, xc=xc, c=c: e.dma_start(out=xc[:], in_=src[c * LC:(c + 1) * LC, :]),
                  reads=[(srctok, c)], writes=[xct])
            bt = (f"{phase}bn", c % nb)
            K.op("vector", lambda e, xc=xc, bnb=bnb: e.bn_stats(bnb[:, 0:6], xc[:, 0:512]),
                 reads=[xct], writes=[bt], tail=True)
            K.op("vector", lambda e, xc=xc, bnb=bnb: e.bn_stats(bnb[:, 6:12], xc[:, 512:1024]),
                 reads=[xct], writes=[bt], tail=True)
            K.op("vector", lambda e, bnb=bnb, mv=mv: e.bn_aggr(
                     mv[:, 0:2], bnb[:, 0:12].rearrange("p (n t) -> p n t", t=6)),
                 reads=[bt], writes=[(f"{phase}mv", c % nb)], tail=True)
            K.op("scalar", lambda e, mv=mv, rs=rs: e.activation(
                     rs[:, 0:1], mv[:, 1:2], AF.Sqrt, bias=CONST(1e-5)),
                 reads=[(f"{phase}mv", c % nb)] + CPOOL_T, writes=[(f"{phase}rs", c % nb)])
            K.op("vector", lambda e, rs=rs: e.reciprocal(rs[:, 0:1], rs[:, 0:1]),
                 reads=[(f"{phase}rs", c % nb)], writes=[(f"{phase}rs", c % nb)], tail=True)
            K.op("vector",
                 lambda e, xc=xc, mv=mv, rs=rs, xn_tok=xn_tok: e.tensor_scalar(
                     xn_tok[:], xc[:], mv[:, 0:1], rs[:, 0:1], ALU.subtract, ALU.mult),
                 reads=[xct, (f"{phase}mv", c % nb), (f"{phase}rs", c % nb)],
                 writes=[(f"{phase}xntok", c % nb)])
            for fc in range(8):
                K.dma("xtrans", "scalar",
                      lambda e, xn_tok=xn_tok, fc=fc, c=c: e.dma_start_transpose(
                          xn_f[fc][:, c * LC:(c + 1) * LC], xn_tok[:, fc * 128:(fc + 1) * 128]),
                      reads=[(f"{phase}xntok", c % nb)], writes=[(f"{phase}xnf", fc, c)])
        return [(f"{phase}xnf", fc, c) for fc in range(8) for c in range(NC)]

    def mk_wstream(phase, sb, n_stage=4, cols=512):
        stg = [sb(f"{phase}wstg{i}", (128, cols)) for i in range(n_stage)]
        wbf = [sb(f"{phase}wbf{i}", (128, cols), BF16) for i in range(n_stage)]
        ctr = [0]

        def load(hbm_slice_fn, ncols, note=""):
            i = ctr[0] % n_stage
            ctr[0] += 1
            stok = (f"{phase}wstg", i)
            btok = (f"{phase}wbf", i)
            K.dma("wload", "sync",
                  lambda e, i=i: e.dma_start(out=stg[i][:, 0:ncols], in_=hbm_slice_fn()),
                  writes=[stok], note=note)
            K.op("vector",
                 lambda e, i=i: e.tensor_copy(wbf[i][:, 0:ncols], stg[i][:, 0:ncols]),
                 reads=[stok], writes=[btok], note=note)
            return wbf[i], btok
        return load

    def accum_to_xs(tile_ap, rows, cols, rtok, c, note=""):
        K.dma("xacc", "gpsimd",
              lambda e: e.dma_start(out=xs[rows[0]:rows[1], cols[0]:cols[1]],
                                    in_=tile_ap, accum_op=ALU.add),
              reads=[rtok, ("xs", c)], writes=[("xs", c)], note=note)

    sl1 = arena()
    sb1 = mk_sb(sl1)
    ps1 = mk_ps(sl1)
    PS_BANKS = [ps1(f"m_bank{i}", (128, 512)) for i in range(8)]
    PSTOK = [("psbank", i) for i in range(8)]

    xn_f = [sb1(f"m_xnf{fc}", (128, T), BF16) for fc in range(8)]
    ln_arena = arena()
    xnf_toks = ln_phase("m", mk_sb(ln_arena), ps1, xn_f, src=x0, srctok="x0dram")
    close_arena(ln_arena, "m_ln")
    sp_arena = arena()
    sb1 = mk_sb(sp_arena)
    XNF = lambda fc: [("m" + "xnf", fc, c) for c in range(NC)]

    for fc in range(8):
        tap_simple(f"xnf{fc}", xn_f[fc][:], (128, T), BF16, XNF(fc))

    u_raw = [sb1(f"m_uraw{fc}", (128, T + 4), BF16) for fc in range(16)]
    B_f = sb1("m_B_f", (128, T))
    dt_f = sb1("m_dt_f", (8, T))
    A_rep = sb1("m_Arep", (128, 128))
    Cbar = sb1("m_Cbar", (128, T), BF16)
    Bbar2 = sb1("m_Bbar2", (128, T))
    dtT_all = sb1("m_dtT", (128, 128))
    Dcol2d = sb1("m_Dcol", (128, 128))
    convw_cols = sb1("m_convw", (128, 64))
    convb_cols = sb1("m_convb", (128, 16))
    dskip_rep = sb1("m_dskip", (128, DI), BF16)
    blockmask = sb1("m_blockm", (128, DI), BF16)
    ones1b = sb1("m_ones1b", (1, 128), BF16)

    proj_arena = arena()
    sbp = mk_sb(proj_arena)
    dth = sbp("m_dth", (64, T), BF16)
    C_f = sbp("m_C_f", (128, T), BF16)

    for fc in range(16):
        K.op("vector", lambda e, fc=fc: e.memset(u_raw[fc][:, 0:4], 0.0),
             writes=[("uraw0", fc)])

    wload_m = mk_wstream("m", sbp, n_stage=6, cols=128)

    def project(moff, msize, bank_grp, evac_fn, note=""):
        banks = PS_BANKS[bank_grp * 4:(bank_grp + 1) * 4]
        btoks = PSTOK[bank_grp * 4:(bank_grp + 1) * 4]
        for kc in range(8):
            wtile, wtok = wload_m(
                (lambda kc=kc: w_inT[kc * 128:(kc + 1) * 128, moff:moff + msize]),
                msize, note=note)
            for ns in range(4):
                K.op("tensor",
                     lambda e, wtile=wtile, msize=msize, kc=kc, ns=ns, b=banks[ns]:
                         e.matmul(b[0:msize, :], wtile[:, 0:msize],
                                  xn_f[kc][:, ns * 512:(ns + 1) * 512],
                                  start=(kc == 0), stop=(kc == 7)),
                     reads=[wtok] + [("m" + "xnf", kc, c) for c in range(ns * 4, ns * 4 + 4)],
                     writes=[btoks[ns]], note=note)
        for ns in range(4):
            evac_fn(ns, banks[ns], btoks[ns])

    for fc in range(16):
        def evac_u(ns, bank, btok, fc=fc):
            K.op("vector",
                 lambda e, ns=ns, bank=bank, fc=fc: e.tensor_copy(
                     u_raw[fc][:, 4 + ns * 512: 4 + (ns + 1) * 512], bank[:, :]),
                 reads=[btok], writes=[("uraw", fc, ns)], note=f"uevac{fc}")
        project(2048 + fc * 128, 128, fc % 2, evac_u, note=f"uproj{fc}")

    def evac_dth(ns, bank, btok):
        K.op("vector", lambda e, ns=ns, bank=bank: e.tensor_copy(
                 dth[:, ns * 512:(ns + 1) * 512], bank[0:64, :]),
             reads=[btok], writes=[("dth", ns)])
    project(4096, 64, 0, evac_dth, note="dthproj")

    def evac_B(ns, bank, btok):
        K.op("vector", lambda e, ns=ns, bank=bank: e.tensor_copy(
                 B_f[:, ns * 512:(ns + 1) * 512], bank[:, :]),
             reads=[btok], writes=[("B_f", ns)])
    project(4160, 128, 1, evac_B, note="Bproj")

    def evac_C(ns, bank, btok):
        K.op("vector", lambda e, ns=ns, bank=bank: e.tensor_copy(
                 C_f[:, ns * 512:(ns + 1) * 512], bank[:, :]),
             reads=[btok], writes=[("C_f", ns)])
    project(4288, 128, 0, evac_C, note="Cproj")
    K.dma("spill", "gpsimd", lambda e: e.dma_start(out=c_scr[:], in_=C_f[:]),
          reads=[("C_f", ns) for ns in range(4)], writes=[("cscr",)])
    K.op("vector", lambda e: e.tensor_copy(ones1b[:], ones1[:]), reads=ONES_T,
         writes=[("ones1b",)])

    UR_T = [("uraw", fc, ns) for fc in range(16) for ns in range(4)] + \
           [("uraw0", fc) for fc in range(16)]
    tap_simple("u_raw0", u_raw[0][:, 4:4 + T], (128, T), BF16, [("uraw", 0, ns) for ns in range(4)])
    tap_simple("dth", dth[:], (64, T), BF16, [("dth", ns) for ns in range(4)])
    tap_simple("B_f", B_f[:], (128, T), F32, [("B_f", ns) for ns in range(4)])
    tap_simple("C_f", C_f[:], (128, T), BF16, [("C_f", ns) for ns in range(4)])

    wdt_stg = sbp("wdt_stg", (64, 8))
    wdt_bf = sbp("wdt_bf", (64, 8), BF16)
    dtbias_c = sbp("dtbias_c", (8, 1))
    K.dma("wload", "sync", lambda e: e.dma_start(out=wdt_stg[:], in_=w_dtT[:]), writes=[("wdts",)])
    K.op("vector", lambda e: e.tensor_copy(wdt_bf[:], wdt_stg[:]), reads=[("wdts",)], writes=[("wdtb",)])
    K.dma("const", "sync", lambda e: e.dma_start(out=dtbias_c[:], in_=dt_bias[:]), writes=[("dtbc",)])

    sptmp = sbp("m_sptmp", (8, T))
    sptmp2 = sbp("m_sptmp2", (8, T))
    for ns in range(4):
        K.op("tensor",
             lambda e, ns=ns: e.matmul(PS_BANKS[4 + ns][0:8, :], wdt_bf[:],
                                       dth[:, ns * 512:(ns + 1) * 512], start=True, stop=True),
             reads=[("wdtb",), ("dth", ns)], writes=[PSTOK[4 + ns]])
        K.op("scalar",
             lambda e, ns=ns: e.activation(sptmp[:, ns * 512:(ns + 1) * 512],
                                           PS_BANKS[4 + ns][0:8, :], AF.Abs, bias=dtbias_c[:]),
             reads=[PSTOK[4 + ns], ("dtbc",)], writes=[("sptmp", ns)])
        K.op("scalar",
             lambda e, ns=ns: e.activation(sptmp2[:, ns * 512:(ns + 1) * 512],
                                           sptmp[:, ns * 512:(ns + 1) * 512], AF.Exp,
                                           scale=CONST(-1.0, 8)),
             reads=[("sptmp", ns)] + CPOOL_T, writes=[("sptmp2", ns)])
        K.op("scalar",
             lambda e, ns=ns: e.activation(sptmp[:, ns * 512:(ns + 1) * 512],
                                           sptmp2[:, ns * 512:(ns + 1) * 512], AF.Ln,
                                           bias=CONST(1.0, 8)),
             reads=[("sptmp2", ns)] + CPOOL_T, writes=[("sptmp", ns)])
        K.op("scalar",
             lambda e, ns=ns: e.activation(dt_f[:, ns * 512:(ns + 1) * 512],
                                           PS_BANKS[4 + ns][0:8, :], AF.Relu, bias=dtbias_c[:]),
             reads=[PSTOK[4 + ns], ("dtbc",)], writes=[("dtf", ns)])
        K.op("vector",
             lambda e, ns=ns: e.tensor_add(dt_f[:, ns * 512:(ns + 1) * 512],
                                           dt_f[:, ns * 512:(ns + 1) * 512],
                                           sptmp[:, ns * 512:(ns + 1) * 512]),
             reads=[("dtf", ns), ("sptmp", ns)], writes=[("dtf", ns)])
    K.op("vector", lambda e: e.tensor_scalar(dt_f[:], dt_f[:], 1.0, 1e-4, ALU.min, ALU.max),
         reads=[("dtf", 0), ("dtf", 1), ("dtf", 2), ("dtf", 3)], writes=[("dtf", "clip")])
    tap_simple("dt_f", dt_f[:], (8, T), F32, [("dtf", "clip")])
    tap_simple("sptmp", sptmp[:], (8, T), F32, [("sptmp", ns) for ns in range(4)])

    D8 = sbp("m_D8", (8, T))
    K.op("vector", lambda e: e.tensor_tensor_scan(D8[:], dt_f[:], dt_f[:], 0.0, ALU.add, ALU.bypass),
         reads=[("dtf", "clip")], writes=[("D8",)], tail=True)
    tap_simple("D8", D8[:], (8, T), F32, [("D8",)])
    K.dma("bcast", "gpsimd", lambda e: e.dma_start(out=d_scr[:], in_=D8[:]),
          reads=[("D8",)], writes=[("dscr8",)])

    DedgeE = sbp("m_DedgeE", (8, 17))
    K.op("vector", lambda e: e.memset(DedgeE[:, 0:1], 0.0), writes=[("De", 0)])
    K.op("vector", lambda e: e.tensor_copy(
            DedgeE[:, 1:17], D8[:].rearrange("p (c l) -> p c l", l=128)[:, :, 127:128]),
         reads=[("D8",)], writes=[("De", 1)], tail=True)
    Dl2d = sbp("m_Dl2d", (8, 16))
    K.op("vector", lambda e: e.tensor_sub(Dl2d[:], DedgeE[:, 1:17], DedgeE[:, 0:16]),
         reads=[("De", 0), ("De", 1)], writes=[("Dl2d",)], tail=True)

    Dfeat128 = sbp("m_Dfeat", (128, T))
    K.dma("bcast", "gpsimd",
          lambda e: e.dma_start(out=Dfeat128[:], in_=D8[:].unsqueeze(1).broadcast_to([8, 16, T])),
          reads=[("D8",)], writes=[("Dfeat",)])
    De128 = sbp("m_De128", (128, 16))
    K.dma("bcast", "gpsimd",
          lambda e: e.dma_start(out=De128[:], in_=DedgeE[:, 0:16].unsqueeze(1).broadcast_to([8, 16, 16])),
          reads=[("De", 0), ("De", 1)], writes=[("De128",)])
    DN128 = sbp("m_DN128", (128, 16))
    K.dma("bcast", "gpsimd",
          lambda e: e.dma_start(out=DN128[:], in_=DedgeE[:, 1:17].unsqueeze(1).broadcast_to([8, 16, 16])),
          reads=[("De", 1)], writes=[("DN128",)])
    Dl128 = sbp("m_Dl128", (128, 16))
    K.dma("bcast", "gpsimd",
          lambda e: e.dma_start(out=Dl128[:], in_=Dl2d[:].unsqueeze(1).broadcast_to([8, 16, 16])),
          reads=[("Dl2d",)], writes=[("Dl128",)])

    alog_s = sbp("m_alog", (1, 128))
    A_flat = sbp("m_Aflat", (1, 128))
    A_colf = sbp("m_Acolf", (128, 1))
    K.dma("const", "sync", lambda e: e.dma_start(out=alog_s[:], in_=a_log[:]), writes=[("alog",)])
    K.op("scalar", lambda e: e.activation(A_flat[:], alog_s[:], AF.Exp), reads=[("alog",)],
         writes=[("Aflat",)])
    K.op("vector", lambda e: e.tensor_scalar_mul(A_flat[:], A_flat[:], -1.0),
         reads=[("Aflat",)], writes=[("Aflat",)], tail=True)
    K.dma("bcast", "gpsimd", lambda e: e.dma_start(out=a_scr[:], in_=A_flat[:]),
          reads=[("Aflat",)], writes=[("ascr",)])
    K.dma("bcast", "gpsimd", lambda e: e.dma_start(out=A_rep[:], in_=a_scr[:].broadcast_to([128, 128])),
          reads=[("ascr",)], writes=[("Arep",)])
    K.op("tensor", lambda e: e.transpose(PS_BANKS[6][0:128, 0:1], A_flat[0:1, 0:128], eyef[0:1, 0:1]),
         reads=[("Aflat",)] + EYEF_T, writes=[PSTOK[6]])
    K.op("vector", lambda e: e.tensor_copy(A_colf[:], PS_BANKS[6][0:128, 0:1]),
         reads=[PSTOK[6]], writes=[("Acolf",)])

    e_scr = sbp("m_escr", (128, T))
    e_scr2 = sbp("m_escr2", (128, T))
    K.op("vector", lambda e: e.tensor_sub(
            e_scr[:].rearrange("p (c l) -> p c l", l=128),
            Dfeat128[:].rearrange("p (c l) -> p c l", l=128),
            De128[:].unsqueeze(2).broadcast_to([128, 16, 128])),
         reads=[("Dfeat",), ("De128",)], writes=[("escr",)])
    tap_simple("e_scr", e_scr[:], (128, T), F32, [("escr",)])
    K.op("scalar", lambda e: e.activation(e_scr2[:], e_scr[:], AF.Exp, scale=A_colf[:]),
         reads=[("escr",), ("Acolf",)], writes=[("escr2",)])
    tap_simple("e_scr2", e_scr2[:], (128, T), F32, [("escr2",)])
    K.op("vector", lambda e: e.tensor_mul(Cbar[:], e_scr2[:], C_f[:]),
         reads=[("escr2",)] + [("C_f", ns) for ns in range(4)], writes=[("Cbar",)])
    K.op("vector", lambda e: e.tensor_sub(
            e_scr[:].rearrange("p (c l) -> p c l", l=128),
            DN128[:].unsqueeze(2).broadcast_to([128, 16, 128]),
            Dfeat128[:].rearrange("p (c l) -> p c l", l=128)),
         reads=[("Dfeat",), ("DN128",), ("Cbar",)], writes=[("escr",)])
    K.op("scalar", lambda e: e.activation(e_scr2[:], e_scr[:], AF.Exp, scale=A_colf[:]),
         reads=[("escr",), ("Acolf",)], writes=[("escr2",)])
    K.op("vector", lambda e: e.tensor_mul(Bbar2[:], e_scr2[:], B_f[:]),
         reads=[("escr2",)] + [("B_f", ns) for ns in range(4)], writes=[("Bbar2",)])
    tap_simple("Dfeat128", Dfeat128[:], (128, T), F32, [("Dfeat",)])
    tap_simple("De128", De128[:], (128, 16), F32, [("De128",)])
    tap_simple("A_rep", A_rep[:], (128, 128), F32, [("Arep",)])
    tap_simple("A_colf", A_colf[:], (128, 1), F32, [("Acolf",)])
    tap_simple("Cbar", Cbar[:], (128, T), BF16, [("Cbar",)])
    tap_simple("Bbar2", Bbar2[:], (128, T), F32, [("Bbar2",)])

    decayspan = sbp("m_dspan", (128, 16))
    dspanT = sbp("m_dspanT", (16, 128))
    K.op("scalar", lambda e: e.activation(decayspan[:], Dl128[:], AF.Exp, scale=A_colf[:]),
         reads=[("Dl128",), ("Acolf",)], writes=[("dspan",)])
    K.op("tensor", lambda e: e.transpose(PS_BANKS[6][0:16, 0:128], decayspan[:], eyef[:]),
         reads=[("dspan",)] + EYEF_T, writes=[PSTOK[6]])
    K.op("vector", lambda e: e.tensor_copy(dspanT[:], PS_BANKS[6][0:16, 0:128]),
         reads=[PSTOK[6]], writes=[("dspanT",)])
    K.dma("bcast", "gpsimd", lambda e: e.dma_start(out=decay_scr[:], in_=dspanT[:]),
          reads=[("dspanT",)], writes=[("dscr",)])

    for c in range(NC):
        K.op("tensor", lambda e, c=c: e.transpose(
                PS_BANKS[7][0:128, 0:8], dt_f[0:8, c * LC:(c + 1) * LC], eyef[0:8, 0:8]),
             reads=[("dtf", "clip")] + EYEF_T, writes=[PSTOK[7]])
        K.op("vector", lambda e, c=c: e.tensor_copy(dtT_all[:, c * 8:(c + 1) * 8],
                                                    PS_BANKS[7][0:128, 0:8]),
             reads=[PSTOK[7]], writes=[("dtT", c)])

    st16 = sbp("m_st16", (16, 128))
    for h in range(H):
        K.dma("bcast", "gpsimd", lambda e, h=h: e.dma_start(
                out=st16[:], in_=d_scr[h:h + 1, :].rearrange("o (c l) -> (o c) l", l=128)),
              reads=[("dscr8",)], writes=[("st16",)])
        K.op("tensor", lambda e: e.transpose(PS_BANKS[6][0:128, 0:16], st16[:], eyef[0:16, 0:16]),
             reads=[("st16",)] + EYEF_T, writes=[PSTOK[6]])
        K.op("vector", lambda e, h=h: e.tensor_copy(Dcol2d[:, h * 16:(h + 1) * 16],
                                                    PS_BANKS[6][0:128, 0:16]),
             reads=[PSTOK[6]], writes=[("Dcol", h)])

    cb16 = sbp("m_cb16", (16, 128))
    K.dma("const", "sync", lambda e: e.dma_start(
            out=convw_cols[:], in_=conv_w[:].rearrange("(fc p) j -> p fc j", p=128)),
          writes=[("convw",)])
    K.dma("const", "sync", lambda e: e.dma_start(
            out=cb16[:], in_=conv_b[:].rearrange("o (fc p) -> (o fc) p", p=128)),
          writes=[("cb16",)])
    K.op("tensor", lambda e: e.transpose(PS_BANKS[7][0:128, 0:16], cb16[:], eyef[0:16, 0:16]),
         reads=[("cb16",)] + EYEF_T, writes=[PSTOK[7]])
    K.op("vector", lambda e: e.tensor_copy(convb_cols[:], PS_BANKS[7][0:128, 0:16]),
         reads=[PSTOK[7]], writes=[("convb",)])

    K.dma("bcast", "gpsimd", lambda e: e.dma_start(out=dskip_rep[:], in_=dskip[:].broadcast_to([128, DI])),
          writes=[("dskiprep",)])
    K.dma("const", "sync", lambda e: e.dma_start(out=blockmask[:], in_=blockmask_h[:]),
          writes=[("blockm",)])

    close_arena(proj_arena, "m_proj")

    scan_arena = arena()
    sbs = mk_sb(scan_arena)
    h0b = sbs("m_h0b", (128, DI), BF16)
    hT_cur = sbs("m_hT", (128, 256))
    K.op("vector", lambda e: e.memset(h0b[:], 0.0), writes=[("h0b",)])
    K.op("vector", lambda e: e.memset(hT_cur[:], 0.0), writes=[("hT",)])

    uconv_s = [sbs(f"m_uconv{i}", (128, 128)) for i in range(2)]
    uT_c = [sbs(f"m_uTc{i}", (128, DI), BF16) for i in range(2)]
    Btok_c = [sbs(f"m_Btok{i}", (128, 128)) for i in range(2)]
    Bb2T_c = [sbs(f"m_Bb2T{i}", (128, 128), BF16) for i in range(2)]
    decrep = [sbs(f"m_decrep{i}", (128, 128)) for i in range(2)]
    Uh = [sbs(f"m_Uh{i}", (128, 256), BF16) for i in range(4)]
    Sb_t = [sbs(f"m_Sb{i}", (128, 128), BF16) for i in range(16)]
    E_t = [sbs(f"m_Et{i}", (128, 128)) for i in range(4)]
    tdec = sbs("m_tdec", (128, 256))
    htF = sbs("m_htF", (128, 256), BF16)
    uD = sbs("m_uD", (128, DI), BF16)
    y_c = [sbs(f"m_yc{i}", (128, DI), BF16) for i in range(1)]
    CrepS = [sbs(f"m_crep{i}", (128, DI), BF16) for i in range(2)]
    DrepS = [sbs(f"m_drep{i}", (128, 128)) for i in range(2)]

    for c in range(NC):
        uT = uT_c[c % 2]
        uTtok = ("uT", c % 2)
        for fc in range(16):
            ub = uconv_s[fc % 2]
            ubtok = ("uconv", fc % 2)
            K.op("scalar",
                 lambda e, fc=fc, c=c, ub=ub: e.activation(
                     ub[:], u_raw[fc][:, 4 + c * LC: 4 + (c + 1) * LC], AF.Identity,
                     scale=convw_cols[:, fc * 4 + 3: fc * 4 + 4],
                     bias=convb_cols[:, fc:fc + 1]),
                 reads=[("uraw", fc, c // 4), ("uraw0", fc), ("convw",), ("convb",)],
                 writes=[ubtok], note=f"conv{c},{fc}")
            for j in range(3):
                K.op("vector",
                     lambda e, fc=fc, c=c, j=j, ub=ub: e.scalar_tensor_tensor(
                         ub[:], u_raw[fc][:, 1 + j + c * LC: 1 + j + c * LC + 128],
                         convw_cols[:, fc * 4 + j: fc * 4 + j + 1], ub[:],
                         ALU.mult, ALU.add),
                     reads=[("uraw", fc, c // 4), ("uraw", fc, max(c - 1, 0) // 4),
                            ("uraw0", fc), ("convw",), ubtok],
                     writes=[ubtok], note=f"conv{c},{fc},{j}")
            tb = PS_BANKS[6 + fc % 2]
            tbtok = PSTOK[6 + fc % 2]
            K.op("tensor", lambda e, ub=ub, tb=tb: e.transpose(tb[:, 0:128], ub[:], eyef[:]),
                 reads=[ubtok] + EYEF_T, writes=[tbtok], note=f"uT{c},{fc}")
            K.op("vector", lambda e, fc=fc, uT=uT, tb=tb: e.tensor_copy(
                    uT[:, fc * 128:(fc + 1) * 128], tb[:, 0:128]),
                 reads=[tbtok], writes=[uTtok], note=f"uTe{c},{fc}")
        Btok = Btok_c[c % 2]
        Bb2T = Bb2T_c[c % 2]
        dcr = decrep[c % 2]
        K.op("tensor", lambda e, c=c: e.transpose(PS_BANKS[6][:, 0:128],
                                                  B_f[:, c * LC:(c + 1) * LC], eyef[:]),
             reads=[("B_f", c // 4)] + EYEF_T, writes=[PSTOK[6]])
        K.op("vector", lambda e, Btok=Btok: e.tensor_copy(Btok[:], PS_BANKS[6][:, 0:128]),
             reads=[PSTOK[6]], writes=[("Btok", c % 2)])
        K.op("tensor", lambda e, c=c: e.transpose(PS_BANKS[7][:, 0:128],
                                                  Bbar2[:, c * LC:(c + 1) * LC], eyef[:]),
             reads=[("Bbar2",)] + EYEF_T, writes=[PSTOK[7]])
        K.op("vector", lambda e, Bb2T=Bb2T: e.tensor_copy(Bb2T[:], PS_BANKS[7][:, 0:128]),
             reads=[PSTOK[7]], writes=[("Bb2T", c % 2)])
        K.dma("bcast", "gpsimd", lambda e, c=c, dcr=dcr: e.dma_start(
                out=dcr[:], in_=decay_scr[c:c + 1, :].broadcast_to([128, 128])),
              reads=[("dscr",)], writes=[("dcrep", c % 2)])

        if c > 0:
            for ns in range(4):
                K.op("tensor", lambda e, c=c, ns=ns: e.matmul(
                        PS_BANKS[ns][:, :], Cbar[:, c * LC:(c + 1) * LC],
                        h0b[:, ns * 512:(ns + 1) * 512], start=True, stop=False),
                     reads=[("Cbar",), ("h0b",)], writes=[PSTOK[ns]], note=f"Y1:{c},{ns}")

        for h in range(H):
            U = Uh[h % 4]
            Utok = ("Uh", h % 4)
            K.op("vector", lambda e, h=h, c=c, U=U, uT=uT: e.tensor_scalar_mul(
                    U[:], uT[:, h * 256:(h + 1) * 256], dtT_all[:, c * 8 + h: c * 8 + h + 1]),
                 reads=[uTtok, ("dtT", c)], writes=[Utok], note=f"Uh{c},{h}")
            dr = DrepS[h % 2]
            K.dma("crep", "gpsimd", lambda e, h=h, c=c, dr=dr: e.dma_start(
                    out=dr[:], in_=d_scr[h:h + 1, c * LC:(c + 1) * LC].broadcast_to([128, 128])),
                  reads=[("dscr8",)], writes=[("DrepS", h % 2)], note=f"Drep{c},{h}")
            E = E_t[h % 4]
            Etok = ("E", h % 4)
            K.op("vector", lambda e, h=h, c=c, E=E, dr=dr: e.scalar_tensor_tensor(
                    E[:], dr[:], Dcol2d[:, h * 16 + c: h * 16 + c + 1],
                    maskm[:], ALU.subtract, ALU.add),
                 reads=[("DrepS", h % 2), ("Dcol", h)] + MASKM_T, writes=[Etok], note=f"E{c},{h}")
            K.dma("crep", "gpsimd", lambda e, h=h, c=c, cr_=cf_scr[h % 2]: e.dma_start(
                    out=cr_[0:1, :], in_=c_scr[h * 16:(h + 1) * 16, c * LC:(c + 1) * LC]),
                  reads=[("cscr",)], writes=[("cfst", h % 2)], note=f"cfst{c},{h}")
            crS = CrepS[h % 2]
            K.dma("crep", "gpsimd", lambda e, h=h, crS=crS, cr_=cf_scr[h % 2]: e.dma_start(
                    out=crS[:], in_=cr_[0:1, :].broadcast_to([128, DI])),
                  reads=[("cfst", h % 2)], writes=[("CrepS", h % 2)], note=f"crbc{c},{h}")
            bank = PS_BANKS[h // 2]
            btok = PSTOK[h // 2]
            for n in range(16):
                hn = h * 16 + n
                Sb = Sb_t[n]
                K.op("scalar", lambda e, hn=hn, Sb=Sb, E=E: e.activation(
                        Sb[:], E[:], AF.Exp, scale=A_rep[:, hn:hn + 1]),
                     reads=[Etok, ("Arep",)], writes=[("Sb", n)], note=f"S{c},{h},{n}")
                K.op("vector", lambda e, hn=hn, n=n, Sb=Sb, Btok=Btok, crS=crS:
                     e.scalar_tensor_tensor(
                        Sb[:], Sb[:], Btok[:, hn:hn + 1],
                        crS[:, n * 128:(n + 1) * 128], ALU.mult, ALU.mult),
                     reads=[("Sb", n), ("Btok", c % 2), ("CrepS", h % 2)],
                     writes=[("Sb", n)], note=f"SB{c},{h},{n}")
                first = (c == 0 and h % 2 == 0 and n == 0)
                last = (h % 2 == 1 and n == 15)
                K.op("tensor", lambda e, h=h, Sb=Sb, U=U, bank=bank, first=first, last=last:
                     e.matmul(bank[:, (h % 2) * 256:(h % 2) * 256 + 256], Sb[:], U[:],
                              start=first, stop=last),
                     reads=[("Sb", n), Utok], writes=[btok], note=f"MM{c},{h},{n}")
            for half in range(2):
                first = (h == 0 and half == 0)
                last = (h == H - 1 and half == 1)
                K.op("tensor", lambda e, h=h, half=half, U=U, Bb2T=Bb2T, first=first, last=last:
                     e.matmul(PS_BANKS[4][:, half * 128 + h * 16: half * 128 + h * 16 + 16],
                              U[:, half * 128:(half + 1) * 128], Bb2T[:, h * 16:(h + 1) * 16],
                              start=first, stop=last),
                     reads=[Utok, ("Bb2T", c % 2)], writes=[PSTOK[4]], note=f"HU{c},{h},{half}")

        K.op("vector", lambda e, dcr=dcr: e.tensor_mul(
                tdec[:].rearrange("p (a b) -> p a b", b=128),
                hT_cur[:].rearrange("p (a b) -> p a b", b=128),
                dcr[:].unsqueeze(1).broadcast_to([128, 2, 128])),
             reads=[("hT",), ("dcrep", c % 2)], writes=[("tdec",)], note=f"hdec{c}")
        K.op("vector", lambda e: e.tensor_add(hT_cur[:], tdec[:], PS_BANKS[4][:, 0:256]),
             reads=[("tdec",), PSTOK[4]], writes=[("hT",)], note=f"hadd{c}")
        for half in range(2):
            K.op("tensor", lambda e, half=half: e.transpose(
                    PS_BANKS[7][:, 0:128], hT_cur[:, half * 128:(half + 1) * 128], eyef[:]),
                 reads=[("hT",)] + EYEF_T, writes=[PSTOK[7]], note=f"hTr{c},{half}")
            K.op("vector", lambda e, half=half: e.tensor_copy(
                    htF[:, half * 128:(half + 1) * 128], PS_BANKS[7][:, 0:128]),
                 reads=[PSTOK[7]], writes=[("htF",)], note=f"htF{c},{half}")
        K.op("vector", lambda e: e.tensor_mul(
                h0b[:].rearrange("p (a b) -> p a b", b=256),
                htF[:].unsqueeze(1).broadcast_to([128, 8, 256]),
                blockmask[:].rearrange("p (a b) -> p a b", b=256)),
             reads=[("htF",), ("blockm",)], writes=[("h0b",)], note=f"h0b{c}")

        K.op("vector", lambda e, uT=uT: e.tensor_mul(uD[:], uT[:], dskip_rep[:]),
             reads=[uTtok, ("dskiprep",)], writes=[("uD",)], note=f"uD{c}")
        yc = y_c[0]
        for b in range(4):
            K.op("vector", lambda e, b=b, yc=yc: e.tensor_add(
                    yc[:, b * 512:(b + 1) * 512], PS_BANKS[b][:, :], uD[:, b * 512:(b + 1) * 512]),
                 reads=[PSTOK[b], ("uD",)], writes=[("yc", 0)], note=f"yev{c},{b}")
        K.dma("spill", "gpsimd", lambda e, c=c, yc=yc: e.dma_start(
                out=y_scr[c * LC:(c + 1) * LC, :], in_=yc[:]),
              reads=[("yc", 0)], writes=[("yscr", c)], note=f"yspill{c}")

    if "y_all" in taps:
        def store_y(dst):
            K.dma("dbg", "gpsimd", lambda e: e.dma_start(out=dst[:], in_=y_scr[:]),
                  reads=[("yscr", c) for c in range(NC)])
        tap("y_all", ((T, DI), BF16, store_y))

    close_arena(scan_arena, "m_scan")
    close_arena(sp_arena, "m_spersist")

    epi = arena()
    sbe = mk_sb(epi)
    normw_rep = sbe("m_normw", (128, DI))
    K.dma("const", "sync", lambda e: e.dma_start(out=normw_rep[:], in_=norm_w[:].broadcast_to([128, DI])),
          writes=[("normw",)])
    z_sl = [sbe(f"m_zsl{i}", (128, DI), BF16) for i in range(4)]
    res_sl = [sbe(f"m_rsl{i}", (128, DI), BF16) for i in range(4)]
    y_sl = [sbe(f"m_ysl{i}", (128, DI), BF16) for i in range(2)]
    gbuf = [sbe(f"m_g{i}", (128, DI)) for i in range(2)]
    gsq = sbe("m_gsq", (128, DI), BF16)
    ssq = sbe("m_ssq", (128, 8))
    gn_c = [sbe(f"m_gn{i}", (128, DI), BF16) for i in range(2)]
    gnT_sl = [sbe(f"m_gnT{i}", (128, 512), BF16) for i in range(16)]
    oacc = [sbe(f"m_oacc{i}", (128, 512)) for i in range(4)]
    wload_e = mk_wstream("e", sbe, n_stage=6, cols=512)

    tap_simple("xnf0_epi", xn_f[0][:], (128, T), BF16, XNF(0) + [("yscr", c) for c in range(NC)])
    for sl in range(4):
        for (wsrc, dst, dtok, act) in ((w_inT, z_sl, "zsl", True), (w_resT, res_sl, "rsl", False)):
            for zs in range(4):
                if DEBUG_EPI_BARRIERS[0]:
                    K.barrier("dbg_epi")
                for kc in range(8):
                    wt, wtok = wload_e(
                        (lambda kc=kc, zs=zs, wsrc=wsrc: wsrc[kc * 128:(kc + 1) * 128,
                                                              zs * 512:(zs + 1) * 512]),
                        512, note=f"{dtok}w{sl}")
                    for tc in range(4):
                        c = sl * 4 + tc
                        K.op("tensor", lambda e, wt=wt, kc=kc, tc=tc, c=c, zs=zs:
                             e.matmul(PS_BANKS[tc][:, :], xn_f[kc][:, c * LC:(c + 1) * LC],
                                      wt[:, 0:512], start=(kc == 0), stop=(kc == 7)),
                             reads=[wtok, ("m" + "xnf", kc, c)], writes=[PSTOK[tc]],
                             note=f"{dtok}mm{sl},{zs},{kc},{tc}")
                for tc in range(4):
                    if act:
                        K.op("scalar", lambda e, tc=tc, zs=zs, dst=dst: e.activation(
                                dst[tc][:, zs * 512:(zs + 1) * 512], PS_BANKS[tc][:, :], AF.Silu),
                             reads=[PSTOK[tc]], writes=[(dtok, tc, zs)], note=f"{dtok}ev{sl},{zs},{tc}")
                    else:
                        K.op("vector", lambda e, tc=tc, zs=zs, dst=dst: e.tensor_copy(
                                dst[tc][:, zs * 512:(zs + 1) * 512], PS_BANKS[tc][:, :]),
                             reads=[PSTOK[tc]], writes=[(dtok, tc, zs)], note=f"{dtok}ev{sl},{zs},{tc}")
        if sl == 0:
            tap_simple("z_c0", z_sl[0][:], (128, DI), BF16, [("zsl", 0, zs) for zs in range(4)])
            tap_simple("res_c0", res_sl[0][:], (128, DI), BF16, [("rsl", 0, zs) for zs in range(4)])
        for tc in range(4):
            if DEBUG_EPI_BARRIERS[0]:
                K.barrier("dbg_epi2")
            c = sl * 4 + tc
            ysl = y_sl[tc % 2]
            K.dma("xload", "sync", lambda e, c=c, ysl=ysl: e.dma_start(
                    out=ysl[:], in_=y_scr[c * LC:(c + 1) * LC, :]),
                  reads=[("yscr", c)], writes=[("ysl", tc % 2)])
            g = gbuf[tc % 2]
            gtok = ("g", tc % 2)
            K.op("vector", lambda e, ysl=ysl, tc=tc, g=g: e.tensor_mul(g[:], ysl[:], z_sl[tc][:]),
                 reads=[("ysl", tc % 2)] + [("zsl", tc, zs) for zs in range(4)],
                 writes=[gtok], note=f"gmul{c}")
            K.op("vector", lambda e, tc=tc, g=g: e.tensor_add(g[:], g[:], res_sl[tc][:]),
                 reads=[gtok] + [("rsl", tc, zs) for zs in range(4)],
                 writes=[gtok], note=f"gadd{c}")
            K.op("scalar", lambda e, g=g, tc=tc: e.activation(
                    gsq[:], g[:], AF.Square, accum_out=ssq[:, tc:tc + 1]),
                 reads=[gtok], writes=[("gsq",), ("ssq", tc)], note=f"gsq{c}", tail=True)
            K.op("scalar", lambda e, tc=tc: e.activation(
                    ssq[:, 4 + tc:5 + tc], ssq[:, tc:tc + 1], AF.Sqrt, scale=CONST(1.0 / DI),
                    bias=CONST(1e-6)),
                 reads=[("ssq", tc)] + CPOOL_T, writes=[("ssq2", tc)], tail=True)
            K.op("vector", lambda e, tc=tc: e.reciprocal(ssq[:, tc:tc + 1], ssq[:, 4 + tc:5 + tc]),
                 reads=[("ssq2", tc)], writes=[("ssq", tc)], tail=True)
            gn = gn_c[tc % 2]
            K.op("vector", lambda e, g=g, gn=gn, tc=tc: e.scalar_tensor_tensor(
                    gn[:], g[:], ssq[:, tc:tc + 1], normw_rep[:], ALU.mult, ALU.mult),
                 reads=[gtok, ("ssq", tc), ("normw",)], writes=[("gn", tc % 2)], note=f"gn{c}")
            if sl == 0 and tc == 0:
                tap_simple("gn_c0", gn_c[0][:], (128, DI), BF16, [("gn", 0)])
            for fc in range(16):
                K.dma("gtrans", "scalar", lambda e, fc=fc, tc=tc, gn=gn: e.dma_start_transpose(
                        gnT_sl[fc][:, tc * 128:(tc + 1) * 128], gn[:, fc * 128:(fc + 1) * 128]),
                      reads=[("gn", tc % 2)], writes=[("gnT", fc, tc)], note=f"gnT{c},{fc}")
        for half in range(2):
            for kc in range(16):
                wt, wtok = wload_e(
                    (lambda kc=kc, half=half: w_outT[kc * 128:(kc + 1) * 128,
                                                     half * 512:(half + 1) * 512]),
                    512, note=f"woutw{sl}")
                for tc in range(4):
                    K.op("tensor", lambda e, wt=wt, kc=kc, tc=tc, half=half:
                         e.matmul(PS_BANKS[4 + tc][:, :], gnT_sl[kc][:, tc * 128:(tc + 1) * 128],
                                  wt[:, 0:512], start=(kc == 0), stop=(kc == 15)),
                         reads=[wtok, ("gnT", kc, tc)], writes=[PSTOK[4 + tc]],
                         note=f"wout{sl},{half},{kc},{tc}")
            for tc in range(4):
                c = sl * 4 + tc
                K.op("vector", lambda e, tc=tc: e.tensor_copy(oacc[tc][:], PS_BANKS[4 + tc][:, :]),
                     reads=[PSTOK[4 + tc]], writes=[("oacc", tc)], note=f"oev{sl},{half},{tc}")
                accum_to_xs(oacc[tc][:], (c * LC, (c + 1) * LC), (half * 512, (half + 1) * 512),
                            ("oacc", tc), c, note=f"oxs{sl},{half},{tc}")

    close_arena(epi, "m_epi")
    close_arena(sl1, "sl1")

    def emit_ffn(ph):
        fa = arena()
        sbf = mk_sb(fa)
        psf = mk_ps(fa)
        BK = [psf(f"{ph}bank{i}", (128, 512)) for i in range(8)]
        BT = [(f"{ph}psb", i) for i in range(8)]
        xnf = [sbf(f"{ph}xnf{fc}", (128, T), BF16) for fc in range(8)]
        la = arena()
        ln_phase(ph, mk_sb(la), None, xnf)
        close_arena(la, ph + "ln")
        XT = lambda kc: [(ph + "xnf", kc, c) for c in range(NC)]

        wl = mk_wstream(ph, sbf, n_stage=6, cols=128)

        wr_stg = sbf(f"{ph}wrs", (128, 64))
        wr_bf = sbf(f"{ph}wrb", (128, 64), BF16)
        K.dma("wload", "sync", lambda e: e.dma_start(
                out=wr_stg[:].rearrange("p (a c) -> p a c", c=8),
                in_=wrT[:].rearrange("(a b) c -> b a c", b=128)),
              writes=[(ph + "wrs",)])
        K.op("vector", lambda e: e.tensor_copy(wr_bf[:], wr_stg[:]),
             reads=[(ph + "wrs",)], writes=[(ph + "wrb",)])
        expr = sbf(f"{ph}expr", (128, 8))
        sumr = sbf(f"{ph}sumr", (128, 2))
        probs7 = sbf(f"{ph}probs7", (128, 16))
        for c in range(NC):
            for kc in range(8):
                K.op("tensor", lambda e, c=c, kc=kc: e.matmul(
                        BK[7][0:128, 0:8], xnf[kc][:, c * LC:(c + 1) * LC],
                        wr_bf[:, kc * 8:(kc + 1) * 8], start=(kc == 0), stop=(kc == 7)),
                     reads=[(ph + "wrb",), (ph + "xnf", kc, c)], writes=[BT[7]],
                     note=f"{ph}r{c}")
            K.op("scalar", lambda e, c=c: e.activation(
                    expr[:], BK[7][0:128, 0:8], AF.Exp, accum_out=sumr[:, 0:1]),
                 reads=[BT[7]], writes=[(ph + "expr",), (ph + "sumr",)], tail=True,
                 note=f"{ph}rexp{c}")
            K.op("vector", lambda e: e.reciprocal(sumr[:, 1:2], sumr[:, 0:1]),
                 reads=[(ph + "sumr",)], writes=[(ph + "sumr", 2)], tail=True)
            K.op("vector", lambda e, c=c: e.tensor_scalar_mul(
                    probs7[:, c:c + 1], expr[:, 7:8], sumr[:, 1:2]),
                 reads=[(ph + "expr",), (ph + "sumr", 2)], writes=[(ph + "p7", c)], tail=True)
        p7T = sbf(f"{ph}p7T", (16, 128))
        p7row = sbf(f"{ph}p7row", (1, T))
        probs_rep = sbf(f"{ph}prep", (128, T))
        p7_scr = nc.dram_tensor(f"{ph}p7scr", [16, 128], F32, kind="Internal").ap()
        K.op("tensor", lambda e: e.transpose(BK[6][0:16, 0:128], probs7[:], eyef[:]),
             reads=[(ph + "p7", c) for c in range(NC)] + EYEF_T, writes=[BT[6]])
        K.op("vector", lambda e: e.tensor_copy(p7T[:], BK[6][0:16, 0:128]),
             reads=[BT[6]], writes=[(ph + "p7T",)])
        K.dma("bcast", "gpsimd", lambda e: e.dma_start(out=p7_scr[:], in_=p7T[:]),
              reads=[(ph + "p7T",)], writes=[(ph + "p7scr",)])
        K.dma("bcast", "gpsimd", lambda e: e.dma_start(
                out=p7row[0:1, :], in_=p7_scr[:]),
              reads=[(ph + "p7scr",)], writes=[(ph + "p7row",)])
        for zs in range(4):
            K.op("tensor", lambda e, zs=zs: e.matmul(
                    BK[6][:, :], ones1[:], p7row[0:1, zs * 512:(zs + 1) * 512],
                    start=True, stop=True),
                 reads=ONES_T + [(ph + "p7row",)], writes=[BT[6]], note=f"{ph}prep{zs}")
            K.op("vector", lambda e, zs=zs: e.tensor_copy(
                    probs_rep[:, zs * 512:(zs + 1) * 512], BK[6][:, :]),
                 reads=[BT[6]], writes=[(ph + "prep", zs)], note=f"{ph}prepe{zs}")

        b1_16 = sbf(f"{ph}b116", (32, 128))
        b1c = sbf(f"{ph}b1c", (128, 32))
        b2_16 = sbf(f"{ph}b216", (8, 128))
        b2c = sbf(f"{ph}b2c", (128, 8))
        K.dma("const", "sync", lambda e: e.dma_start(
                out=b1_16[:], in_=b1[:].rearrange("o (a b) -> (o a) b", b=128)),
              writes=[(ph + "b116",)])
        K.op("tensor", lambda e: e.transpose(BK[6][0:128, 0:32], b1_16[:], eyef[0:32, 0:32]),
             reads=[(ph + "b116",)] + EYEF_T, writes=[BT[6]])
        K.op("vector", lambda e: e.tensor_copy(b1c[:], BK[6][0:128, 0:32]),
             reads=[BT[6]], writes=[(ph + "b1c",)])
        K.dma("const", "sync", lambda e: e.dma_start(
                out=b2_16[:], in_=b2[:].rearrange("o (a b) -> (o a) b", b=128)),
              writes=[(ph + "b216",)])
        K.op("tensor", lambda e: e.transpose(BK[6][0:128, 0:8], b2_16[:], eyef[0:8, 0:8]),
             reads=[(ph + "b216",)] + EYEF_T, writes=[BT[6]])
        K.op("vector", lambda e: e.tensor_copy(b2c[:], BK[6][0:128, 0:8]),
             reads=[BT[6]], writes=[(ph + "b2c",)])

        h1g = [sbf(f"{ph}h1g{m}", (128, T), BF16) for m in range(32)]
        for m in range(32):
            for kc in range(8):
                wt, wtok = wl((lambda kc=kc, m=m: w1[kc * 128:(kc + 1) * 128,
                                                    m * 128:(m + 1) * 128]), 128,
                              note=f"{ph}w1{m}")
                for ns in range(4):
                    K.op("tensor", lambda e, wt=wt, kc=kc, ns=ns, m=m: e.matmul(
                            BK[(m % 2) * 4 + ns][:, :], wt[:, 0:128],
                            xnf[kc][:, ns * 512:(ns + 1) * 512],
                            start=(kc == 0), stop=(kc == 7)),
                         reads=[wtok] + XT(kc), writes=[BT[(m % 2) * 4 + ns]],
                         note=f"{ph}w1mm{m},{kc},{ns}")
            for ns in range(4):
                K.op("scalar", lambda e, m=m, ns=ns: e.activation(
                        h1g[m][:, ns * 512:(ns + 1) * 512], BK[(m % 2) * 4 + ns][:, :],
                        AF.Gelu, bias=b1c[:, m:m + 1]),
                     reads=[BT[(m % 2) * 4 + ns], (ph + "b1c",)],
                     writes=[(ph + "h1g", m, ns)], note=f"{ph}gelu{m},{ns}")

        outf = [sbf(f"{ph}outf{i}", (128, T)) for i in range(2)]
        otok = [sbf(f"{ph}otok{i}", (128, 128)) for i in range(4)]
        for m in range(8):
            g = (m % 2) * 4
            for kc in range(32):
                wt, wtok = wl((lambda kc=kc, m=m: w2[kc * 128:(kc + 1) * 128,
                                                    m * 128:(m + 1) * 128]), 128,
                              note=f"{ph}w2{m}")
                for ns in range(4):
                    K.op("tensor", lambda e, wt=wt, kc=kc, ns=ns, g=g, m=m: e.matmul(
                            BK[g + ns][:, :], wt[:, 0:128],
                            h1g[kc][:, ns * 512:(ns + 1) * 512],
                            start=(kc == 0), stop=(kc == 31)),
                         reads=[wtok, (ph + "h1g", kc, ns)], writes=[BT[g + ns]],
                         note=f"{ph}w2mm{m},{kc},{ns}")
            of = outf[m % 2]
            oft = (ph + "outf", m % 2)
            for ns in range(4):
                K.op("vector", lambda e, m=m, ns=ns, g=g, of=of: e.scalar_tensor_tensor(
                        of[:, ns * 512:(ns + 1) * 512], BK[g + ns][:, :], b2c[:, m:m + 1],
                        probs_rep[:, ns * 512:(ns + 1) * 512], ALU.add, ALU.mult),
                     reads=[BT[g + ns], (ph + "b2c",)] + [(ph + "prep", ns)],
                     writes=[oft], note=f"{ph}oev{m},{ns}")
            for tc in range(NC):
                tbk = ((m + 1) % 2) * 4 + (tc % 4)
                K.op("tensor", lambda e, tc=tc, of=of, tbk=tbk: e.transpose(
                        BK[tbk][:, 0:128], of[:, tc * 128:(tc + 1) * 128], eyef[:]),
                     reads=[oft] + EYEF_T, writes=[BT[tbk]], note=f"{ph}otr{m},{tc}")
                oa = otok[tc % 4]
                K.op("vector", lambda e, tc=tc, oa=oa, tbk=tbk: e.tensor_copy(
                        oa[:], BK[tbk][:, 0:128]),
                     reads=[BT[tbk]], writes=[(ph + "otok", tc % 4)], note=f"{ph}oco{m},{tc}")
                K.dma("xacc", "gpsimd",
                      lambda e, tc=tc, m=m, oa=oa: e.dma_start(
                          out=xs[tc * LC:(tc + 1) * LC, m * 128:(m + 1) * 128],
                          in_=oa[:], accum_op=ALU.add),
                      reads=[(ph + "otok", tc % 4), ("xs", tc)], writes=[("xs", tc)],
                      note=f"{ph}oacc{m},{tc}")
        close_arena(fa, ph)

    if n_sublayers >= 2:
        emit_ffn("f1")

    def emit_attn(ph="a"):
        aa = arena()
        sba = mk_sb(aa)
        psa = mk_ps(aa)
        SC = psa(f"{ph}sc", (128, 2048))
        BKrest = [psa(f"{ph}bank{i}", (128, 512)) for i in range(4)]
        BK = [SC[:, i * 512:(i + 1) * 512] for i in range(4)] + BKrest
        BT = [(f"{ph}psb", i) for i in range(8)]
        xnf = [sba(f"{ph}xnf{fc}", (128, T), BF16) for fc in range(8)]
        la = arena()
        ln_phase(ph, mk_sb(la), None, xnf)
        close_arena(la, ph + "ln")
        XT = lambda kc: [(ph + "xnf", kc, c) for c in range(NC)]
        wl = mk_wstream(ph, sba, n_stage=6, cols=128)

        q_f = [sba(f"{ph}qf{m}", (128, T), BF16) for m in range(8)]
        for m in range(8):
            for kc in range(8):
                wt, wtok = wl((lambda kc=kc, m=m: wq[kc * 128:(kc + 1) * 128,
                                                    m * 128:(m + 1) * 128]), 128,
                              note=f"{ph}wq{m}")
                for ns in range(4):
                    K.op("tensor", lambda e, wt=wt, kc=kc, ns=ns, m=m: e.matmul(
                            BK[(m % 2) * 4 + ns][:, :], wt[:, 0:128],
                            xnf[kc][:, ns * 512:(ns + 1) * 512],
                            start=(kc == 0), stop=(kc == 7)),
                         reads=[wtok] + XT(kc), writes=[BT[(m % 2) * 4 + ns]],
                         note=f"{ph}qmm{m},{kc},{ns}")
            for ns in range(4):
                K.op("vector", lambda e, m=m, ns=ns: e.tensor_scalar_mul(
                        q_f[m][:, ns * 512:(ns + 1) * 512], BK[(m % 2) * 4 + ns][:, :],
                        1.0 / np.sqrt(128.0)),
                     reads=[BT[(m % 2) * 4 + ns]], writes=[(ph + "qf", m, ns)],
                     note=f"{ph}qev{m},{ns}")
        klat = sba(f"{ph}klat", (64, T), BF16)
        vlat = sba(f"{ph}vlat", (64, T), BF16)
        for (wsrc, dst, dtok) in ((wk_down, klat, "klat"), (wv_down, vlat, "vlat")):
            for kc in range(8):
                wt, wtok = wl((lambda kc=kc, wsrc=wsrc: wsrc[kc * 128:(kc + 1) * 128, 0:64]),
                              64, note=f"{ph}{dtok}")
                for ns in range(4):
                    K.op("tensor", lambda e, wt=wt, kc=kc, ns=ns: e.matmul(
                            BK[4 + ns][0:64, :], wt[:, 0:64],
                            xnf[kc][:, ns * 512:(ns + 1) * 512],
                            start=(kc == 0), stop=(kc == 7)),
                         reads=[wtok] + XT(kc), writes=[BT[4 + ns]],
                         note=f"{ph}{dtok}mm{kc},{ns}")
            for ns in range(4):
                K.op("vector", lambda e, ns=ns, dst=dst: e.tensor_copy(
                        dst[:, ns * 512:(ns + 1) * 512], BK[4 + ns][0:64, :]),
                     reads=[BT[4 + ns]], writes=[(ph + dtok, ns)], note=f"{ph}{dtok}ev{ns}")

        wup_stg = sba(f"{ph}wups", (64, 128))
        wkup2 = sba(f"{ph}wkup2", (64, 128), BF16)
        wvup2 = sba(f"{ph}wvup2", (64, 128), BF16)
        for (hbm, dst, tok) in ((wk_up, wkup2, "wkup2"), (wv_up, wvup2, "wvup2")):
            for kv in range(2):
                K.dma("wload", "sync", lambda e, hbm=hbm, kv=kv: e.dma_start(
                        out=wup_stg[kv * 32:(kv + 1) * 32, :], in_=hbm[:]),
                      writes=[(ph + "wups", kv)])
            K.op("vector", lambda e, dst=dst: e.tensor_copy(dst[:], wup_stg[:]),
                 reads=[(ph + "wups", 0), (ph + "wups", 1)], writes=[(ph + tok,)])
        k_f = [sba(f"{ph}kf{kv}", (128, T), BF16) for kv in range(2)]
        v_tok = [sba(f"{ph}vt{kv}", (128, T), BF16) for kv in range(2)]
        for kv in range(2):
            for ns in range(4):
                K.op("tensor", lambda e, kv=kv, ns=ns: e.matmul(
                        BK[ns][:, :], wkup2[kv * 32:(kv + 1) * 32, :],
                        klat[kv * 32:(kv + 1) * 32, ns * 512:(ns + 1) * 512],
                        start=True, stop=True),
                     reads=[(ph + "wkup2",)] + [(ph + "klat", ns)], writes=[BT[ns]],
                     note=f"{ph}kup{kv},{ns}")
                K.op("vector", lambda e, kv=kv, ns=ns: e.tensor_copy(
                        k_f[kv][:, ns * 512:(ns + 1) * 512], BK[ns][:, :]),
                     reads=[BT[ns]], writes=[(ph + "kf", kv, ns)], note=f"{ph}kev{kv},{ns}")
            for tc in range(NC):
                K.op("tensor", lambda e, kv=kv, tc=tc: e.matmul(
                        BK[4 + tc % 2][:, 0:128], vlat[kv * 32:(kv + 1) * 32, tc * LC:(tc + 1) * LC],
                        wvup2[kv * 32:(kv + 1) * 32, :], start=True, stop=True),
                     reads=[(ph + "wvup2",)] + [(ph + "vlat", tc // 4)], writes=[BT[4 + tc % 2]],
                     note=f"{ph}vup{kv},{tc}")
                K.op("vector", lambda e, kv=kv, tc=tc: e.tensor_copy(
                        v_tok[kv][:, tc * 128:(tc + 1) * 128], BK[4 + tc % 2][:, 0:128]),
                     reads=[BT[4 + tc % 2]], writes=[(ph + "vt", kv, tc)], note=f"{ph}vev{kv},{tc}")

        OT = [sba(f"{ph}ot{m}", (128, T), BF16) for m in range(8)]
        P_sb = [sba(f"{ph}P{i}", (128, T), BF16) for i in range(2)]
        PT_t = [[sba(f"{ph}PT{i}_{k}", (128, 128), BF16) for k in range(NC)] for i in range(2)]
        mxr = [sba(f"{ph}mx{i}", (128, 8)) for i in range(2)]
        rnormA = [sba(f"{ph}rnA{m}", (128, 16)) for m in range(8)]

        def attn_front(h, qc):
            kv = h // 4
            nk = (qc + 1) * 128
            nsb = (nk + 511) // 512
            mx = mxr[qc % 2]
            for sb_ in range(nsb):
                w = min(512, nk - sb_ * 512)
                K.op("tensor", lambda e, h=h, qc=qc, sb_=sb_, w=w, kv=kv: e.matmul(
                        BK[sb_][:, 0:w], q_f[h][:, qc * LC:(qc + 1) * LC],
                        k_f[kv][:, sb_ * 512: sb_ * 512 + w], start=True, stop=True),
                     reads=[(ph + "qf", h, qc // 4)] + [(ph + "kf", kv, n_) for n_ in range(4)],
                     writes=[BT[sb_]], note=f"{ph}sc{h},{qc},{sb_}")
            dsb = (nk - 128) // 512
            doff = (nk - 128) % 512
            K.op("vector", lambda e, dsb=dsb, doff=doff: e.tensor_add(
                    BK[dsb][:, doff:doff + 128], BK[dsb][:, doff:doff + 128], maska[:]),
                 reads=[BT[dsb]] + MASKA_T, writes=[BT[dsb]], note=f"{ph}msk{h},{qc}")
            sctoks = [BT[s_] for s_ in range(nsb)]
            K.op("vector", lambda e, nk=nk, mx=mx: e.tensor_reduce(
                    mx[:, 5:6], SC[:, 0:nk], mybir.AxisListType.X, ALU.max, negate=True),
                 reads=sctoks, writes=[(ph + "mx", qc % 2, "n")], tail=True,
                 note=f"{ph}mxn{h},{qc}")
            P = P_sb[qc % 2]
            K.op("scalar", lambda e, nk=nk, P=P, mx=mx: e.activation(
                    P[:, 0:nk], SC[:, 0:nk], AF.Exp, bias=mx[:, 5:6],
                    accum_out=mx[:, 3:4]),
                 reads=sctoks + [(ph + "mx", qc % 2, "n")],
                 writes=[(ph + "P", qc % 2, sb_) for sb_ in range(nsb)] +
                        [(ph + "rsum", qc % 2)],
                 tail=True, note=f"{ph}exp{h},{qc}")
            K.op("vector", lambda e, h=h, qc=qc, mx=mx: e.reciprocal(
                    rnormA[h][:, qc:qc + 1], mx[:, 3:4]),
                 reads=[(ph + "rsum", qc % 2)], writes=[(ph + "rnA", h, qc)], tail=True,
                 note=f"{ph}rn{h},{qc}")

        def attn_back(h, qc):
            kv = h // 4
            P = P_sb[qc % 2]
            for kb in range(qc + 1):
                PT = PT_t[qc % 2][kb]
                tb = 6 + kb % 2
                K.op("tensor", lambda e, kb=kb, P=P, tb=tb: e.transpose(
                        BK[tb][:, 0:64].bitcast(BF16), P[:, kb * 128:(kb + 1) * 128], eyeb[:]),
                     reads=[(ph + "P", qc % 2, kb * 128 // 512)] + EYEB_T,
                     writes=[BT[tb]], note=f"{ph}ptr{h},{qc},{kb}")
                K.op("vector" if kb % 2 == 0 else "scalar",
                     (lambda e, kb=kb, PT=PT, tb=tb: e.tensor_copy(
                         PT[:], BK[tb][:, 0:64].bitcast(BF16))) if kb % 2 == 0 else
                     (lambda e, kb=kb, PT=PT, tb=tb: e.copy(
                         PT[:], BK[tb][:, 0:64].bitcast(BF16))),
                     reads=[BT[tb]], writes=[(ph + "PT", qc % 2, kb)],
                     note=f"{ph}pte{h},{qc},{kb}")
            for kb in range(qc + 1):
                PT = PT_t[qc % 2][kb]
                K.op("tensor", lambda e, kb=kb, kv=kv, qc=qc, PT=PT: e.matmul(
                        BK[4 + qc % 2][:, 0:128], v_tok[kv][:, kb * 128:(kb + 1) * 128],
                        PT[:], start=(kb == 0), stop=(kb == qc)),
                     reads=[(ph + "PT", qc % 2, kb), (ph + "vt", kv, kb)],
                     writes=[BT[4 + qc % 2]], note=f"{ph}pv{h},{qc},{kb}")
            K.op("vector", lambda e, h=h, qc=qc: e.tensor_copy(
                    OT[h][:, qc * 128:(qc + 1) * 128], BK[4 + qc % 2][:, 0:128]),
                 reads=[BT[4 + qc % 2]], writes=[(ph + "ot", h, qc)], note=f"{ph}oev{h},{qc}")

        prev = None
        for h in range(NH):
            for qc in range(NC):
                attn_front(h, qc)
                if prev is not None:
                    attn_back(*prev)
                prev = (h, qc)
        attn_back(*prev)

        rnT = sba(f"{ph}rnT", (16, 128))
        rnrow = sba(f"{ph}rnrow", (1, T))
        rnrep = sba(f"{ph}rnrep", (128, T))
        rn_scr = nc.dram_tensor(f"{ph}rnscr", [16, 128], F32, kind="Internal").ap()
        for h in range(NH):
            K.op("tensor", lambda e, h=h: e.transpose(BK[6][0:16, 0:128], rnormA[h][:], eyef[:]),
                 reads=[(ph + "rnA", h, qc) for qc in range(NC)] + EYEF_T, writes=[BT[6]],
                 note=f"{ph}rnT{h}")
            K.op("vector", lambda e: e.tensor_copy(rnT[:], BK[6][0:16, 0:128]),
                 reads=[BT[6]], writes=[(ph + "rnT",)], note=f"{ph}rnTe{h}")
            K.dma("bcast", "gpsimd", lambda e: e.dma_start(out=rn_scr[:], in_=rnT[:]),
                  reads=[(ph + "rnT",)], writes=[(ph + "rnscr",)])
            K.dma("bcast", "gpsimd", lambda e: e.dma_start(out=rnrow[0:1, :], in_=rn_scr[:]),
                  reads=[(ph + "rnscr",)], writes=[(ph + "rnrow",)])
            for zs in range(4):
                K.op("tensor", lambda e, zs=zs: e.matmul(
                        BK[6][:, :], ones1[:], rnrow[0:1, zs * 512:(zs + 1) * 512],
                        start=True, stop=True),
                     reads=ONES_T + [(ph + "rnrow",)], writes=[BT[6]], note=f"{ph}rnb{h},{zs}")
                K.op("vector", lambda e, zs=zs: e.tensor_copy(
                        rnrep[:, zs * 512:(zs + 1) * 512], BK[6][:, :]),
                     reads=[BT[6]], writes=[(ph + "rnrep", zs)], note=f"{ph}rne{h},{zs}")
            K.op("vector", lambda e, h=h: e.tensor_mul(OT[h][:], OT[h][:], rnrep[:]),
                 reads=[(ph + "ot", h, qc) for qc in range(NC)] +
                       [(ph + "rnrep", zs) for zs in range(4)],
                 writes=[(ph + "otn", h)], note=f"{ph}on{h}")

        wo_sb = [sba(f"{ph}wo{oc}", (128, DM), BF16) for oc in range(8)]
        wo_stg = [sba(f"{ph}wos{i}", (128, DM)) for i in range(2)]
        for oc in range(8):
            stgt = wo_stg[oc % 2]
            K.dma("wload", "sync", lambda e, oc=oc, stgt=stgt: e.dma_start(
                    out=stgt[:], in_=wo[oc * 128:(oc + 1) * 128, :]),
                  writes=[(ph + "wos", oc % 2)])
            K.op("vector", lambda e, oc=oc, stgt=stgt: e.tensor_copy(wo_sb[oc][:], stgt[:]),
                 reads=[(ph + "wos", oc % 2)], writes=[(ph + "wo", oc)])
        oacc = [sba(f"{ph}oacc{i}", (128, 512)) for i in range(4)]
        for tc in range(NC):
            for half in range(2):
                b = (tc % 2) * 4 + half
                for oc in range(8):
                    K.op("tensor", lambda e, tc=tc, half=half, oc=oc, b=b: e.matmul(
                            BK[b][:, :], OT[oc][:, tc * LC:(tc + 1) * LC],
                            wo_sb[oc][:, half * 512:(half + 1) * 512],
                            start=(oc == 0), stop=(oc == 7)),
                         reads=[(ph + "otn", oc), (ph + "wo", oc)], writes=[BT[b]],
                         note=f"{ph}wo{tc},{half},{oc}")
                oa = oacc[(tc % 2) * 2 + half]
                K.op("vector", lambda e, b=b, oa=oa: e.tensor_copy(oa[:], BK[b][:, :]),
                     reads=[BT[b]], writes=[(ph + "oacc", (tc % 2) * 2 + half)],
                     note=f"{ph}woev{tc},{half}")
                K.dma("xacc", "gpsimd", lambda e, tc=tc, half=half, oa=oa: e.dma_start(
                        out=xs[tc * LC:(tc + 1) * LC, half * 512:(half + 1) * 512],
                        in_=oa[:], accum_op=ALU.add),
                      reads=[(ph + "oacc", (tc % 2) * 2 + half), ("xs", tc)],
                      writes=[("xs", tc)], note=f"{ph}woacc{tc},{half}")
        close_arena(aa, ph)

    if n_sublayers >= 3:
        emit_attn("a")
    if n_sublayers >= 4:
        emit_ffn("f2")

    for c in range(NC):
        K.dma("ostore", "gpsimd",
              lambda e, c=c: e.dma_start(out=out[c * LC:(c + 1) * LC, :], in_=xs[c * LC:(c + 1) * LC, :]),
              reads=[("xs", c)], writes=[("outd", c)])

    fin = gsb("fin", (1, 1))
    K.op("vector", lambda e: e.memset(fin[:], 1.0), reads=[("outd", c) for c in range(NC)],
         writes=[("fin",)], note="fence")

    K.lower()
    for s in stacks:
        try:
            s.close()
        except Exception:
            pass
    return nc, dbg



def _mask_mamba():
    s = np.arange(128)[:, None]
    t = np.arange(128)[None, :]
    return np.where(t >= s, 0.0, MASKBIG).astype(np.float32)


def _blockmask():
    import ml_dtypes
    m = np.zeros((128, DI), np.float32)
    for h in range(8):
        m[h * 16:(h + 1) * 16, h * 256:(h + 1) * 256] = 1.0
    return m.astype(ml_dtypes.bfloat16)


def _mask_attn():
    q = np.arange(128)[:, None]
    k = np.arange(128)[None, :]
    return np.where(k <= q, 0.0, -MASKBIG).astype(np.float32)


def host_inputs(inputs, core):
    import ml_dtypes
    f = np.ascontiguousarray
    x = inputs["L0_attn_a_x"]
    d = {
        "x0": f(np.asarray(x[core], dtype=np.float32)),
        "w_inT": f(np.asarray(inputs["L0_attn_a_W_in"]).T),
        "w_dtT": f(np.asarray(inputs["L0_attn_a_W_dt"]).T),
        "conv_w": f(np.asarray(inputs["L0_attn_a_conv_w"])),
        "conv_b": f(np.asarray(inputs["L0_attn_a_conv_b"]).reshape(1, DI)),
        "a_log": f(np.asarray(inputs["L0_attn_a_A_log"]).reshape(1, H * DS)),
        "dskip": f(np.asarray(inputs["L0_attn_a_Dskip"]).reshape(1, DI)),
        "dt_bias": f(np.asarray(inputs["L0_attn_a_dt_bias"]).reshape(H, 1)),
        "norm_w": f(np.asarray(inputs["L0_attn_a_norm_w"]).reshape(1, DI)),
        "w_resT": f(np.asarray(inputs["L0_attn_a_W_res"]).T),
        "w_outT": f(np.asarray(inputs["L0_attn_a_W_out"]).T),
        "wq": f(np.asarray(inputs["L0_attn_b_wq"])),
        "wk_down": f(np.asarray(inputs["L0_attn_b_wk_down"])),
        "wv_down": f(np.asarray(inputs["L0_attn_b_wv_down"])),
        "wk_up": f(np.asarray(inputs["L0_attn_b_wk_up"])),
        "wv_up": f(np.asarray(inputs["L0_attn_b_wv_up"])),
        "wo": f(np.asarray(inputs["L0_attn_b_wo"])),
        "wrT": f(np.asarray(inputs["L0_ffn_Wr"]).T),
        "w1": f(np.asarray(inputs["L0_ffn_W1"][7])),
        "b1": f(np.asarray(inputs["L0_ffn_b1"][7]).reshape(1, F1)),
        "w2": f(np.asarray(inputs["L0_ffn_W2"][7])),
        "b2": f(np.asarray(inputs["L0_ffn_b2"][7]).reshape(1, DM)),
        "cpool": np.broadcast_to(np.asarray(CONSTS, dtype=np.float32)[None, :],
                                 (128, len(CONSTS))).copy(),
        "eye_f": np.eye(128, dtype=np.float32),
        "eye_b": np.eye(128, dtype=ml_dtypes.bfloat16),
        "ones1": np.ones((1, 128), dtype=np.float32),
        "mask_mamba": _mask_mamba(),
        "mask_attn": _mask_attn(),
        "blockmask": _blockmask(),
    }
    return d


_PROG_CACHE = {}


def _get_prog(taps=()):
    key = tuple(sorted(taps))
    if key not in _PROG_CACHE:
        _PROG_CACHE[key] = build_program(taps=key)
    return _PROG_CACHE[key]


def kernel(**inputs):
    nc, dbg = _get_prog()
    in_maps = [host_inputs(inputs, core) for core in range(8)]
    res = run_bass_kernel_spmd(nc, in_maps, core_ids=list(range(8)))
    outs = np.stack([np.asarray(r["out"]) for r in res.results], axis=0)
    return outs.astype(np.float32)
